# version 01
# baseline (speedup 1.0000x reference)
import math
import numpy as np
import contextlib
import concourse.bass as bass
import concourse.mybir as mybir
from concourse.bass import IndirectOffsetOnAxis
from concourse.alu_op_type import AluOpType as OP
from concourse.bass_utils import run_bass_kernel_spmd

F32 = mybir.dt.float32
BF16 = mybir.dt.bfloat16
I32 = mybir.dt.int32
AF = mybir.ActivationFunctionType
AXX = mybir.AxisListType.X

B_CORE, S, D, NT, NC = 2, 2048, 1024, 16, 8

_BUILD_UPTO = 4
_DEBUG_DUMPS = []
_USE_TOKBUF = True
_B_LIST = [0, 1]
_FFN_UPTO = 9



def _bcast_kernel():
    g = np.zeros(S, np.float64); g[0] = 1.0
    res = np.zeros(S, np.float64); cur = g.copy(); st = 1
    while st < S:
        cur = cur + 0.5 * (np.roll(cur, st) + np.roll(cur, -st))
        res = res + cur
        st *= 2
    g = res / (math.log2(S) + 1.0)
    idx = (np.arange(S)[None, :] - np.arange(S)[:, None]) % S
    return g[idx].astype(np.float32)


def _host_constants():
    c = {}
    half = 16
    inv = (10000.0 ** (-np.arange(half, dtype=np.float32) / half))
    ang = np.arange(S, dtype=np.float32)[:, None] * inv[None, :]
    c["c_cos"] = np.cos(ang).astype(np.float32)
    c["c_sin"] = np.sin(ang).astype(np.float32)
    c["c_circ"] = _bcast_kernel()
    tri = np.linspace(0.5, 1.5, 512).astype(np.float32)
    c["c_tri"] = tri.reshape(512, 1)
    den = np.zeros(S, np.float32)
    for n in range(7):
        den[n * 256:n * 256 + 512] += tri
    c["c_denr"] = (1.0 / (den + 1e-6)).reshape(S, 1).astype(np.float32)
    c["c_lcol"] = np.arange(S, dtype=np.float32).reshape(S, 1)
    offs = np.arange(-16, 17, dtype=np.float32)
    c["c_offs"] = np.ascontiguousarray(np.broadcast_to(offs, (128, 33)))
    c["c_aoffs"] = np.ascontiguousarray(np.broadcast_to(np.abs(offs), (128, 33)))
    c["c_ident"] = np.eye(128, dtype=np.float32)
    c["c_ones"] = np.ones((128, 128), np.float32)
    return c


def _host_weights(inp):
    w = {}
    gm = inp["L0_attn_a_global_memory"]
    w["aa_gmT"] = np.ascontiguousarray(
        np.transpose(gm, (0, 2, 1, 3)).reshape(64, 1024).T)
    cw = inp["L0_attn_a_conv_w"]
    w["aa_wcat"] = np.ascontiguousarray(cw.transpose(2, 1, 0).reshape(4096, 1024))
    w["aa_convb"] = np.asarray(inp["L0_attn_a_conv_b"]).reshape(1024, 1)
    mw = inp["L0_attn_a_mix_w"]
    w["aa_m1"] = np.ascontiguousarray(mw[:1024])
    w["aa_m2"] = np.ascontiguousarray(mw[1024:])
    w["aa_mixb"] = np.asarray(inp["L0_attn_a_mix_b"]).reshape(1024, 1)
    w["aa_outw"] = inp["L0_attn_a_out_w"]
    w["aa_outb"] = np.asarray(inp["L0_attn_a_out_b"]).reshape(1024, 1)

    wq = inp["L0_ffn_Wq"].transpose(1, 0, 2).reshape(1024, 1024)
    wk = inp["L0_ffn_Wk"].transpose(1, 0, 2).reshape(1024, 1024)
    w["f_wqk"] = np.ascontiguousarray(np.concatenate([wq, wk], 1))
    sv = inp["L0_ffn_Wsel_v"].transpose(2, 0, 1).reshape(1024, 128)
    so = inp["L0_ffn_Wsel_o"].transpose(2, 0, 1).reshape(1024, 128)
    w["f_wsel"] = np.ascontiguousarray(np.concatenate([sv, so], 1))
    w["f_wv"] = np.ascontiguousarray(
        inp["L0_ffn_Wv"].transpose(0, 2, 1, 3).reshape(16, 1024, 512))
    w["f_wo"] = np.ascontiguousarray(np.asarray(inp["L0_ffn_Wo"]).reshape(16, 4, 128, 1024))
    w["f_ln1g"] = np.ascontiguousarray(
        np.broadcast_to(inp["L0_ffn_ln1_g"], (128, 1024)))
    w["f_ln1b"] = np.ascontiguousarray(
        np.broadcast_to(inp["L0_ffn_ln1_b"], (128, 1024)))
    w["f_ln2g"] = np.ascontiguousarray(
        np.broadcast_to(inp["L0_ffn_ln2_g"], (128, 1024)))
    w["f_ln2b"] = np.ascontiguousarray(
        np.broadcast_to(inp["L0_ffn_ln2_b"], (128, 1024)))
    w["f_w1"] = inp["L0_ffn_W1"]
    w["f_b1"] = np.asarray(inp["L0_ffn_b1"]).reshape(4096, 1)
    w["f_w2"] = inp["L0_ffn_W2"]
    w["f_b2"] = np.asarray(inp["L0_ffn_b2"]).reshape(1024, 1)

    w["ab_wwq"] = np.ascontiguousarray(
        np.concatenate([inp["L0_attn_b_wave_w"], inp["L0_attn_b_query_w"]], 1))
    w["ab_wqb"] = np.ascontiguousarray(np.broadcast_to(
        np.concatenate([np.asarray(inp["L0_attn_b_wave_b"]),
                        np.asarray(inp["L0_attn_b_query_b"])]), (128, 152)))
    w["ab_kw"] = np.ascontiguousarray(
        np.broadcast_to(inp["L0_attn_b_key_w"], (128, 16)))
    w["ab_outw"] = inp["L0_attn_b_out_w"]
    w["ab_se1"] = inp["L0_attn_b_se1_w"]
    w["ab_se1b"] = np.ascontiguousarray(
        np.broadcast_to(inp["L0_attn_b_se1_b"], (128, 256)))
    w["ab_se2"] = inp["L0_attn_b_se2_w"]
    w["ab_se2b"] = np.ascontiguousarray(
        np.broadcast_to(inp["L0_attn_b_se2_b"], (128, 1024)))
    return {k: np.ascontiguousarray(v, dtype=np.float32) for k, v in w.items()}


_W_SHAPES = {
    "aa_gmT": [1024, 64], "aa_wcat": [4096, 1024], "aa_convb": [1024, 1],
    "aa_m1": [1024, 1024], "aa_m2": [1024, 1024], "aa_mixb": [1024, 1],
    "aa_outw": [1024, 1024], "aa_outb": [1024, 1],
    "f_wqk": [1024, 2048], "f_wsel": [1024, 256],
    "f_wv": [16, 1024, 512], "f_wo": [16, 4, 128, 1024],
    "f_ln1g": [128, 1024], "f_ln1b": [128, 1024],
    "f_ln2g": [128, 1024], "f_ln2b": [128, 1024],
    "f_w1": [1024, 4096], "f_b1": [4096, 1], "f_w2": [4096, 1024],
    "f_b2": [1024, 1],
    "ab_wwq": [1024, 152], "ab_wqb": [128, 152], "ab_kw": [128, 16],
    "ab_outw": [1024, 1024], "ab_se1": [1024, 256], "ab_se1b": [128, 256],
    "ab_se2": [256, 1024], "ab_se2b": [128, 1024],
    "c_cos": [2048, 16], "c_sin": [2048, 16], "c_circ": [2048, 2048],
    "c_tri": [512, 1], "c_denr": [2048, 1], "c_lcol": [2048, 1],
    "c_offs": [128, 33], "c_aoffs": [128, 33],
    "c_ident": [128, 128], "c_ones": [128, 128],
}

_WBF = ["aa_wcat", "aa_m1", "aa_m2", "aa_outw", "f_wqk", "f_wsel", "f_wv",
        "f_wo", "f_w1", "f_w2", "ab_wwq", "ab_outw", "ab_se1", "ab_se2",
        "c_circ"]



def G(e, n=6):
    for _ in range(n):
        e.nop()


class Ctx:
    def __init__(self, nc, es):
        self.nc, self.es, self._s = nc, es, {}

    def s(self, name):
        if name not in self._s:
            self._s[name] = [self.es.enter_context(self.nc.semaphore(name)), 0]
        return self._s[name]

    def inc(self, name, inst, k=1):
        s = self.s(name)
        inst.then_inc(s[0], k)
        s[1] += k
        return s[1]

    def val(self, name):
        return self.s(name)[1]

    def wait(self, eng, name, v=None):
        s = self.s(name)
        eng.wait_ge(s[0], s[1] if v is None else v)

    def sw(self, eng, name, inst):
        n = self.inc(name, inst)
        eng.wait_ge(self.s(name)[0], n)


def build_program():
    nc = bass.Bass("TRN2", target_bir_lowering=False)
    es = contextlib.ExitStack()
    E = es.enter_context
    C = Ctx(nc, es)

    x_in = nc.dram_tensor("x", [B_CORE, S, D], F32, kind="ExternalInput").ap()
    out = nc.dram_tensor("out", [B_CORE, S, D], F32, kind="ExternalOutput").ap()
    W = {k: nc.dram_tensor(k, sh, F32, kind="ExternalInput").ap()
         for k, sh in _W_SHAPES.items()}
    WB = {k: nc.dram_tensor("bf_" + k, _W_SHAPES[k], BF16, kind="Internal").ap()
          for k in _WBF}

    DS = {}
    for name, sh in [
        ("xh", [S, D]), ("qd", [S, D]), ("kd", [S, D]),
        ("od", [D, S]), ("opd", [S, 16 * 512]),
        ("loc", [S, D]), ("aoT", [D, S]), ("otw", [D, 512]),
        ("l1", [S, D]), ("m2T", [D, S]), ("xhl", [S, D]),
        ("opT", [16 * 512, S]),
    ]:
        DS[name] = nc.dram_tensor("ds_" + name, sh, BF16, kind="Internal").ap()
    sums_d = nc.dram_tensor("ds_sums", [16, S], F32, kind="Internal").ap()
    dbg = {}
    for i, (nm, r, c_) in enumerate(_DEBUG_DUMPS):
        dbg[i] = nc.dram_tensor(f"dbg{i}", [r, c_], BF16, kind="ExternalOutput").ap()

    XRES = E(nc.sbuf_tensor("XRES", [128, B_CORE * NT * D], BF16))
    IDENT = E(nc.sbuf_tensor("IDENT", [128, 128], BF16))
    ONES = E(nc.sbuf_tensor("ONESB", [128, 128], BF16))
    COS = E(nc.sbuf_tensor("COS", [128, NT * 16], F32))
    SIN = E(nc.sbuf_tensor("SIN", [128, NT * 16], F32))
    TRI = E(nc.sbuf_tensor("TRI", [128, 4], F32))
    DENR = E(nc.sbuf_tensor("DENR", [128, NT], F32))
    LCOL = E(nc.sbuf_tensor("LCOL", [128, NT], F32))
    OFFS = E(nc.sbuf_tensor("OFFS", [128, 33], F32))
    AOFFS = E(nc.sbuf_tensor("AOFFS", [128, 33], F32))
    GMT = E(nc.sbuf_tensor("GMT", [128, NC * 64], BF16))
    SCR = E(nc.sbuf_tensor("SCR", [128, D], F32))
    SCRB = E(nc.sbuf_tensor("SCRB", [128, 2 * D], BF16))
    STAT = E(nc.sbuf_tensor("STAT", [128, 32], F32))
    XHT = E(nc.sbuf_tensor("XHT", [128, NC * S], BF16))
    IDENTF = E(nc.sbuf_tensor("IDENTF", [128, 128], F32))
    JUNK = E(nc.sbuf_tensor("JUNK", [128, D], BF16))

    def xres(b, t):
        return XRES[:, (b * NT + t) * D:(b * NT + t + 1) * D]

    with nc.Block() as blk:
        def _(g):
            for b in range(B_CORE):
                for t in range(NT):
                    C.inc("g", g.dma_start(xres(b, t),
                                           x_in[b, t * 128:(t + 1) * 128, :]), 16)
            for k in _WBF:
                src, dst = W[k], WB[k]
                n = int(np.prod(_W_SHAPES[k]))
                dims = " ".join(f"a{i}" for i in range(len(_W_SHAPES[k])))
                fs = src.rearrange(f"{dims} -> ({dims})")
                fd = dst.rearrange(f"{dims} -> ({dims})")
                step = 1 << 21
                for off in range(0, n, step):
                    m = min(step, n - off)
                    C.inc("g", g.dma_start(fd[off:off + m], fs[off:off + m]), 16)
        blk.gpsimd(_)

        def _(s):
            C.inc("d", s.dma_start(SCR[:, 0:128], W["c_ident"]), 16)
            C.inc("d", s.dma_start(IDENTF[:], W["c_ident"]), 16)
            C.inc("d", s.dma_start(SCR[:, 128:256], W["c_ones"]), 16)
            for t in range(NT):
                C.inc("d", s.dma_start(COS[:, t * 16:(t + 1) * 16],
                                       W["c_cos"][t * 128:(t + 1) * 128, :]), 16)
                C.inc("d", s.dma_start(SIN[:, t * 16:(t + 1) * 16],
                                       W["c_sin"][t * 128:(t + 1) * 128, :]), 16)
                C.inc("d", s.dma_start(DENR[:, t:t + 1],
                                       W["c_denr"][t * 128:(t + 1) * 128, :]), 16)
                C.inc("d", s.dma_start(LCOL[:, t:t + 1],
                                       W["c_lcol"][t * 128:(t + 1) * 128, :]), 16)
            for q in range(4):
                C.inc("d", s.dma_start(TRI[:, q:q + 1],
                                       W["c_tri"][q * 128:(q + 1) * 128, :]), 16)
            C.inc("d", s.dma_start(OFFS[:], W["c_offs"]), 16)
            C.inc("d", s.dma_start(AOFFS[:], W["c_aoffs"]), 16)
            for cc in range(NC):
                C.inc("d", s.dma_start(SCR[:, 256 + cc * 64:256 + (cc + 1) * 64],
                                       W["aa_gmT"][cc * 128:(cc + 1) * 128, :]), 16)
        blk.sync(_)

        def _(v):
            C.wait(v, "d")
            C.wait(v, "g")
            v.tensor_copy(IDENT[:], SCR[:, 0:128])
            v.tensor_copy(ONES[:], SCR[:, 128:256])
            v.tensor_copy(GMT[:], SCR[:, 256:256 + NC * 64])
        blk.vector(_)

    def emit_ln(b, dram, make_xht=True, tok_buf=None, lo_dram=None):
        nv = 3 if lo_dram is not None else 2
        with nc.Block() as blk:
            a0, v0, d0 = C.val("a"), C.val("v"), C.val("d")

            def _(v):
                C.wait(v, "d", d0)
                for t in range(NT):
                    src = xres(b, t)
                    buf = (tok_buf[:, t * D:(t + 1) * D] if tok_buf is not None
                           else SCRB[:, (t % 2) * D:(t % 2) * D + D])
                    lobuf = SCRB[:, ((t + 1) % 2) * D:((t + 1) % 2) * D + D]
                    C.sw(v, "vS", v.tensor_reduce(STAT[:, 0:1], src, axis=AXX,
                                                  op=OP.add))
                    C.sw(v, "vS", v.tensor_scalar(STAT[:, 5:6], STAT[:, 0:1],
                                                  1.0 / D, None, OP.mult))
                    v.wait_ge(C.s("a")[0], a0 + 2 * t + 1)
                    C.sw(v, "vS", v.tensor_scalar(STAT[:, 3:4], STAT[:, 1:2],
                                                  1.0 / D, None, OP.mult))
                    C.sw(v, "vS", v.tensor_tensor(STAT[:, 2:3], STAT[:, 5:6],
                                                  STAT[:, 5:6], OP.mult))
                    C.sw(v, "vS", v.tensor_tensor(STAT[:, 3:4], STAT[:, 3:4],
                                                  STAT[:, 2:3], OP.subtract))
                    C.inc("v", v.tensor_scalar(STAT[:, 3:4], STAT[:, 3:4],
                                               1e-5, None, OP.add))
                    v.wait_ge(C.s("a")[0], a0 + 2 * t + 2)
                    if t >= 1 and tok_buf is None:
                        v.wait_ge(C.s("d")[0],
                                  d0 + 16 * (2 if lo_dram is not None else 1) * t)
                    v.tensor_scalar(SCR[:], src, STAT[:, 5:6],
                                    STAT[:, 4:5], OP.subtract, OP.mult)
                    C.inc("v", v.tensor_copy(buf, SCR[:]))
                    if lo_dram is not None:
                        C.inc("v", v.tensor_tensor(lobuf, SCR[:], buf,
                                                   OP.subtract))
            blk.vector(_)

            def _(a):
                for t in range(NT):
                    C.inc("a", a.activation(JUNK[:], xres(b, t), AF.Square,
                                            accum_out=STAT[:, 1:2]))
                    a.wait_ge(C.s("v")[0], v0 + nv * t + 1)
                    C.sw(a, "aS", a.activation(STAT[:, 4:5], STAT[:, 3:4],
                                                AF.Ln))
                    C.inc("a", a.activation(STAT[:, 4:5], STAT[:, 4:5], AF.Exp,
                                            scale=-0.5))
            blk.scalar(_)

            def _(s):
                for t in range(NT):
                    buf = (tok_buf[:, t * D:(t + 1) * D] if tok_buf is not None
                           else SCRB[:, (t % 2) * D:(t % 2) * D + D])
                    s.wait_ge(C.s("v")[0], v0 + nv * t + 2)
                    C.inc("d", s.dma_start(dram[t * 128:(t + 1) * 128, :], buf),
                          16)
                    if lo_dram is not None:
                        s.wait_ge(C.s("v")[0], v0 + nv * t + 3)
                        C.inc("d", s.dma_start(
                            lo_dram[t * 128:(t + 1) * 128, :],
                            SCRB[:, ((t + 1) % 2) * D:((t + 1) % 2) * D + D]),
                            16)
                if make_xht:
                    s.wait_ge(C.s("d")[0], C.val("d"))
                    for cc in range(NC):
                        C.inc("d", s.dma_start_transpose(
                            XHT[:, cc * S:(cc + 1) * S],
                            dram[:, cc * 128:(cc + 1) * 128]), 16)
            blk.sync(_)

    def emit_resadd_featT(b, srcT):
        with nc.Block() as blk:
            d0, v0 = C.val("d"), C.val("v")

            def _(s):
                s.wait_ge(C.s("d")[0], d0)
                for t in range(NT):
                    buf = SCRB[:, (t % 2) * D:(t % 2) * D + D]
                    if t >= 2:
                        s.wait_ge(C.s("v")[0], v0 + t - 1)
                    C.inc("d", s.dma_start_transpose(
                        buf, srcT[:, t * 128:(t + 1) * 128]), 16)
            blk.sync(_)

            def _(v):
                for t in range(NT):
                    buf = SCRB[:, (t % 2) * D:(t % 2) * D + D]
                    v.wait_ge(C.s("d")[0], d0 + 16 * (t + 1))
                    C.inc("v", v.tensor_tensor(xres(b, t), xres(b, t), buf,
                                               OP.add))
            blk.vector(_)

    COLS = E(nc.sbuf_tensor("COLS", [128, 24], F32))
    GTT = E(nc.sbuf_tensor("GTT", [128, NC * 128], BF16))
    GTOK = E(nc.sbuf_tensor("GTOK", [128, D], BF16))

    def emit_attn_a(b, XHTOK):
        pes = contextlib.ExitStack()
        PE_ = pes.enter_context
        b1s = contextlib.ExitStack()
        B1_ = b1s.enter_context
        LOCAL = B1_(nc.sbuf_tensor(f"aa{b}_LOCAL", [128, NT * D], BF16))
        EBUF = B1_(nc.sbuf_tensor(f"aa{b}_EBUF", [128, 8 * 512], BF16))
        OTWSB = B1_(nc.sbuf_tensor(f"aa{b}_OTW", [128, 2 * 512], BF16))
        OTOK = B1_(nc.sbuf_tensor(f"aa{b}_OTOK", [128, 4 * D], BF16))
        SUMW = B1_(nc.sbuf_tensor(f"aa{b}_SUMW", [128, 64], F32))
        RS = B1_(nc.sbuf_tensor(f"aa{b}_RS", [128, 64], F32))
        ONF = B1_(nc.sbuf_tensor(f"aa{b}_ONF", [128, D], F32))
        NW, NHP = 7, 8

        with nc.Block() as blk:
            pps = contextlib.ExitStack()
            SA = [pps.enter_context(nc.psum_tensor(f"aa{b}_SA{i}", [128, 512], F32))
                  for i in range(2)]
            SB = [pps.enter_context(nc.psum_tensor(f"aa{b}_SB{i}", [128, 512], F32))
                  for i in range(2)]
            OW = [pps.enter_context(nc.psum_tensor(f"aa{b}_OW{i}", [128, 512], F32))
                  for i in range(2)]
            t0, a0, v0, d0, g0 = (C.val("t"), C.val("a"), C.val("v"),
                                  C.val("d"), C.val("g"))

            def t_smm(n, hp, qt, j):
                return t0 + (n * NHP + hp) * 9 + 2 * qt + j

            def t_av(n, hp):
                return t0 + (n * NHP + hp) * 9 + 9

            def a_exp(n, hp, qt, j):
                return a0 + (n * NHP + hp) * 8 + 2 * qt + j

            def a_hp(n, hp):
                return a0 + (n * NHP + hp + 1) * 8

            def v_evac(n, hp):
                return v0 + n * 13 + hp + 1

            def v_recip(n):
                return v0 + n * 13 + 9

            def v_fold(n, qt):
                return v0 + n * 13 + 9 + qt + 1

            def d_st(n, hp):
                return d0 + n * 192 + 16 * (hp + 1)

            def d_tr(n, qt):
                return d0 + n * 192 + 128 + 16 * (qt + 1)

            def _(g):
                C.inc("g", g.memset(LOCAL[:], 0.0))
            blk.gpsimd(_)

            def _(te):
                C.wait(te, "d", d0)
                C.wait(te, "g", g0 + 1)
                for n in range(NW):
                    for hp in range(NHP):
                        xh_c = XHT[:, hp * S:(hp + 1) * S]
                        for qt in range(4):
                            pa, pb = SA[qt % 2], SB[qt % 2]
                            idx = (n * NHP + hp) * 4 + qt
                            if idx >= 2:
                                pn, rem = divmod(idx - 2, NHP * 4)
                                ph, pq = divmod(rem, 4)
                                te.wait_ge(C.s("a")[0], a_exp(pn, ph, pq, 2))
                            qlo = 256 * n + 128 * qt
                            C.inc("t", te.matmul(
                                pa[:], xh_c[0:64, qlo:qlo + 128],
                                xh_c[0:64, 256 * n:256 * n + 512],
                                start=True, stop=True, tile_position=(0, 0)))
                            C.inc("t", te.matmul(
                                pb[:], xh_c[64:128, qlo:qlo + 128],
                                xh_c[64:128, 256 * n:256 * n + 512],
                                start=True, stop=True, tile_position=(64, 0)))
                        te.wait_ge(C.s("a")[0], a_hp(n, hp))
                        j = n * NHP + hp
                        if j >= 2:
                            pn, ph = divmod(j - 2, NHP)
                            te.wait_ge(C.s("v")[0], v_evac(pn, ph))
                        ow = OW[hp % 2]
                        mm = None
                        for kc in range(4):
                            tt = 2 * n + kc
                            te.matmul(
                                ow[0:64, :],
                                XHTOK[:, tt * D + (2 * hp) * 64:
                                      tt * D + (2 * hp) * 64 + 64],
                                EBUF[:, kc * 512:(kc + 1) * 512],
                                start=(kc == 0), stop=(kc == 3),
                                tile_position=(0, 0))
                            mm = te.matmul(
                                ow[64:128, :],
                                XHTOK[:, tt * D + (2 * hp + 1) * 64:
                                      tt * D + (2 * hp + 1) * 64 + 64],
                                EBUF[:, (4 + kc) * 512:(4 + kc + 1) * 512],
                                start=(kc == 0), stop=(kc == 3),
                                tile_position=(0, 64))
                        C.inc("t", mm)
            blk.tensor(_)

            def _(a):
                for n in range(NW):
                    for hp in range(NHP):
                        for qt in range(4):
                            pa, pb = SA[qt % 2], SB[qt % 2]
                            h0, h1 = 2 * hp, 2 * hp + 1
                            a.wait_ge(C.s("t")[0], t_smm(n, hp, qt, 2))
                            if n > 0 and hp == 0 and qt == 0:
                                a.wait_ge(C.s("v")[0], v_recip(n - 1))
                            C.inc("a", a.activation(
                                EBUF[:, qt * 512:(qt + 1) * 512], pa[:],
                                AF.Exp, scale=0.125,
                                accum_out=SUMW[:, qt * 16 + h0:qt * 16 + h0 + 1]))
                            C.inc("a", a.activation(
                                EBUF[:, (4 + qt) * 512:(4 + qt + 1) * 512],
                                pb[:], AF.Exp, scale=0.125,
                                accum_out=SUMW[:, qt * 16 + h1:qt * 16 + h1 + 1]))
            blk.scalar(_)

            def _(v):
                for n in range(NW):
                    for hp in range(NHP):
                        v.wait_ge(C.s("t")[0], t_av(n, hp))
                        j = n * NHP + hp
                        if j >= 2:
                            pn, ph = divmod(j - 2, NHP)
                            v.wait_ge(C.s("d")[0], d_st(pn, ph))
                        C.inc("v", v.tensor_copy(
                            OTWSB[:, (hp % 2) * 512:(hp % 2) * 512 + 512],
                            OW[hp % 2][:]))
                    v.wait_ge(C.s("a")[0], a0 + (n + 1) * NHP * 8)
                    C.sw(v, "vS", v.reciprocal(RS[:], SUMW[:]))
                    C.inc("v", v.tensor_copy(STAT[:, 8:9], RS[:, 0:1]))
                    for qt in range(4):
                        v.wait_ge(C.s("d")[0], d_tr(n, qt))
                        rsb = RS[:, qt * 16:(qt + 1) * 16].rearrange(
                            "p (h o) -> p h o", o=1).broadcast_to((128, 16, 64))
                        v.tensor_tensor(
                            ONF[:].rearrange("p (h e) -> p h e", h=16),
                            OTOK[:, qt * D:(qt + 1) * D].rearrange(
                                "p (h e) -> p h e", h=16),
                            rsb, OP.mult)
                        tile = 2 * n + qt
                        C.inc("v", v.scalar_tensor_tensor(
                            LOCAL[:, tile * D:(tile + 1) * D], ONF[:],
                            TRI[:, qt:qt + 1],
                            LOCAL[:, tile * D:(tile + 1) * D],
                            OP.mult, OP.add))
            blk.vector(_)

            def _(s):
                for n in range(NW):
                    for hp in range(NHP):
                        s.wait_ge(C.s("v")[0], v_evac(n, hp))
                        C.inc("d", s.dma_start(
                            DS["otw"][hp * 128:(hp + 1) * 128, :],
                            OTWSB[:, (hp % 2) * 512:(hp % 2) * 512 + 512]), 16)
                    for qt in range(4):
                        s.wait_ge(C.s("d")[0], d_st(n, 7))
                        if n > 0:
                            s.wait_ge(C.s("v")[0], v_fold(n - 1, qt))
                        C.inc("d", s.dma_start_transpose(
                            OTOK[:, qt * D:(qt + 1) * D],
                            DS["otw"][:, qt * 128:(qt + 1) * 128]), 16)
            blk.sync(_)
            pps.close()

        with nc.Block() as blk:
            v0b = C.val("v")

            def _(v):
                for t in range(NT):
                    C.inc("v", v.tensor_scalar(
                        LOCAL[:, t * D:(t + 1) * D],
                        LOCAL[:, t * D:(t + 1) * D],
                        DENR[:, t:t + 1], None, OP.mult))
            blk.vector(_)

            def _(s):
                for t in range(NT):
                    s.wait_ge(C.s("v")[0], v0b + t + 1)
                    C.inc("d", s.dma_start(
                        DS["loc"][t * 128:(t + 1) * 128, :],
                        LOCAL[:, t * D:(t + 1) * D]), 16)
            blk.sync(_)

        with nc.Block() as blk:
            pps = contextlib.ExitStack()
            CPP = pps.enter_context(nc.psum_tensor(f"aa{b}_CP", [128, 512], F32))
            CP = [CPP[:, i * 64:(i + 1) * 64] for i in range(8)]
            PT = pps.enter_context(nc.psum_tensor(f"aa{b}_PT", [128, 128], BF16))
            WCB = pps.enter_context(nc.sbuf_tensor(f"aa{b}_WCB", [128, 2 * D], BF16))
            t0, v0, d0 = C.val("t"), C.val("v"), C.val("d")

            def _(s):
                C.wait(s, "g")
                for j in range(8):
                    C.inc("d", s.dma_start(
                        COLS[:, j:j + 1],
                        W["aa_convb"][j * 128:(j + 1) * 128, :]), 16)
                    C.inc("d", s.dma_start(
                        COLS[:, 8 + j:9 + j],
                        W["aa_mixb"][j * 128:(j + 1) * 128, :]), 16)
                    C.inc("d", s.dma_start(
                        COLS[:, 16 + j:17 + j],
                        W["aa_outb"][j * 128:(j + 1) * 128, :]), 16)
                for rc in range(32):
                    if rc >= 2:
                        s.wait_ge(C.s("t")[0], t0 + rc - 1)
                    C.inc("d", s.dma_start(
                        WCB[:, (rc % 2) * D:(rc % 2) * D + D],
                        WB["aa_wcat"][rc * 128:(rc + 1) * 128, :]), 16)
            blk.sync(_)

            def _(te):
                for rc in range(32):
                    te.wait_ge(C.s("d")[0], d0 + 24 * 16 + 16 * (rc + 1))
                    k, cc = divmod(rc, 8)
                    rhs = XHT[:, cc * S + k:cc * S + k + 256].rearrange(
                        "p (t four) -> p t four", four=4)[:, :, 0:1]
                    mm = None
                    for dt in range(8):
                        mm = te.matmul(CP[dt],
                                       WCB[:, (rc % 2) * D + dt * 128:
                                           (rc % 2) * D + (dt + 1) * 128],
                                       rhs, start=(rc == 0), stop=(rc == 31))
                    C.inc("t", mm)
                for cc2 in range(8):
                    te.wait_ge(C.s("v")[0], v0 + 2 * (cc2 + 1))
                    if cc2 >= 1:
                        te.wait_ge(C.s("v")[0], v0 + 16 + cc2)
                    C.inc("t", te.transpose(PT[:],
                                            GTT[:, cc2 * 128:(cc2 + 1) * 128],
                                            IDENT[:]))
            blk.tensor(_)

            def _(v):
                v.wait_ge(C.s("t")[0], t0 + 32)
                C.wait(v, "d", d0 + 24 * 16)
                for dt in range(8):
                    v.tensor_scalar(GTT[:, dt * 128:dt * 128 + 64], CP[dt],
                                    COLS[:, dt:dt + 1], None, OP.add)
                    C.inc("v", v.tensor_copy(
                        GTT[:, dt * 128 + 64:(dt + 1) * 128],
                        GMT[:, dt * 64:(dt + 1) * 64]), 2)
                for cc2 in range(8):
                    v.wait_ge(C.s("t")[0], t0 + 32 + cc2 + 1)
                    C.inc("v", v.tensor_copy(GTOK[:, cc2 * 128:(cc2 + 1) * 128],
                                             PT[:]))
            blk.vector(_)
            pps.close()

        b1s.close()
        LTB = PE_(nc.sbuf_tensor(f"aa{b}_LTB", [128, NC * 512], BF16))
        GTBUF = PE_(nc.sbuf_tensor(f"aa{b}_GTB", [128, NC * 512], BF16))
        GBK = PE_(nc.sbuf_tensor(f"aa{b}_GBK", [128, 2 * 512], BF16))
        MXT = PE_(nc.sbuf_tensor(f"aa{b}_MXT", [128, NC * 512], BF16))
        CIRCB = PE_(nc.sbuf_tensor(f"aa{b}_CIRCB", [128, NT * 512], BF16))
        EG = PE_(nc.sbuf_tensor(f"aa{b}_EG", [128, 2 * 512], BF16))
        RSF = PE_(nc.sbuf_tensor(f"aa{b}_RSF", [1, 2 * 512], F32))
        RSBF = PE_(nc.sbuf_tensor(f"aa{b}_RSBF", [1, 2 * 512], BF16))
        D1F = SCR[:, 0:512]
        D2F = SCR[:, 512:1024]
        RB = SCR[:, 512:1024]
        WMB = PE_(nc.sbuf_tensor(f"aa{b}_WMB", [128, 4 * 1024], BF16))
        AOT = PE_(nc.sbuf_tensor(f"aa{b}_AOT", [128, 512], BF16))

        with nc.Block() as blk:
            pps = contextlib.ExitStack()
            GA = pps.enter_context(nc.psum_tensor(f"aa{b}_GA", [128, 512], F32))
            GB_ = pps.enter_context(nc.psum_tensor(f"aa{b}_GB", [128, 512], F32))
            PS0 = pps.enter_context(nc.psum_tensor(f"aa{b}_PS0", [1, 512], F32))
            PS1 = pps.enter_context(nc.psum_tensor(f"aa{b}_PS1", [1, 512], F32))
            PR = pps.enter_context(nc.psum_tensor(f"aa{b}_PR", [128, 512], F32))
            PO = pps.enter_context(nc.psum_tensor(f"aa{b}_PO", [128, 512], F32))
            PG, PBC, PAO = GA, GB_, PR

            for qb in range(4):
                qsl = slice(qb * 512, (qb + 1) * 512)
                tq, aq, vq, dq = (C.val("t"), C.val("a"), C.val("v"),
                                  C.val("d"))

                def _(s, qb=qb, tq=tq):
                    C.wait(s, "t", tq)
                    C.wait(s, "d")
                    for cc in range(NC):
                        C.inc("d", s.dma_start_transpose(
                            LTB[:, cc * 512:(cc + 1) * 512],
                            DS["loc"][qb * 512:(qb + 1) * 512,
                                      cc * 128:(cc + 1) * 128]), 16)
                    for st in range(NT):
                        C.inc("d", s.dma_start(
                            CIRCB[:, st * 512:(st + 1) * 512],
                            WB["c_circ"][st * 128:(st + 1) * 128,
                                         qb * 512:(qb + 1) * 512]), 16)
                blk.sync(_)
                d_l = C.val("d")

                for hp in range(NHP):
                    h0, h1 = 2 * hp, 2 * hp + 1
                    tb, ab, vb = C.val("t"), C.val("a"), C.val("v")

                    def _(te, hp=hp, tb=tb, ab=ab, vb=vb, qb=qb):
                        xh_c = XHT[:, hp * S:(hp + 1) * S]
                        if hp > 0 or qb > 0:
                            te.wait_ge(C.s("v")[0], vb)
                            te.wait_ge(C.s("a")[0], ab)
                        C.inc("t", te.matmul(
                            GA[:], GTT[0:64, hp * 128:(hp + 1) * 128],
                            xh_c[0:64, qb * 512:(qb + 1) * 512],
                            start=True, stop=True, tile_position=(0, 0)))
                        C.inc("t", te.matmul(
                            GB_[:], GTT[64:128, hp * 128:(hp + 1) * 128],
                            xh_c[64:128, qb * 512:(qb + 1) * 512],
                            start=True, stop=True, tile_position=(64, 0)))
                        te.wait_ge(C.s("a")[0], ab + 2)
                        C.inc("t", te.matmul(PS0[:], ONES[:, 0:1],
                                             EG[:, 0:512],
                                             start=True, stop=True))
                        C.inc("t", te.matmul(PS1[:], ONES[:, 0:1],
                                             EG[:, 512:1024],
                                             start=True, stop=True))
                        te.wait_ge(C.s("v")[0], vb + 1)
                        te.matmul(PR[0:64, :], ONES[0:1, 0:64],
                                  RSBF[0:1, 0:512],
                                  start=True, stop=True, tile_position=(0, 0))
                        C.inc("t", te.matmul(PR[64:128, :], ONES[0:1, 0:64],
                                             RSBF[0:1, 512:1024],
                                             start=True, stop=True,
                                             tile_position=(0, 64)))
                        te.matmul(PO[0:64, :], GTOK[:, h0 * 64:h0 * 64 + 64],
                                  EG[:, 0:512], start=True, stop=True,
                                  tile_position=(0, 0))
                        C.inc("t", te.matmul(PO[64:128, :],
                                             GTOK[:, h1 * 64:h1 * 64 + 64],
                                             EG[:, 512:1024],
                                             start=True, stop=True,
                                             tile_position=(0, 64)))
                    blk.tensor(_)

                    def _(a, tb=tb, vb=vb):
                        a.wait_ge(C.s("t")[0], tb + 2)
                        if True:
                            a.wait_ge(C.s("t")[0], tb)
                        C.inc("a", a.activation(EG[:, 0:512], GA[:], AF.Exp,
                                                scale=0.125))
                        C.inc("a", a.activation(EG[:, 512:1024], GB_[:],
                                                AF.Exp, scale=0.125))
                    blk.scalar(_)

                    def _(v, hp=hp, tb=tb, vb=vb):
                        v.wait_ge(C.s("t")[0], tb + 4)
                        C.sw(v, "vS", v.reciprocal(RSF[0:1, 0:512], PS0[:]))
                        C.sw(v, "vS", v.reciprocal(RSF[0:1, 512:1024], PS1[:]))
                        C.inc("v", v.tensor_copy(RSBF[:], RSF[:]))
                        v.wait_ge(C.s("t")[0], tb + 6)
                        C.inc("v", v.tensor_copy(RB, PR[:]))
                        C.inc("v", v.tensor_tensor(
                            GTBUF[:, hp * 512:(hp + 1) * 512], PO[:], RB,
                            OP.mult))
                    blk.vector(_)

                a_glob_end = C.val("a")
                v_glob_end = C.val("v")
                gate_tend = []
                mix_vend = []
                for dt in range(8):
                    tb, ab, vb, dbq = (C.val("t"), C.val("a"), C.val("v"),
                                       C.val("d"))

                    def _(s, dt=dt):
                        if dt >= 2:
                            s.wait_ge(C.s("t")[0], gate_tend[dt - 2])
                        w0 = (dt % 2) * 1024
                        C.inc("d", s.dma_start(
                            WMB[:, w0:w0 + 1024].rearrange(
                                "p (c n) -> p c n", c=8),
                            WB["aa_m1"][:, dt * 128:(dt + 1) * 128].rearrange(
                                "(c p) n -> p c n", p=128)), 16)
                        C.inc("d", s.dma_start(
                            WMB[:, 2048 + w0:3072 + w0].rearrange(
                                "p (c n) -> p c n", c=8),
                            WB["aa_m2"][:, dt * 128:(dt + 1) * 128].rearrange(
                                "(c p) n -> p c n", p=128)), 16)
                    blk.sync(_)

                    def _(te, dt=dt, tb=tb, ab=ab, vb=vb, dbq=dbq):
                        te.wait_ge(C.s("d")[0], dbq + 32)
                        if dt == 0:
                            te.wait_ge(C.s("a")[0], a_glob_end)
                            te.wait_ge(C.s("v")[0], v_glob_end)
                        else:
                            te.wait_ge(C.s("a")[0], ab)
                        w0 = (dt % 2) * 1024
                        mm = None
                        for cc in range(NC):
                            mm = te.matmul(
                                PG[:], WMB[:, w0 + cc * 128:w0 + (cc + 1) * 128],
                                LTB[:, cc * 512:(cc + 1) * 512],
                                start=(cc == 0), stop=False)
                        for cc in range(NC):
                            mm = te.matmul(
                                PG[:],
                                WMB[:, 2048 + w0 + cc * 128:
                                    2048 + w0 + (cc + 1) * 128],
                                GTBUF[:, cc * 512:(cc + 1) * 512],
                                start=False, stop=(cc == NC - 1))
                        C.inc("t", mm)
                        gate_tend.append(C.val("t"))
                        te.wait_ge(C.s("v")[0], vb)
                        te.wait_ge(C.s("d")[0], d_l)
                        mm = None
                        for st in range(NT):
                            mm = te.matmul(
                                PBC[:], XHTOK[:, st * D + dt * 128:
                                              st * D + (dt + 1) * 128],
                                CIRCB[:, st * 512:(st + 1) * 512],
                                start=(st == 0), stop=(st == NT - 1))
                        C.inc("t", mm)
                    blk.tensor(_)

                    def _(a, dt=dt):
                        a.wait_ge(C.s("t")[0], gate_tend[dt])
                        if dt >= 2:
                            a.wait_ge(C.s("v")[0], mix_vend[dt - 2])
                        C.inc("a", a.activation(
                            GBK[:, (dt % 2) * 512:(dt % 2) * 512 + 512], PG[:],
                            AF.Sigmoid, bias=COLS[:, 8 + dt:9 + dt]))
                    blk.scalar(_)

                    def _(v, dt=dt, tb=tb):
                        C.wait(v, "a")
                        C.wait(v, "t")
                        dsl = slice(dt * 512, (dt + 1) * 512)
                        gsl = slice((dt % 2) * 512, (dt % 2) * 512 + 512)
                        v.tensor_tensor(D1F, LTB[:, dsl], GTBUF[:, dsl],
                                        OP.subtract)
                        v.tensor_tensor(D2F, GBK[:, gsl], D1F, OP.mult)
                        v.tensor_tensor(D1F, D2F, GTBUF[:, dsl], OP.add)
                        C.inc("v", v.tensor_tensor(MXT[:, dsl], D1F,
                                                   PBC[:], OP.add))
                        mix_vend.append(C.val("v"))
                    blk.vector(_)
                v_mx_end = C.val("v")

                outw_tend = []
                aot_dend = []
                for dot in range(8):
                    tb, ab, dbq = C.val("t"), C.val("a"), C.val("d")

                    def _(s, dot=dot):
                        if dot < 2:
                            s.wait_ge(C.s("t")[0], gate_tend[7])
                        else:
                            s.wait_ge(C.s("t")[0], outw_tend[dot - 2])
                        C.inc("d", s.dma_start(
                            WMB[:, (dot % 2) * 1024:
                                (dot % 2) * 1024 + 1024].rearrange(
                                "p (c n) -> p c n", c=8),
                            WB["aa_outw"][:, dot * 128:(dot + 1) * 128].rearrange(
                                "(c p) n -> p c n", p=128)), 16)
                    blk.sync(_)

                    def _(te, dot=dot, tb=tb, ab=ab, dbq=dbq):
                        te.wait_ge(C.s("d")[0], dbq + 16)
                        te.wait_ge(C.s("v")[0], v_mx_end)
                        if dot == 0:
                            pass
                        else:
                            te.wait_ge(C.s("a")[0], ab)
                        w0 = (dot % 2) * 1024
                        mm = None
                        for cc in range(NC):
                            mm = te.matmul(
                                PAO[:], WMB[:, w0 + cc * 128:w0 + (cc + 1) * 128],
                                MXT[:, cc * 512:(cc + 1) * 512],
                                start=(cc == 0), stop=(cc == NC - 1))
                        C.inc("t", mm)
                        outw_tend.append(C.val("t"))
                    blk.tensor(_)

                    def _(a, dot=dot):
                        C.wait(a, "t")
                        if dot >= 1:
                            a.wait_ge(C.s("d")[0], aot_dend[dot - 1])
                        C.inc("a", a.activation(
                            AOT[:], PAO[:], AF.Identity,
                            bias=COLS[:, 16 + dot:17 + dot]))
                    blk.scalar(_)

                    def _(s, dot=dot, qb=qb):
                        C.wait(s, "a")
                        C.inc("d", s.dma_start(
                            DS["aoT"][dot * 128:(dot + 1) * 128,
                                      qb * 512:(qb + 1) * 512],
                            AOT[:]), 16)
                        aot_dend.append(C.val("d"))
                    blk.sync(_)
            pps.close()
        pes.close()
        emit_resadd_featT(b, DS["aoT"])

    def emit_ffn(b, call):
        tag = f"f{b}{call}"
        fes = contextlib.ExitStack()
        FE = fes.enter_context
        GATES = FE(nc.sbuf_tensor(tag + "GATES", [128, NT * 256], BF16))
        RSUMf = FE(nc.sbuf_tensor(tag + "RSUM", [128, 256], F32))

        xls = contextlib.ExitStack()
        XHLT = xls.enter_context(nc.sbuf_tensor(tag + "XHLT", [128, NC * S],
                                                BF16))
        WSHh = xls.enter_context(nc.sbuf_tensor(tag + "WSH", [128, NC * 256],
                                                BF16))
        WSLl = xls.enter_context(nc.sbuf_tensor(tag + "WSL", [128, NC * 256],
                                                BF16))
        with nc.Block() as blk:
            def _(s):
                C.wait(s, "d")
                for cc in range(NC):
                    C.inc("d", s.dma_start_transpose(
                        XHLT[:, cc * S:(cc + 1) * S],
                        DS["xhl"][:, cc * 128:(cc + 1) * 128]), 16)
            blk.sync(_)
            v_ws = []
            for cc in range(NC):
                def _(s, cc=cc):
                    if cc >= 2:
                        s.wait_ge(C.s("v")[0], v_ws[cc - 2])
                    C.inc("d", s.dma_start(
                        SCR[:, (cc % 2) * 256:(cc % 2) * 256 + 256],
                        W["f_wsel"][cc * 128:(cc + 1) * 128, :]), 16)
                blk.sync(_)
                d_ws = C.val("d")

                def _(v, cc=cc, d_ws=d_ws):
                    v.wait_ge(C.s("d")[0], d_ws)
                    csl = slice(cc * 256, (cc + 1) * 256)
                    ssl = slice((cc % 2) * 256, (cc % 2) * 256 + 256)
                    v.tensor_copy(WSHh[:, csl], SCR[:, ssl])
                    G(v)
                    C.inc("v", v.tensor_tensor(WSLl[:, csl], SCR[:, ssl],
                                               WSHh[:, csl], OP.subtract))
                blk.vector(_)
                v_ws.append(C.val("v"))

        with nc.Block() as blk:
            pps = contextlib.ExitStack()
            PGT = [pps.enter_context(nc.psum_tensor(f"{tag}PGT{i}", [128, 256],
                                                    F32)) for i in range(2)]
            SG = xls.enter_context(nc.sbuf_tensor(tag + "SG", [128, 2 * 256],
                                                   F32))
            M12 = xls.enter_context(nc.sbuf_tensor(tag + "M12", [128, 64], F32))
            EQS = xls.enter_context(nc.sbuf_tensor(tag + "EQS", [128, 256],
                                                   F32))
            a_sig = []
            v_gate = []
            for t in range(NT):
                tb, ab, vb = C.val("t"), C.val("a"), C.val("v")

                def _(te, t=t, vb=vb):
                    if t == 0:
                        C.wait(te, "d")
                    if t >= 2:
                        te.wait_ge(C.s("a")[0], a_sig[t - 2])
                    pg = PGT[t % 2]
                    mm = None
                    for cc in range(NC):
                        csl = slice(cc * 256, (cc + 1) * 256)
                        xsl = slice(cc * S + t * 128, cc * S + (t + 1) * 128)
                        mm = te.matmul(pg[:], XHT[:, xsl], WSHh[:, csl],
                                       start=(cc == 0), stop=False)
                        mm = te.matmul(pg[:], XHT[:, xsl], WSLl[:, csl],
                                       start=False, stop=False)
                        mm = te.matmul(pg[:], XHLT[:, xsl], WSHh[:, csl],
                                       start=False, stop=(cc == NC - 1))
                    C.inc("t", mm)
                blk.tensor(_)
                t_mm = C.val("t")

                def _(a, t=t, t_mm=t_mm, vb=vb):
                    a.wait_ge(C.s("t")[0], t_mm)
                    if t >= 2:
                        a.wait_ge(C.s("v")[0], v_gate[t - 2])
                    C.inc("a", a.activation(SG[:, (t % 2) * 256:(t % 2) * 256 + 256],
                                            PGT[t % 2][:], AF.Sigmoid))
                blk.scalar(_)
                a_sig.append(C.val("a"))

                def _(v, t=t):
                    C.wait(v, "a", a_sig[t])
                    sg = SG[:, (t % 2) * 256:(t % 2) * 256 + 256]
                    sg3 = sg.rearrange("p (g e) -> p g e", e=8)
                    C.sw(v, "vS", v.tensor_reduce(M12[:, 0:32], sg3, axis=AXX,
                                                  op=OP.max))
                    m1b = M12[:, 0:32].rearrange("p (g o) -> p g o", o=1
                                                 ).broadcast_to((128, 32, 8))
                    v.tensor_tensor(EQS[:].rearrange("p (g e) -> p g e", e=8),
                                    sg3, m1b, OP.is_ge)
                    G(v)
                    v.scalar_tensor_tensor(SCR[:, 0:256], EQS[:], -1e30, sg,
                                           OP.mult, OP.add)
                    C.sw(v, "vS", v.tensor_reduce(
                        M12[:, 32:64],
                        SCR[:, 0:256].rearrange("p (g e) -> p g e", e=8),
                        axis=AXX, op=OP.max))
                    m2b = M12[:, 32:64].rearrange("p (g o) -> p g o", o=1
                                                  ).broadcast_to((128, 32, 8))
                    v.tensor_tensor(EQS[:].rearrange("p (g e) -> p g e", e=8),
                                    sg3, m2b, OP.is_ge)
                    G(v)
                    C.inc("v", v.tensor_tensor(
                        GATES[:, t * 256:(t + 1) * 256], sg, EQS[:], OP.mult))
                blk.vector(_)
                v_gate.append(C.val("v"))
            pps.close()
        xls.close()

        if _FFN_UPTO < 2:
            fes.close()
            return
        with nc.Block() as blk:
            pps = contextlib.ExitStack()
            QP = [pps.enter_context(nc.psum_tensor(f"{tag}QP{i}", [128, 512],
                                                   F32)) for i in range(4)]
            WQKB = pps.enter_context(nc.sbuf_tensor(tag + "WQKB",
                                                    [128, NC * 2048], BF16))
            QKB = pps.enter_context(nc.sbuf_tensor(tag + "QKB", [128, 2048],
                                                   BF16))
            d0 = C.val("d")

            def _(s):
                for cc in range(NC):
                    C.inc("d", s.dma_start(
                        WQKB[:, cc * 2048:(cc + 1) * 2048],
                        WB["f_wqk"][cc * 128:(cc + 1) * 128, :]), 16)
            blk.sync(_)
            d_w = C.val("d")
            v_rope = []
            for t in range(NT):
                def _(te, t=t):
                    te.wait_ge(C.s("d")[0], d_w)
                    if t >= 1:
                        te.wait_ge(C.s("v")[0], v_rope[t - 1])
                    for nb in range(4):
                        mm = None
                        for cc in range(NC):
                            mm = te.matmul(
                                QP[nb][:],
                                XHT[:, cc * S + t * 128:cc * S + (t + 1) * 128],
                                WQKB[:, cc * 2048 + nb * 512:
                                     cc * 2048 + (nb + 1) * 512],
                                start=(cc == 0), stop=(cc == NC - 1))
                        C.inc("t", mm)
                blk.tensor(_)
                t_qk = C.val("t")

                def _(v, t=t, t_qk=t_qk):
                    last = [None]
                    v.wait_ge(C.s("t")[0], t_qk)
                    if t >= 1:
                        C.wait(v, "d")
                    cosb = COS[:, t * 16:(t + 1) * 16].rearrange(
                        "p (o i) -> p o i", o=1).broadcast_to((128, 8, 16))
                    sinb = SIN[:, t * 16:(t + 1) * 16].rearrange(
                        "p (o i) -> p o i", o=1).broadcast_to((128, 8, 16))
                    for half in range(2):
                        for nb in range(2):
                            pp = QP[half * 2 + nb][:]
                            p3 = pp.rearrange("p (h x) -> p h x", h=8)
                            x1 = p3[:, :, 0:16]
                            x2 = p3[:, :, 16:32]
                            out = QKB[:, half * 1024 + nb * 512:
                                      half * 1024 + (nb + 1) * 512].rearrange(
                                      "p (h x) -> p h x", h=8)
                            t1 = SCR[:, 0:128].rearrange("p (h x) -> p h x", h=8)
                            t2 = SCR[:, 128:256].rearrange("p (h x) -> p h x", h=8)
                            v.tensor_tensor(t1, x1, cosb, OP.mult)
                            v.tensor_tensor(t2, x2, sinb, OP.mult)
                            v.tensor_tensor(out[:, :, 0:16], t1, t2, OP.subtract)
                            v.tensor_tensor(t1, x1, sinb, OP.mult)
                            v.tensor_tensor(t2, x2, cosb, OP.mult)
                            v.tensor_tensor(out[:, :, 16:32], t1, t2, OP.add)
                            last[0] = v.tensor_copy(out[:, :, 32:64],
                                                    p3[:, :, 32:64])
                    C.inc("v", last[0])
                blk.vector(_)
                v_rope.append(C.val("v"))

                def _(s, t=t):
                    s.wait_ge(C.s("v")[0], v_rope[t])
                    C.inc("d", s.dma_start(DS["qd"][t * 128:(t + 1) * 128, :],
                                           QKB[:, 0:1024]), 16)
                    C.inc("d", s.dma_start(DS["kd"][t * 128:(t + 1) * 128, :],
                                           QKB[:, 1024:2048]), 16)
                blk.sync(_)
            pps.close()

        if _FFN_UPTO < 3:
            fes.close()
            return
        with nc.Block() as blk:
            pps = contextlib.ExitStack()
            SA = [pps.enter_context(nc.psum_tensor(f"{tag}SA{i}", [128, 512],
                                                   F32)) for i in range(2)]
            SB = [pps.enter_context(nc.psum_tensor(f"{tag}SB{i}", [128, 512],
                                                   F32)) for i in range(2)]
            PAV = [pps.enter_context(nc.psum_tensor(f"{tag}PAV{i}", [65, 512],
                                                    F32)) for i in range(2)]
            ZP = [pps.enter_context(nc.psum_tensor(f"{tag}ZP{i}", [128, 512],
                                                   F32)) for i in range(2)]
            QT = pps.enter_context(nc.sbuf_tensor(tag + "QT", [128, S], BF16))
            KT = pps.enter_context(nc.sbuf_tensor(tag + "KT", [128, S], BF16))
            VAUG = pps.enter_context(nc.sbuf_tensor(tag + "VAUG",
                                                    [128, 2 * NT * 68], BF16))
            EB = pps.enter_context(nc.sbuf_tensor(tag + "EB", [128, 2 * NT * 512],
                                                  BF16))
            WVB = pps.enter_context(nc.sbuf_tensor(tag + "WVB", [128, 4096],
                                                   BF16))
            OTS = pps.enter_context(nc.sbuf_tensor(tag + "OTS", [128, 512], BF16))
            SRW = SCR[:, 0:512]
            ZC = SCR[:, 512:1024]

            def _(g):
                C.inc("g", g.memset(
                    VAUG[:].rearrange("p (k s) -> p k s", s=68)[:, :, 64:65],
                    1.0))
            blk.gpsimd(_)
            g_ones = C.val("g")

            for hp in range(8):
                h0, h1 = 2 * hp, 2 * hp + 1
                d_qk = []

                def _(s, hp=hp):
                    C.wait(s, "d")
                    if hp >= 1:
                        C.wait(s, "t")
                    C.inc("d", s.dma_start_transpose(
                        QT[:], DS["qd"][:, hp * 128:(hp + 1) * 128]), 16)
                    C.inc("d", s.dma_start_transpose(
                        KT[:], DS["kd"][:, hp * 128:(hp + 1) * 128]), 16)
                    d_qk.append(C.val("d"))
                blk.sync(_)

                for j, h in enumerate((h0, h1)):
                    def _(s, j=j, h=h):
                        if not (hp == 0 and j == 0):
                            C.wait(s, "t")
                        C.inc("d", s.dma_start(
                            WVB[:].rearrange("p (c n) -> p c n", c=8),
                            WB["f_wv"][h].rearrange("(c p) n -> p c n", p=128)),
                            16)
                        d_qk.append(C.val("d"))
                    blk.sync(_)
                    d_wv = C.val("d")
                    v_comb = []
                    for t in range(NT):
                        def _(te, j=j, t=t, hp=hp, d_wv=d_wv):
                            te.wait_ge(C.s("d")[0], d_wv)
                            if t >= 2:
                                te.wait_ge(C.s("v")[0], v_comb[t - 2])
                            elif not (hp == 0 and j == 0):
                                C.wait(te, "v")
                            mm = None
                            for cc in range(NC):
                                mm = te.matmul(
                                    ZP[t % 2][:],
                                    XHT[:, cc * S + t * 128:cc * S + (t + 1) * 128],
                                    WVB[:, cc * 512:(cc + 1) * 512],
                                    start=(cc == 0), stop=(cc == NC - 1))
                            C.inc("t", mm)
                        blk.tensor(_)
                        t_z = C.val("t")

                        def _(v, j=j, h=h, t=t, t_z=t_z):
                            v.wait_ge(C.s("t")[0], t_z)
                            gvb = GATES[:, t * 256 + h * 8:t * 256 + (h + 1) * 8
                                        ].rearrange("p (e o) -> p e o", o=1
                                                    ).broadcast_to((128, 8, 64))
                            zin = ZP[t % 2][:].rearrange("p (e x) -> p e x", x=64)
                            zout = ZC.rearrange("p (x e) -> p x e", e=8
                                                ).transpose((0, 2, 1))
                            v.tensor_tensor(zout, zin, gvb, OP.mult)
                            with nc.allow_low_precision(
                                    "fp32-internal reduce, bf16 store"):
                                C.sw(v, "vS", v.tensor_reduce(
                                    VAUG[:, (j * NT + t) * 68:
                                         (j * NT + t) * 68 + 64],
                                    ZC.rearrange("p (x e) -> p x e", e=8),
                                    axis=AXX, op=OP.add))
                            C.inc("v", v.nop())
                        blk.vector(_)
                        v_comb.append(C.val("v"))

                for qb in range(4):
                    a_exp = []
                    for kc in range(NT):
                        def _(te, hp=hp, qb=qb, kc=kc):
                            if kc == 0 and qb == 0:
                                C.wait(te, "v")
                                te.wait_ge(C.s("d")[0], d_qk[0])
                            idx = kc
                            if len(a_exp) >= 2:
                                te.wait_ge(C.s("a")[0], a_exp[-2])
                            elif qb > 0 or hp > 0:
                                C.wait(te, "a")
                            C.inc("t", te.matmul(
                                SA[kc % 2][:], KT[0:64, kc * 128:(kc + 1) * 128],
                                QT[0:64, qb * 512:(qb + 1) * 512],
                                start=True, stop=True, tile_position=(0, 0)))
                            C.inc("t", te.matmul(
                                SB[kc % 2][:], KT[64:128, kc * 128:(kc + 1) * 128],
                                QT[64:128, qb * 512:(qb + 1) * 512],
                                start=True, stop=True, tile_position=(64, 0)))
                        blk.tensor(_)
                        t_s = C.val("t")

                        def _(a, kc=kc, t_s=t_s):
                            a.wait_ge(C.s("t")[0], t_s)
                            C.inc("a", a.activation(
                                EB[:, kc * 512:(kc + 1) * 512], SA[kc % 2][:],
                                AF.Exp, scale=0.125))
                            C.inc("a", a.activation(
                                EB[:, (NT + kc) * 512:(NT + kc + 1) * 512],
                                SB[kc % 2][:], AF.Exp, scale=0.125))
                        blk.scalar(_)
                        a_exp.append(C.val("a"))

                    for j, h in enumerate((h0, h1)):
                        def _(te, j=j, qb=qb):
                            te.wait_ge(C.s("a")[0], a_exp[-1])
                            te.wait_ge(C.s("g")[0], g_ones)
                            if True:
                                C.wait(te, "v")
                            mm = None
                            for kc in range(NT):
                                mm = te.matmul(
                                    PAV[j][:],
                                    VAUG[:, (j * NT + kc) * 68:
                                         (j * NT + kc) * 68 + 65],
                                    EB[:, (j * NT + kc) * 512:
                                       (j * NT + kc + 1) * 512],
                                    start=(kc == 0), stop=(kc == NT - 1))
                            C.inc("t", mm)
                        blk.tensor(_)
                        t_av = C.val("t")

                        def _(v, j=j, t_av=t_av):
                            v.wait_ge(C.s("t")[0], t_av)
                            C.wait(v, "d")
                            v.tensor_copy(OTS[0:64, :], PAV[j][0:64, :])
                            C.inc("v", v.tensor_copy(SRW[64:65, 0:512],
                                                     PAV[j][64:65, :]))
                        blk.vector(_)
                        v_o = C.val("v")

                        def _(s, j=j, h=h, qb=qb, v_o=v_o):
                            s.wait_ge(C.s("v")[0], v_o)
                            C.inc("d", s.dma_start(
                                DS["od"][h * 64:(h + 1) * 64,
                                         qb * 512:(qb + 1) * 512],
                                OTS[0:64, :]), 16)
                            C.inc("d", s.dma_start(
                                sums_d[h:h + 1, qb * 512:(qb + 1) * 512],
                                SRW[64:65, 0:512]), 16)
                        blk.sync(_)
            pps.close()

        if _FFN_UPTO < 4:
            fes.close()
            return
        with nc.Block() as blk:
            pps = contextlib.ExitStack()
            PTR = pps.enter_context(nc.psum_tensor(tag + "PTR", [128, 16], F32))
            S16 = pps.enter_context(nc.sbuf_tensor(tag + "S16", [16, S], F32))
            OTOK = pps.enter_context(nc.sbuf_tensor(tag + "OTOK", [128, D], BF16))
            GS = pps.enter_context(nc.sbuf_tensor(tag + "GS", [128, 128], F32))
            OPX = pps.enter_context(nc.sbuf_tensor(tag + "OPX", [128, 16 * 512],
                                                   BF16))
            d0 = C.val("d")

            def _(s):
                C.wait(s, "d")
                C.inc("d", s.dma_start(S16[:], sums_d), 16)
            blk.sync(_)
            d_s16 = C.val("d")

            t_tr = []
            v_rs = []
            for t in range(NT):
                def _(te, t=t):
                    te.wait_ge(C.s("d")[0], d_s16)
                    if t >= 1:
                        te.wait_ge(C.s("v")[0], v_rs[t - 1])
                    C.inc("t", te.transpose(PTR[:],
                                            S16[0:16, t * 128:(t + 1) * 128],
                                            IDENTF[0:16, 0:16]))
                blk.tensor(_)
                t_tr.append(C.val("t"))

                def _(v, t=t):
                    v.wait_ge(C.s("t")[0], t_tr[t])
                    C.sw(v, "vS", v.reciprocal(RSUMf[:, t * 16:(t + 1) * 16],
                                               PTR[:]))
                    C.inc("v", v.nop())
                blk.vector(_)
                v_rs.append(C.val("v"))

            v_opx = []
            for t in range(NT):
                def _(s, t=t):
                    if t == 0:
                        C.wait(s, "d")
                    if t >= 1:
                        s.wait_ge(C.s("v")[0], v_opx[t - 1])
                    C.inc("d", s.dma_start_transpose(
                        OTOK[:], DS["od"][:, t * 128:(t + 1) * 128]), 16)
                blk.sync(_)
                d_ot = C.val("d")

                def _(v, t=t, d_ot=d_ot):
                    v.wait_ge(C.s("d")[0], d_ot)
                    if t >= 1:
                        C.wait(v, "d")
                    v.wait_ge(C.s("v")[0], v_rs[NT - 1])
                    gob = GATES[:, t * 256 + 128:(t + 1) * 256].rearrange(
                        "p (h e) -> p h e", e=8)
                    rsb = RSUMf[:, t * 16:(t + 1) * 16].rearrange(
                        "p (h o) -> p h o", o=1).broadcast_to((128, 16, 8))
                    v.tensor_tensor(GS[:].rearrange("p (h e) -> p h e", e=8),
                                    gob, rsb, OP.mult)
                    G(v)
                    lastv = None
                    for h in range(16):
                        orep = OTOK[:, h * 64:(h + 1) * 64].rearrange(
                            "p (o x) -> p o x", o=1).broadcast_to((128, 8, 64))
                        gsb = GS[:, h * 8:(h + 1) * 8].rearrange(
                            "p (e o) -> p e o", o=1).broadcast_to((128, 8, 64))
                        lastv = v.tensor_tensor(
                            OPX[:, h * 512:(h + 1) * 512].rearrange(
                                "p (e x) -> p e x", e=8),
                            orep, gsb, OP.mult)
                    C.inc("v", lastv)
                blk.vector(_)
                v_opx.append(C.val("v"))

                def _(s, t=t):
                    s.wait_ge(C.s("v")[0], v_opx[t])
                    C.inc("d", s.dma_start(DS["opd"][t * 128:(t + 1) * 128, :],
                                           OPX[:]), 16)
                blk.sync(_)
            pps.close()

        if _FFN_UPTO < 5:
            fes.close()
            return
        with nc.Block() as blk:
            pps = contextlib.ExitStack()
            TCH = pps.enter_context(nc.sbuf_tensor(tag + "TCH", [128, 2 * S],
                                                   BF16))
            d_pr = C.val("d")

            def _(s):
                C.wait(s, "d", d_pr)
                dch = []
                for ch in range(64):
                    if ch >= 2:
                        s.wait_ge(C.s("d")[0], dch[ch - 2])
                    C.inc("d", s.dma_start_transpose(
                        TCH[:, (ch % 2) * S:(ch % 2) * S + S],
                        DS["opd"][:, ch * 128:(ch + 1) * 128]), 16)
                    s.wait_ge(C.s("d")[0], C.val("d"))
                    C.inc("d", s.dma_start(
                        DS["opT"][ch * 128:(ch + 1) * 128, :],
                        TCH[:, (ch % 2) * S:(ch % 2) * S + S]), 16)
                    dch.append(C.val("d"))
            blk.sync(_)
            pps.close()

        if _FFN_UPTO < 6:
            fes.close()
            return
        LN1Gr = FE(nc.sbuf_tensor(tag + "LN1G", [128, D], F32))
        LN1Br = FE(nc.sbuf_tensor(tag + "LN1B", [128, D], F32))
        LN2Gr = FE(nc.sbuf_tensor(tag + "LN2G", [128, D], F32))
        LN2Br = FE(nc.sbuf_tensor(tag + "LN2B", [128, D], F32))
        B1C = FE(nc.sbuf_tensor(tag + "B1C", [128, 32], F32))
        B2C = FE(nc.sbuf_tensor(tag + "B2C", [128, 8], F32))
        ylns = contextlib.ExitStack()
        YLN = ylns.enter_context(nc.sbuf_tensor(tag + "YLN", [128, NT * D],
                                                BF16))
        with nc.Block() as blk:
            def _(s):
                C.inc("d", s.dma_start(LN1Gr[:], W["f_ln1g"]), 16)
                C.inc("d", s.dma_start(LN1Br[:], W["f_ln1b"]), 16)
                C.inc("d", s.dma_start(LN2Gr[:], W["f_ln2g"]), 16)
                C.inc("d", s.dma_start(LN2Br[:], W["f_ln2b"]), 16)
                C.inc("d", s.dma_start(
                    B1C[:].rearrange("p (c o) -> p c o", o=1),
                    W["f_b1"].rearrange("(c p) o -> p c o", p=128),
                    allow_slow_non_contiguous=True), 16)
                C.inc("d", s.dma_start(
                    B2C[:].rearrange("p (c o) -> p c o", o=1),
                    W["f_b2"].rearrange("(c p) o -> p c o", p=128),
                    allow_slow_non_contiguous=True), 16)
            blk.sync(_)
        with nc.Block() as blk:
            pps = contextlib.ExitStack()
            PW = [pps.enter_context(nc.psum_tensor(f"{tag}PW{i}", [128, 1024],
                                                   F32)) for i in range(4)]
            LHW = [pps.enter_context(nc.sbuf_tensor(f"{tag}LHW{i}", [128, 512],
                                                    BF16)) for i in range(2)]
            WOB = [pps.enter_context(nc.sbuf_tensor(f"{tag}WOB{i}", [128, 1024],
                                                    BF16)) for i in range(2)]
            d_all = C.val("d")
            XRT = pps.enter_context(nc.sbuf_tensor(tag + "XRT", [128, 4 * D],
                                                   BF16))
            for grp in range(4):
                def _(s, grp=grp):
                    if grp == 0:
                        C.wait(s, "d")
                    else:
                        C.wait(s, "v")
                    for ti in range(4):
                        t = grp * 4 + ti
                        C.inc("d", s.dma_start(
                            XRT[:, ti * D:(ti + 1) * D],
                            DS["xh"][t * 128:(t + 1) * 128, :]), 16)
                blk.sync(_)
                d_xr = C.val("d")
                t_wo = []
                for ce in range(64):
                    h, ep = divmod(ce, 4)

                    def _(s, grp=grp, ce=ce, h=h, ep=ep):
                        if grp == 0 and ce == 0:
                            C.wait(s, "d", d_all)
                        if ce >= 2 or grp > 0:
                            k = ce - 2 if ce >= 2 else 62
                            s.wait_ge(C.s("t")[0],
                                      t_wo[ce - 2] if ce >= 2 else t_wo_prev[62 + (ce % 2)])
                        C.inc("d", s.dma_start(
                            LHW[ce % 2][:],
                            DS["opT"][ce * 128:(ce + 1) * 128,
                                      grp * 512:(grp + 1) * 512]), 16)
                        C.inc("d", s.dma_start(
                            WOB[ce % 2][:].rearrange("p (a n) -> p a n", a=1),
                            WB["f_wo"][h, ep].rearrange("p (a n) -> p a n", a=1)),
                            16)
                    blk.sync(_)
                    d_lw = C.val("d")

                    def _(te, grp=grp, ce=ce, d_lw=d_lw):
                        te.wait_ge(C.s("d")[0], d_lw)
                        if ce == 0 and grp > 0:
                            C.wait(te, "v")
                        mm = None
                        for ti in range(4):
                            tsl = slice(ti * 128, (ti + 1) * 128)
                            for nh in range(2):
                                mm = te.matmul(
                                    PW[ti][:, nh * 512:(nh + 1) * 512],
                                    LHW[ce % 2][:, tsl],
                                    WOB[ce % 2][:, nh * 512:(nh + 1) * 512],
                                    start=(ce == 0), stop=(ce == 63))
                        C.inc("t", mm)
                    blk.tensor(_)
                    t_wo.append(C.val("t"))
                t_wo_prev = t_wo

                def _(v, grp=grp, d_xr=d_xr):
                    v.wait_ge(C.s("t")[0], t_wo[63])
                    v.wait_ge(C.s("d")[0], d_xr)
                    for ti in range(4):
                        t = grp * 4 + ti
                        C.inc("v", v.tensor_tensor(
                            YLN[:, t * D:(t + 1) * D], PW[ti][:],
                            XRT[:, ti * D:(ti + 1) * D], OP.add))
                blk.vector(_)
            pps.close()

        if _FFN_UPTO < 7:
            ylns.close()
            fes.close()
            return
        with nc.Block() as blk:
            a0, v0 = C.val("a"), C.val("v")

            def _(v):
                C.wait(v, "d")
                for t in range(NT):
                    src = YLN[:, t * D:(t + 1) * D]
                    C.sw(v, "vS", v.tensor_reduce(STAT[:, 0:1], src, axis=AXX,
                                                  op=OP.add))
                    C.sw(v, "vS", v.tensor_scalar(STAT[:, 5:6], STAT[:, 0:1],
                                                  1.0 / D, None, OP.mult))
                    v.wait_ge(C.s("a")[0], a0 + 2 * t + 1)
                    C.sw(v, "vS", v.tensor_scalar(STAT[:, 3:4], STAT[:, 1:2],
                                                  1.0 / D, None, OP.mult))
                    C.sw(v, "vS", v.tensor_tensor(STAT[:, 2:3], STAT[:, 5:6],
                                                  STAT[:, 5:6], OP.mult))
                    C.sw(v, "vS", v.tensor_tensor(STAT[:, 3:4], STAT[:, 3:4],
                                                  STAT[:, 2:3], OP.subtract))
                    C.inc("v", v.tensor_scalar(STAT[:, 3:4], STAT[:, 3:4],
                                               1e-5, None, OP.add))
                    v.wait_ge(C.s("a")[0], a0 + 2 * t + 2)
                    v.tensor_scalar(SCR[:], src, STAT[:, 5:6], STAT[:, 4:5],
                                    OP.subtract, OP.mult)
                    v.tensor_tensor(SCR[:], SCR[:], LN1Gr[:], OP.mult)
                    C.inc("v", v.tensor_tensor(src, SCR[:], LN1Br[:], OP.add))
            blk.vector(_)

            def _(a):
                for t in range(NT):
                    C.inc("a", a.activation(JUNK[:], YLN[:, t * D:(t + 1) * D],
                                            AF.Square, accum_out=STAT[:, 1:2]))
                    a.wait_ge(C.s("v")[0], v0 + 2 * t + 1)
                    C.sw(a, "aS", a.activation(STAT[:, 4:5], STAT[:, 3:4], AF.Ln))
                    C.inc("a", a.activation(STAT[:, 4:5], STAT[:, 4:5], AF.Exp,
                                            scale=-0.5))
            blk.scalar(_)

            def _(s):
                for t in range(NT):
                    s.wait_ge(C.s("v")[0], v0 + 2 * (t + 1))
                    C.inc("d", s.dma_start(DS["l1"][t * 128:(t + 1) * 128, :],
                                           YLN[:, t * D:(t + 1) * D]), 16)
            blk.sync(_)

        if _FFN_UPTO < 8:
            ylns.close()
            fes.close()
            return
        ylns.close()

        with nc.Block() as blk:
            pps = contextlib.ExitStack()
            HP = [pps.enter_context(nc.psum_tensor(f"{tag}HP{i}", [128, 512],
                                                   F32)) for i in range(2)]
            OP2 = [pps.enter_context(nc.psum_tensor(f"{tag}OP2{i}", [128, 512],
                                                    F32)) for i in range(2)]
            L1T = pps.enter_context(nc.sbuf_tensor(tag + "L1T", [128, NC * 512],
                                                   BF16))
            HT = pps.enter_context(nc.sbuf_tensor(tag + "HT", [128, 32 * 512],
                                                  BF16))
            W1C = [pps.enter_context(nc.sbuf_tensor(f"{tag}W1C{i}", [128, 1024],
                                                    BF16)) for i in range(2)]
            W2C = [pps.enter_context(nc.sbuf_tensor(f"{tag}W2C0", [128, 4096],
                                                    BF16))] * 2
            OT2 = pps.enter_context(nc.sbuf_tensor(tag + "OT2", [128, 2 * 512],
                                                   BF16))

            for tb in range(4):
                d0 = C.val("d")

                def _(s, tb=tb, d0=d0):
                    C.wait(s, "d", d0)
                    if tb >= 1:
                        C.wait(s, "t")
                    for cc in range(NC):
                        C.inc("d", s.dma_start_transpose(
                            L1T[:, cc * 512:(cc + 1) * 512],
                            DS["l1"][tb * 512:(tb + 1) * 512,
                                     cc * 128:(cc + 1) * 128]), 16)
                blk.sync(_)
                d_l1 = C.val("d")

                a_relu = []
                t_w1 = []
                for ft in range(32):
                    def _(s, ft=ft):
                        if ft >= 2:
                            s.wait_ge(C.s("t")[0], t_w1[ft - 2])
                        C.inc("d", s.dma_start(
                            W1C[ft % 2][:].rearrange("p (c n) -> p c n", c=8),
                            WB["f_w1"][:, ft * 128:(ft + 1) * 128].rearrange(
                                "(c p) n -> p c n", p=128)), 16)
                    blk.sync(_)
                    d_w1 = C.val("d")

                    def _(te, ft=ft, d_w1=d_w1, d_l1=d_l1):
                        te.wait_ge(C.s("d")[0], max(d_w1, d_l1))
                        if ft >= 2:
                            te.wait_ge(C.s("a")[0], a_relu[ft - 2])
                        mm = None
                        for cc in range(NC):
                            mm = te.matmul(
                                HP[ft % 2][:],
                                W1C[ft % 2][:, cc * 128:(cc + 1) * 128],
                                L1T[:, cc * 512:(cc + 1) * 512],
                                start=(cc == 0), stop=(cc == NC - 1))
                        C.inc("t", mm)
                    blk.tensor(_)
                    t_w1.append(C.val("t"))

                    def _(a, ft=ft):
                        a.wait_ge(C.s("t")[0], t_w1[ft])
                        C.inc("a", a.activation(HT[:, ft * 512:(ft + 1) * 512],
                                                HP[ft % 2][:], AF.Relu,
                                                bias=B1C[:, ft:ft + 1]))
                    blk.scalar(_)
                    a_relu.append(C.val("a"))

                t_w2 = []
                a_o2 = []
                for dt in range(8):
                    def _(s, dt=dt, tb=tb):
                        if dt >= 1:
                            s.wait_ge(C.s("t")[0], t_w2[dt - 1])
                        C.inc("d", s.dma_start(
                            W2C[dt % 2][:].rearrange("p (c n) -> p c n", c=32),
                            WB["f_w2"][:, dt * 128:(dt + 1) * 128].rearrange(
                                "(c p) n -> p c n", p=128)), 16)
                    blk.sync(_)
                    d_w2 = C.val("d")

                    def _(te, dt=dt, d_w2=d_w2):
                        te.wait_ge(C.s("d")[0], d_w2)
                        te.wait_ge(C.s("a")[0], a_relu[31])
                        if dt >= 2:
                            te.wait_ge(C.s("a")[0], a_o2[dt - 2])
                        mm = None
                        for fc in range(32):
                            mm = te.matmul(
                                OP2[dt % 2][:],
                                W2C[dt % 2][:, fc * 128:(fc + 1) * 128],
                                HT[:, fc * 512:(fc + 1) * 512],
                                start=(fc == 0), stop=(fc == 31))
                        C.inc("t", mm)
                    blk.tensor(_)
                    t_w2.append(C.val("t"))

                    def _(a, dt=dt):
                        a.wait_ge(C.s("t")[0], t_w2[dt])
                        if dt >= 2:
                            C.wait(a, "d")
                        C.inc("a", a.activation(
                            OT2[:, (dt % 2) * 512:(dt % 2) * 512 + 512],
                            OP2[dt % 2][:], AF.Identity,
                            bias=B2C[:, dt:dt + 1]))
                    blk.scalar(_)
                    a_o2.append(C.val("a"))

                    def _(s, dt=dt, tb=tb):
                        C.wait(s, "a", a_o2[dt])
                        C.inc("d", s.dma_start(
                            DS["m2T"][dt * 128:(dt + 1) * 128,
                                      tb * 512:(tb + 1) * 512],
                            OT2[:, (dt % 2) * 512:(dt % 2) * 512 + 512]), 16)
                    blk.sync(_)
            pps.close()

        if _FFN_UPTO < 9:
            fes.close()
            return
        with nc.Block() as blk:
            d0 = C.val("d")
            v_fin = []
            LB = FE(nc.sbuf_tensor(tag + "LB", [128, 2 * D], BF16))
            for t in range(NT):
                def _(s, t=t):
                    if t == 0:
                        C.wait(s, "d", d0)
                    if t >= 2:
                        s.wait_ge(C.s("v")[0], v_fin[t - 2])
                    C.inc("d", s.dma_start_transpose(
                        SCRB[:, (t % 2) * D:(t % 2) * D + D],
                        DS["m2T"][:, t * 128:(t + 1) * 128]), 16)
                    C.inc("d", s.dma_start(
                        LB[:, (t % 2) * D:(t % 2) * D + D],
                        DS["l1"][t * 128:(t + 1) * 128, :]), 16)
                blk.sync(_)
                d_m2 = C.val("d")
                a0, v0 = C.val("a"), C.val("v")

                def _(v, t=t, d_m2=d_m2, a0=a0, v0=v0):
                    v.wait_ge(C.s("d")[0], d_m2)
                    v.tensor_tensor(SCR[:], LB[:, (t % 2) * D:(t % 2) * D + D],
                                    SCRB[:, (t % 2) * D:(t % 2) * D + D], OP.add)
                    C.sw(v, "vS", v.tensor_reduce(STAT[:, 0:1], SCR[:],
                                                  axis=AXX, op=OP.add))
                    C.sw(v, "vS", v.tensor_scalar(STAT[:, 5:6], STAT[:, 0:1],
                                                  1.0 / D, None, OP.mult))
                    C.inc("v", v.nop())
                    v.wait_ge(C.s("a")[0], a0 + 1)
                    C.sw(v, "vS", v.tensor_scalar(STAT[:, 3:4], STAT[:, 1:2],
                                                  1.0 / D, None, OP.mult))
                    C.sw(v, "vS", v.tensor_tensor(STAT[:, 2:3], STAT[:, 5:6],
                                                  STAT[:, 5:6], OP.mult))
                    C.sw(v, "vS", v.tensor_tensor(STAT[:, 3:4], STAT[:, 3:4],
                                                  STAT[:, 2:3], OP.subtract))
                    C.inc("v", v.tensor_scalar(STAT[:, 3:4], STAT[:, 3:4],
                                               1e-5, None, OP.add))
                    v.wait_ge(C.s("a")[0], a0 + 2)
                    v.tensor_scalar(SCR[:], SCR[:], STAT[:, 5:6], STAT[:, 4:5],
                                    OP.subtract, OP.mult)
                    v.tensor_tensor(SCR[:], SCR[:], LN2Gr[:], OP.mult)
                    v.tensor_tensor(SCR[:], SCR[:], LN2Br[:], OP.add)
                    C.inc("v", v.tensor_tensor(xres(b, t), xres(b, t), SCR[:],
                                               OP.add))
                blk.vector(_)
                v_fin.append(C.val("v"))

                def _(a, t=t, a0=a0, v0=v0):
                    a.wait_ge(C.s("v")[0], v0 + 1)
                    C.inc("a", a.activation(JUNK[:], SCR[:], AF.Square,
                                            accum_out=STAT[:, 1:2]))
                    a.wait_ge(C.s("v")[0], v0 + 2)
                    C.sw(a, "aS", a.activation(STAT[:, 4:5], STAT[:, 3:4], AF.Ln))
                    C.inc("a", a.activation(STAT[:, 4:5], STAT[:, 4:5], AF.Exp,
                                            scale=-0.5))
                blk.scalar(_)
        fes.close()

    def emit_attn_b(b):
        tag = f"ab{b}"
        bes = contextlib.ExitStack()
        BE = bes.enter_context
        WALL = BE(nc.sbuf_tensor(tag + "WALL", [128, NT * 264], F32))
        IDXA = BE(nc.sbuf_tensor(tag + "IDXA", [128, NT * 33], I32))

        d1s = contextlib.ExitStack()
        WWH = d1s.enter_context(nc.sbuf_tensor(tag + "WWH", [128, NC * 152], BF16))
        WWL = d1s.enter_context(nc.sbuf_tensor(tag + "WWL", [128, NC * 152], BF16))
        WQB = d1s.enter_context(nc.sbuf_tensor(tag + "WQB", [128, 152], F32))
        KWR = d1s.enter_context(nc.sbuf_tensor(tag + "KWR", [128, 16], F32))
        SWV = d1s.enter_context(nc.sbuf_tensor(tag + "SWV", [128, 152], F32))
        FPD = d1s.enter_context(nc.sbuf_tensor(tag + "FPD", [128, 32], F32))
        REL = d1s.enter_context(nc.sbuf_tensor(tag + "REL", [128, 264], F32))
        KEY = d1s.enter_context(nc.sbuf_tensor(tag + "KEY", [128, 4224], F32))
        LOGB = d1s.enter_context(nc.sbuf_tensor(tag + "LOG", [128, 264], F32))
        ENV = d1s.enter_context(nc.sbuf_tensor(tag + "ENV", [128, 264], F32))
        TMW = d1s.enter_context(nc.sbuf_tensor(tag + "TMW", [128, 264], F32))
        VMB = d1s.enter_context(nc.sbuf_tensor(tag + "VMB", [128, 33], F32))
        IWK = d1s.enter_context(nc.sbuf_tensor(tag + "IWK", [128, 4 * 33], F32))
        XHLT = d1s.enter_context(nc.sbuf_tensor(tag + "XHLT", [128, NC * S],
                                                BF16))

        with nc.Block() as blk:
            pps = contextlib.ExitStack()
            PWQ = [pps.enter_context(nc.psum_tensor(f"{tag}PWQ{i}", [128, 152],
                                                    F32)) for i in range(2)]
            d0 = C.val("d")

            def _(s):
                C.wait(s, "d")
                for cc in range(NC):
                    C.inc("d", s.dma_start_transpose(
                        XHLT[:, cc * S:(cc + 1) * S],
                        DS["xhl"][:, cc * 128:(cc + 1) * 128]), 16)
                C.inc("d", s.dma_start(WQB[:], W["ab_wqb"]), 16)
                C.inc("d", s.dma_start(KWR[:], W["ab_kw"]), 16)
            blk.sync(_)
            v_wl = []
            for cc in range(NC):
                def _(s, cc=cc):
                    if cc >= 2:
                        s.wait_ge(C.s("v")[0], v_wl[cc - 2])
                    C.inc("d", s.dma_start(
                        SCR[:, (cc % 2) * 256:(cc % 2) * 256 + 152],
                        W["ab_wwq"][cc * 128:(cc + 1) * 128, :]), 16)
                blk.sync(_)
                d_w = C.val("d")

                def _(v, cc=cc, d_w=d_w):
                    v.wait_ge(C.s("d")[0], d_w)
                    csl = slice(cc * 152, (cc + 1) * 152)
                    ssl = slice((cc % 2) * 256, (cc % 2) * 256 + 152)
                    v.tensor_copy(WWH[:, csl], SCR[:, ssl])
                    G(v)
                    C.inc("v", v.tensor_tensor(WWL[:, csl], SCR[:, ssl],
                                               WWH[:, csl], OP.subtract))
                blk.vector(_)
                v_wl.append(C.val("v"))

            for t in range(NT):
                tb, ab, vb = C.val("t"), C.val("a"), C.val("v")

                def _(te, t=t, vb=vb):
                    if t == 0:
                        C.wait(te, "d")
                        C.wait(te, "v")
                    if t >= 2:
                        te.wait_ge(C.s("v")[0], v_d1[t - 2])
                    pg = PWQ[t % 2]
                    mm = None
                    for cc in range(NC):
                        csl = slice(cc * 152, (cc + 1) * 152)
                        xsl = slice(cc * S + t * 128, cc * S + (t + 1) * 128)
                        mm = te.matmul(pg[:], XHT[:, xsl], WWH[:, csl],
                                       start=(cc == 0), stop=False)
                        mm = te.matmul(pg[:], XHT[:, xsl], WWL[:, csl],
                                       start=False, stop=False)
                        mm = te.matmul(pg[:], XHLT[:, xsl], WWH[:, csl],
                                       start=False, stop=(cc == NC - 1))
                    C.inc("t", mm)
                blk.tensor(_)
                t_p = C.val("t")

                def _(v, t=t, t_p=t_p, ab=ab):
                    v.wait_ge(C.s("t")[0], t_p)
                    C.inc("v", v.tensor_tensor(SWV[:], PWQ[t % 2][:], WQB[:],
                                               OP.add))
                blk.vector(_)
                v_bias = C.val("v")

                def _(a, t=t, v_bias=v_bias):
                    a.wait_ge(C.s("v")[0], v_bias)
                    C.sw(a, "aS", a.activation(SWV[:], SWV[:], AF.Silu))
                    C.inc("a", a.activation(FPD[:, 0:8], SWV[:, 0:8],
                                            AF.Sigmoid))
                    C.inc("a", a.activation(FPD[:, 8:16], SWV[:, 8:16],
                                            AF.Tanh))
                    C.inc("a", a.activation(FPD[:, 16:24], SWV[:, 16:24],
                                            AF.Sigmoid))
                blk.scalar(_)
                a_fpd = C.val("a")

                def _(v, t=t, a_fpd=a_fpd):
                    v.wait_ge(C.s("a")[0], a_fpd)
                    C.sw(v, "vS", v.tensor_scalar(FPD[:, 0:8], FPD[:, 0:8],
                                                  15.0, 1.0, OP.mult, OP.add))
                    C.sw(v, "vS", v.tensor_scalar(FPD[:, 8:16], FPD[:, 8:16],
                                                  16.0, None, OP.mult))
                    C.sw(v, "vS", v.tensor_scalar(FPD[:, 16:24], FPD[:, 16:24],
                                                  9.5, 0.5, OP.mult, OP.add))
                    C.sw(v, "vS", v.tensor_reduce(FPD[:, 24:25], FPD[:, 0:8],
                                                  axis=AXX, op=OP.add))
                    C.sw(v, "vS", v.tensor_scalar(FPD[:, 24:25], FPD[:, 24:25],
                                                  0.125, None, OP.mult))
                    C.sw(v, "vS", v.tensor_reduce(FPD[:, 25:26], FPD[:, 8:16],
                                                  axis=AXX, op=OP.add))
                    C.sw(v, "vS", v.tensor_scalar(FPD[:, 25:26], FPD[:, 25:26],
                                                  0.125, None, OP.mult))
                    C.sw(v, "vS", v.tensor_scalar(IWK[:, 0:33], OFFS[:],
                                                  FPD[:, 24:25], FPD[:, 25:26],
                                                  OP.mult, OP.add))
                    C.sw(v, "vS", v.tensor_scalar(IWK[:, 0:33], IWK[:, 0:33],
                                                  LCOL[:, t:t + 1], None,
                                                  OP.add))
                    v.tensor_single_scalar(VMB[:], IWK[:, 0:33], 0.0, OP.is_ge)
                    G(v)
                    v.tensor_single_scalar(IWK[:, 33:66], IWK[:, 0:33], 2048.0,
                                           OP.is_lt)
                    G(v)
                    C.sw(v, "vS", v.tensor_tensor(VMB[:], VMB[:], IWK[:, 33:66],
                                                  OP.mult))
                    C.sw(v, "vS", v.tensor_scalar(IWK[:, 0:33], IWK[:, 0:33],
                                                  0.0, 2047.0, OP.max, OP.min))
                    C.sw(v, "vS", v.tensor_copy(
                        IDXA[:, t * 33:(t + 1) * 33], IWK[:, 0:33]))
                    C.sw(v, "vS", v.tensor_copy(IWK[:, 33:66],
                                                IDXA[:, t * 33:(t + 1) * 33]))
                    v.tensor_tensor(IWK[:, 66:99], IWK[:, 33:66], IWK[:, 0:33],
                                    OP.is_gt)
                    G(v)
                    v.tensor_tensor(IWK[:, 33:66], IWK[:, 33:66], IWK[:, 66:99],
                                    OP.subtract)
                    G(v)
                    C.sw(v, "vS", v.tensor_copy(IDXA[:, t * 33:(t + 1) * 33],
                                                IWK[:, 33:66]))
                    frq = FPD[:, 0:8].rearrange("p (h o) -> p h o", o=1
                                                ).broadcast_to((128, 8, 33))
                    ab_ = AOFFS[:].rearrange("p (o s) -> p o s", o=1
                                             ).broadcast_to((128, 8, 33))
                    C.inc("v", v.tensor_tensor(
                        REL[:].rearrange("p (h s) -> p h s", h=8), ab_, frq,
                        OP.mult))
                    relb = REL[:].rearrange("p (x o) -> p x o", o=1
                                            ).broadcast_to((128, 264, 16))
                    kwb = KWR[:].rearrange("p (o i) -> p o i", o=1
                                           ).broadcast_to((128, 264, 16))
                    C.inc("v", v.tensor_tensor(
                        KEY[:].rearrange("p (x i) -> p x i", i=16), relb, kwb,
                        OP.mult))
                blk.vector(_)
                v_idx = C.val("v")

                def _(a, t=t, v_idx=v_idx):
                    a.wait_ge(C.s("v")[0], v_idx)
                    C.inc("a", a.activation(KEY[:], KEY[:], AF.Silu))
                blk.scalar(_)
                a_key = C.val("a")

                def _(v, t=t, a_key=a_key):
                    v.wait_ge(C.s("a")[0], a_key)
                    qb_ = SWV[:, 24:152].rearrange("p (h i) -> p h i", i=16
                                                   ).rearrange(
                        "p h i -> p h () i" if False else "p h i -> p h i")
                    qb3 = SWV[:, 24:152].rearrange(
                        "p (h i) -> p h i", i=16).unsqueeze(2).broadcast_to(
                        (128, 8, 33, 16))
                    C.inc("v", v.tensor_tensor(
                        KEY[:].rearrange("p (h s i) -> p h s i", h=8, s=33),
                        KEY[:].rearrange("p (h s i) -> p h s i", h=8, s=33),
                        qb3, OP.mult))
                    C.sw(v, "vS", v.tensor_reduce(
                        LOGB[:], KEY[:].rearrange("p (x i) -> p x i", i=16),
                        axis=AXX, op=OP.add))
                    C.sw(v, "vS", v.tensor_scalar(LOGB[:], LOGB[:], 0.25,
                                                  None, OP.mult))
                    C.sw(v, "vS", v.tensor_reduce(
                        FPD[:, 26:34] if False else IWK[:, 0:8],
                        LOGB[:].rearrange("p (h s) -> p h s", s=33),
                        axis=AXX, op=OP.max))
                    mb = IWK[:, 0:8].rearrange("p (h o) -> p h o", o=1
                                               ).broadcast_to((128, 8, 33))
                    C.inc("v", v.tensor_tensor(
                        LOGB[:].rearrange("p (h s) -> p h s", s=33),
                        LOGB[:].rearrange("p (h s) -> p h s", s=33), mb,
                        OP.subtract))
                    C.sw(v, "vS", v.tensor_scalar(IWK[:, 8:16], FPD[:, 16:24],
                                                  0.1, None, OP.max))
                    C.sw(v, "vS", v.reciprocal(IWK[:, 16:24], IWK[:, 8:16]))
                    rdb = IWK[:, 16:24].rearrange("p (h o) -> p h o", o=1
                                                  ).broadcast_to((128, 8, 33))
                    C.inc("v", v.tensor_tensor(
                        ENV[:].rearrange("p (h s) -> p h s", s=33),
                        REL[:].rearrange("p (h s) -> p h s", s=33), rdb,
                        OP.mult))
                blk.vector(_)
                v_log = C.val("v")

                def _(a, t=t, v_log=v_log):
                    a.wait_ge(C.s("v")[0], v_log)
                    C.sw(a, "aS", a.activation(LOGB[:], LOGB[:], AF.Exp))
                    C.inc("a", a.activation(ENV[:], ENV[:], AF.Exp, scale=-1.0))
                blk.scalar(_)
                a_exp = C.val("a")

                def _(v, t=t, a_exp=a_exp):
                    v.wait_ge(C.s("a")[0], a_exp)
                    vmb3 = VMB[:].rearrange("p (o s) -> p o s", o=1
                                            ).broadcast_to((128, 8, 33))
                    C.inc("v", v.tensor_tensor(
                        TMW[:].rearrange("p (h s) -> p h s", s=33),
                        LOGB[:].rearrange("p (h s) -> p h s", s=33), vmb3,
                        OP.mult))
                    C.sw(v, "vS", v.tensor_reduce(
                        IWK[:, 0:8], TMW[:].rearrange("p (h s) -> p h s", s=33),
                        axis=AXX, op=OP.add))
                    G(v)
                    v.tensor_tensor(TMW[:], TMW[:], ENV[:], OP.mult)
                    G(v)
                    C.sw(v, "vS", v.tensor_reduce(
                        IWK[:, 8:16], TMW[:].rearrange("p (h s) -> p h s", s=33),
                        axis=AXX, op=OP.add))
                    C.sw(v, "vS", v.scalar_tensor_tensor(
                        IWK[:, 16:24], IWK[:, 0:8], 1e-8, IWK[:, 8:16],
                        OP.mult, OP.add))
                    C.sw(v, "vS", v.reciprocal(IWK[:, 24:32], IWK[:, 16:24]))
                    rwb = IWK[:, 24:32].rearrange("p (h o) -> p h o", o=1
                                                  ).broadcast_to((128, 8, 33))
                    C.inc("v", v.tensor_tensor(
                        WALL[:, t * 264:(t + 1) * 264].rearrange(
                            "p (h s) -> p h s", s=33),
                        TMW[:].rearrange("p (h s) -> p h s", s=33), rwb,
                        OP.mult))
                blk.vector(_)
                if t == 0:
                    v_d1 = []
                v_d1.append(C.val("v"))
            pps.close()
        d1s.close()

        d2s = contextlib.ExitStack()
        D2 = d2s.enter_context
        VALS = D2(nc.sbuf_tensor(tag + "VALS", [128, 17 * D], BF16))
        ACC = D2(nc.sbuf_tensor(tag + "ACC", [128, D], F32))
        OUTB = D2(nc.sbuf_tensor(tag + "OUTB", [128, D], BF16))
        OTT = D2(nc.sbuf_tensor(tag + "OTT", [128, D], BF16))
        SE1W = D2(nc.sbuf_tensor(tag + "SE1W", [128, NC * 256], BF16))
        SE2W = D2(nc.sbuf_tensor(tag + "SE2W", [128, 2 * 1024], BF16))
        ABOW = D2(nc.sbuf_tensor(tag + "ABOW", [128, NC * 1024], BF16))
        SE1B = D2(nc.sbuf_tensor(tag + "SE1B", [128, 256], F32))
        SE2B = D2(nc.sbuf_tensor(tag + "SE2B", [128, 1024], F32))
        SEH = D2(nc.sbuf_tensor(tag + "SEH", [128, 256], BF16))
        SEHT = D2(nc.sbuf_tensor(tag + "SEHT", [128, 256], BF16))
        OT2T = D2(nc.sbuf_tensor(tag + "OT2T", [128, D], BF16))
        SEG = SCR

        with nc.Block() as blk:
            pps = contextlib.ExitStack()
            PTP = pps.enter_context(nc.psum_tensor(tag + "PTP", [128, 128],
                                                   BF16))
            PSE = pps.enter_context(nc.psum_tensor(tag + "PSE", [128, 256], F32))
            PS2 = pps.enter_context(nc.psum_tensor(tag + "PS2", [128, 1024],
                                                   F32))
            PAB = pps.enter_context(nc.psum_tensor(tag + "PAB", [128, 1024],
                                                   F32))

            def _(s):
                C.inc("d", s.dma_start(
                    SE1W[:].rearrange("p (c n) -> p c n", c=NC),
                    WB["ab_se1"].rearrange("(c p) n -> p c n", p=128)), 16)
                C.inc("d", s.dma_start(
                    SE2W[:].rearrange("p (c n) -> p c n", c=2),
                    WB["ab_se2"].rearrange("(c p) n -> p c n", p=128)), 16)
                C.inc("d", s.dma_start(
                    ABOW[:].rearrange("p (c n) -> p c n", c=NC),
                    WB["ab_outw"].rearrange("(c p) n -> p c n", p=128)), 16)
                C.inc("d", s.dma_start(SE1B[:], W["ab_se1b"]), 16)
                C.inc("d", s.dma_start(SE2B[:], W["ab_se2b"]), 16)
            blk.sync(_)
            d_w2 = C.val("d")

            for t in range(NT):
                for half in range(2):
                    s_lo = 0 if half == 0 else 17
                    s_hi = 17 if half == 0 else 33

                    def _(g, t=t, s_lo=s_lo, s_hi=s_hi, half=half):
                        if t == 0 and half == 0:
                            C.wait(g, "d")
                        if t >= 1 or half == 1:
                            C.wait(g, "v")
                        for s_ in range(s_lo, s_hi):
                            C.inc("ga", g.indirect_dma_start(
                                VALS[:, (s_ - s_lo) * D:(s_ - s_lo + 1) * D],
                                None, DS["xh"],
                                IndirectOffsetOnAxis(
                                    IDXA[:, t * 33 + s_:t * 33 + s_ + 1], 0)),
                                16)
                    blk.gpsimd(_)
                    g_t = C.val("ga")

                    def _(v, t=t, g_t=g_t, s_lo=s_lo, s_hi=s_hi):
                        v.wait_ge(C.s("ga")[0], g_t)
                        lastv = None
                        for s_ in range(s_lo, s_hi):
                            for h in range(8):
                                wsc = WALL[:, t * 264 + h * 33 + s_:
                                           t * 264 + h * 33 + s_ + 1]
                                vsl = VALS[:, (s_ - s_lo) * D + h * 128:
                                           (s_ - s_lo) * D + (h + 1) * 128]
                                asl = ACC[:, h * 128:(h + 1) * 128]
                                if s_ == 0:
                                    lastv = v.tensor_scalar(asl, vsl, wsc, None,
                                                            OP.mult)
                                else:
                                    lastv = v.scalar_tensor_tensor(
                                        asl, vsl, wsc, asl, OP.mult, OP.add)
                        C.inc("v", lastv)
                    blk.vector(_)

                def _(v, t=t):
                    G(v)
                    C.inc("v", v.tensor_copy(OUTB[:], ACC[:]))
                blk.vector(_)
                v_red = C.val("v")

                def tr_block(srcbuf, dstbuf, nchunks, t):
                    for cc in range(nchunks):
                        tb2 = C.val("t")

                        def _(te, cc=cc, tb2=tb2):
                            C.wait(te, "v")
                            if t == 0 and cc == 0:
                                te.wait_ge(C.s("d")[0], d_w2)
                            C.inc("t", te.transpose(
                                PTP[:], srcbuf[:, cc * 128:(cc + 1) * 128],
                                IDENT[:]))
                        blk.tensor(_)
                        t_tr = C.val("t")

                        def _(v, cc=cc, t_tr=t_tr):
                            v.wait_ge(C.s("t")[0], t_tr)
                            C.inc("v", v.tensor_copy(
                                dstbuf[:, cc * 128:(cc + 1) * 128], PTP[:]))
                        blk.vector(_)

                tr_block(OUTB, OTT, NC, t)

                def _(te, t=t):
                    C.wait(te, "v")
                    mm = None
                    for cc in range(NC):
                        mm = te.matmul(PSE[:], OTT[:, cc * 128:(cc + 1) * 128],
                                       SE1W[:, cc * 256:(cc + 1) * 256],
                                       start=(cc == 0), stop=(cc == NC - 1))
                    C.inc("t", mm)
                blk.tensor(_)

                def _(v):
                    C.wait(v, "t")
                    C.inc("v", v.tensor_tensor(SCR[:, 0:256], PSE[:], SE1B[:],
                                               OP.add))
                blk.vector(_)

                def _(a):
                    C.wait(a, "v")
                    C.inc("a", a.activation(SEH[:], SCR[:, 0:256], AF.Silu))
                blk.scalar(_)
                a_seh = C.val("a")

                def _(v, a_seh=a_seh):
                    v.wait_ge(C.s("a")[0], a_seh)
                    C.inc("v", v.nop())
                blk.vector(_)
                tr_block(SEH, SEHT, 2, 1)

                def _(te):
                    C.wait(te, "v")
                    mm = None
                    for cc in range(2):
                        for nh in range(2):
                            mm = te.matmul(
                                PS2[:, nh * 512:(nh + 1) * 512],
                                SEHT[:, cc * 128:(cc + 1) * 128],
                                SE2W[:, cc * 1024 + nh * 512:
                                     cc * 1024 + (nh + 1) * 512],
                                start=(cc == 0), stop=(cc == 1))
                    C.inc("t", mm)
                blk.tensor(_)

                def _(v):
                    C.wait(v, "t")
                    C.inc("v", v.tensor_tensor(SEG[:, 0:D], PS2[:], SE2B[:], OP.add))
                blk.vector(_)

                def _(a):
                    C.wait(a, "v")
                    C.inc("a", a.activation(SEG[:, 0:D], SEG[:, 0:D], AF.Sigmoid))
                blk.scalar(_)
                a_seg = C.val("a")

                def _(v, a_seg=a_seg):
                    v.wait_ge(C.s("a")[0], a_seg)
                    C.inc("v", v.tensor_tensor(OUTB[:], OUTB[:], SEG[:, 0:D],
                                               OP.mult))
                blk.vector(_)
                tr_block(OUTB, OT2T, NC, 1)

                def _(te):
                    C.wait(te, "v")
                    mm = None
                    for nh in range(2):
                        for cc in range(NC):
                            mm = te.matmul(
                                PAB[:, nh * 512:(nh + 1) * 512],
                                OT2T[:, cc * 128:(cc + 1) * 128],
                                ABOW[:, cc * 1024 + nh * 512:
                                     cc * 1024 + (nh + 1) * 512],
                                start=(cc == 0), stop=(cc == NC - 1))
                    C.inc("t", mm)
                blk.tensor(_)

                def _(a):
                    C.wait(a, "t")
                    C.inc("a", a.activation(OTT[:], PAB[:], AF.Silu))
                blk.scalar(_)
                a_res = C.val("a")

                def _(v, t=t, a_res=a_res):
                    v.wait_ge(C.s("a")[0], a_res)
                    C.inc("v", v.tensor_tensor(xres(b, t), xres(b, t), OTT[:],
                                               OP.add))
                blk.vector(_)
            pps.close()
        d2s.close()
        bes.close()

    for b in _B_LIST:
        with nc.sbuf_tensor(f"XT{b}", [128, NT * D], BF16) as XT_A:
            emit_ln(b, DS["xh"], tok_buf=XT_A)
            if _BUILD_UPTO >= 1:
                emit_attn_a(b, XT_A)
        if _BUILD_UPTO >= 2:
            emit_ln(b, DS["xh"], lo_dram=DS["xhl"])
            emit_ffn(b, 0)
        if _BUILD_UPTO >= 3:
            emit_ln(b, DS["xh"], lo_dram=DS["xhl"])
            emit_attn_b(b)
        if _BUILD_UPTO >= 4:
            emit_ln(b, DS["xh"], lo_dram=DS["xhl"])
            emit_ffn(b, 1)

    with nc.Block() as blk:
        v0 = C.val("v")
        fin0 = C.val("fin")

        def _(v):
            for i, (b, t) in enumerate((b, t) for b in range(B_CORE)
                                       for t in range(NT)):
                if i >= 1:
                    v.wait_ge(C.s("fin")[0], fin0 + 16 * i)
                C.inc("v", v.tensor_copy(SCR[:], xres(b, t)))
        blk.vector(_)

        def _(s):
            for i, (b, t) in enumerate((b, t) for b in range(B_CORE)
                                       for t in range(NT)):
                s.wait_ge(C.s("v")[0], v0 + i + 1)
                C.inc("fin", s.dma_start(out[b, t * 128:(t + 1) * 128, :],
                                         SCR[:]), 16)
            C.wait(s, "d")
            for i, (nm, r, c_) in enumerate(_DEBUG_DUMPS):
                if nm == "XHT":
                    C.inc("fin", s.dma_start(dbg[i], XHT[0:r, 0:c_]), 16)
                    C.wait(s, "fin")
                    continue
                srcap = DS[nm] if nm in DS else None
                for rr in range(0, r, 128):
                    for c0 in range(0, c_, 2048):
                        cw = min(2048, c_ - c0)
                        C.inc("fin", s.dma_start(
                            SCRB[:, 0:cw], srcap[rr:rr + 128, c0:c0 + cw]), 16)
                        C.wait(s, "fin")
                        C.inc("fin", s.dma_start(
                            dbg[i][rr:rr + 128, c0:c0 + cw], SCRB[:, 0:cw]), 16)
                        C.wait(s, "fin")
            C.wait(s, "fin")
        blk.sync(_)

    es.close()
    return nc


def _make_attn_a(nc, C, E, env):
    def emit(b):
        pass
    return emit



def kernel(**inputs):
    x = np.ascontiguousarray(inputs["L0_attn_a_hidden_states"], dtype=np.float32)
    wts = _host_weights({k: np.asarray(v) for k, v in inputs.items()})
    consts = _host_constants()

    nc = build_program()
    in_maps = []
    for core in range(8):
        m = {"x": x[core * 2:(core + 1) * 2]}
        m.update(wts)
        m.update(consts)
        in_maps.append(m)
    res = run_bass_kernel_spmd(nc, in_maps, core_ids=list(range(8)))
    full = np.concatenate([r["out"] for r in res.results], axis=0)
    full = np.ascontiguousarray(full, dtype=np.float32)
    if _DEBUG_DUMPS:
        kernel.debug = [
            {i: r[f"dbg{i}"] for i in range(len(_DEBUG_DUMPS))}
            for r in res.results]
    return full


# version 05
# speedup vs baseline: 1.2421x; 1.2421x over previous
import math
import numpy as np
import contextlib
import concourse.bass as bass
import concourse.mybir as mybir
from concourse.bass import IndirectOffsetOnAxis
from concourse.alu_op_type import AluOpType as OP
from concourse.bass_utils import run_bass_kernel_spmd

F32 = mybir.dt.float32
BF16 = mybir.dt.bfloat16
I32 = mybir.dt.int32
AF = mybir.ActivationFunctionType
AXX = mybir.AxisListType.X

B_CORE, S, D, NT, NC = 2, 2048, 1024, 16, 8

_BUILD_UPTO = 4
_DEBUG_DUMPS = []
_USE_TOKBUF = True
_B_LIST = [0, 1]
_FFN_UPTO = 9



def _bcast_kernel():
    g = np.zeros(S, np.float64); g[0] = 1.0
    res = np.zeros(S, np.float64); cur = g.copy(); st = 1
    while st < S:
        cur = cur + 0.5 * (np.roll(cur, st) + np.roll(cur, -st))
        res = res + cur
        st *= 2
    g = res / (math.log2(S) + 1.0)
    idx = (np.arange(S)[None, :] - np.arange(S)[:, None]) % S
    return g[idx].astype(np.float32)


def _host_constants():
    c = {}
    half = 16
    inv = (10000.0 ** (-np.arange(half, dtype=np.float32) / half))
    ang = np.arange(S, dtype=np.float32)[:, None] * inv[None, :]
    c["c_cos"] = np.cos(ang).astype(np.float32)
    c["c_sin"] = np.sin(ang).astype(np.float32)
    c["c_circ"] = _bcast_kernel()
    tri = np.linspace(0.5, 1.5, 512).astype(np.float32)
    c["c_tri"] = tri.reshape(512, 1)
    den = np.zeros(S, np.float32)
    for n in range(7):
        den[n * 256:n * 256 + 512] += tri
    c["c_denr"] = (1.0 / (den + 1e-6)).reshape(S, 1).astype(np.float32)
    c["c_lcol"] = np.arange(S, dtype=np.float32).reshape(S, 1)
    offs = np.arange(-16, 17, dtype=np.float32)
    c["c_offs"] = np.ascontiguousarray(np.broadcast_to(offs, (128, 33)))
    c["c_aoffs"] = np.ascontiguousarray(np.broadcast_to(np.abs(offs), (128, 33)))
    c["c_ident"] = np.eye(128, dtype=np.float32)
    c["c_ones"] = np.ones((128, 128), np.float32)
    return c


def _host_weights(inp):
    w = {}
    gm = inp["L0_attn_a_global_memory"]
    w["aa_gmT"] = np.ascontiguousarray(
        np.transpose(gm, (0, 2, 1, 3)).reshape(64, 1024).T)
    cw = inp["L0_attn_a_conv_w"]
    w["aa_wcat"] = np.ascontiguousarray(cw.transpose(2, 1, 0).reshape(4096, 1024))
    w["aa_convb"] = np.asarray(inp["L0_attn_a_conv_b"]).reshape(1024, 1)
    mw = inp["L0_attn_a_mix_w"]
    w["aa_m1"] = np.ascontiguousarray(mw[:1024])
    w["aa_m2"] = np.ascontiguousarray(mw[1024:])
    w["aa_mixb"] = np.asarray(inp["L0_attn_a_mix_b"]).reshape(1024, 1)
    w["aa_outw"] = inp["L0_attn_a_out_w"]
    w["aa_outb"] = np.asarray(inp["L0_attn_a_out_b"]).reshape(1024, 1)

    wq = inp["L0_ffn_Wq"].transpose(1, 0, 2).reshape(1024, 1024)
    wk = inp["L0_ffn_Wk"].transpose(1, 0, 2).reshape(1024, 1024)
    w["f_wqk"] = np.ascontiguousarray(np.concatenate([wq, wk], 1))
    sv = inp["L0_ffn_Wsel_v"].transpose(2, 0, 1).reshape(1024, 128)
    so = inp["L0_ffn_Wsel_o"].transpose(2, 0, 1).reshape(1024, 128)
    w["f_wsel"] = np.ascontiguousarray(np.concatenate([sv, so], 1))
    w["f_wv"] = np.ascontiguousarray(
        inp["L0_ffn_Wv"].transpose(0, 2, 1, 3).reshape(16, 1024, 512))
    w["f_wo"] = np.ascontiguousarray(np.asarray(inp["L0_ffn_Wo"]).reshape(16, 4, 128, 1024))
    w["f_ln1g"] = np.ascontiguousarray(
        np.broadcast_to(inp["L0_ffn_ln1_g"], (128, 1024)))
    w["f_ln1b"] = np.ascontiguousarray(
        np.broadcast_to(inp["L0_ffn_ln1_b"], (128, 1024)))
    w["f_ln2g"] = np.ascontiguousarray(
        np.broadcast_to(inp["L0_ffn_ln2_g"], (128, 1024)))
    w["f_ln2b"] = np.ascontiguousarray(
        np.broadcast_to(inp["L0_ffn_ln2_b"], (128, 1024)))
    w["f_w1"] = inp["L0_ffn_W1"]
    w["f_b1"] = np.asarray(inp["L0_ffn_b1"]).reshape(4096, 1)
    w["f_w2"] = inp["L0_ffn_W2"]
    w["f_b2"] = np.asarray(inp["L0_ffn_b2"]).reshape(1024, 1)

    w["ab_wwq"] = np.ascontiguousarray(
        np.concatenate([inp["L0_attn_b_wave_w"], inp["L0_attn_b_query_w"]], 1))
    w["ab_wqb"] = np.ascontiguousarray(np.broadcast_to(
        np.concatenate([np.asarray(inp["L0_attn_b_wave_b"]),
                        np.asarray(inp["L0_attn_b_query_b"])]), (128, 152)))
    w["ab_kw"] = np.ascontiguousarray(
        np.broadcast_to(inp["L0_attn_b_key_w"], (128, 16)))
    w["ab_outw"] = inp["L0_attn_b_out_w"]
    w["ab_se1"] = inp["L0_attn_b_se1_w"]
    w["ab_se1b"] = np.ascontiguousarray(
        np.broadcast_to(inp["L0_attn_b_se1_b"], (128, 256)))
    w["ab_se2"] = inp["L0_attn_b_se2_w"]
    w["ab_se2b"] = np.ascontiguousarray(
        np.broadcast_to(inp["L0_attn_b_se2_b"], (128, 1024)))
    return {k: np.ascontiguousarray(v, dtype=np.float32) for k, v in w.items()}


_W_SHAPES = {
    "aa_gmT": [1024, 64], "aa_wcat": [4096, 1024], "aa_convb": [1024, 1],
    "aa_m1": [1024, 1024], "aa_m2": [1024, 1024], "aa_mixb": [1024, 1],
    "aa_outw": [1024, 1024], "aa_outb": [1024, 1],
    "f_wqk": [1024, 2048], "f_wsel": [1024, 256],
    "f_wv": [16, 1024, 512], "f_wo": [16, 4, 128, 1024],
    "f_ln1g": [128, 1024], "f_ln1b": [128, 1024],
    "f_ln2g": [128, 1024], "f_ln2b": [128, 1024],
    "f_w1": [1024, 4096], "f_b1": [4096, 1], "f_w2": [4096, 1024],
    "f_b2": [1024, 1],
    "ab_wwq": [1024, 152], "ab_wqb": [128, 152], "ab_kw": [128, 16],
    "ab_outw": [1024, 1024], "ab_se1": [1024, 256], "ab_se1b": [128, 256],
    "ab_se2": [256, 1024], "ab_se2b": [128, 1024],
    "c_cos": [2048, 16], "c_sin": [2048, 16], "c_circ": [2048, 2048],
    "c_tri": [512, 1], "c_denr": [2048, 1], "c_lcol": [2048, 1],
    "c_offs": [128, 33], "c_aoffs": [128, 33],
    "c_ident": [128, 128], "c_ones": [128, 128],
}

_WBF = ["aa_wcat", "aa_m1", "aa_m2", "aa_outw", "f_wqk", "f_wsel", "f_wv",
        "f_wo", "f_w1", "f_w2", "ab_wwq", "ab_outw", "ab_se1", "ab_se2",
        "c_circ"]



def G(e, n=6):
    for _ in range(n):
        e.nop()


class Ctx:
    def __init__(self, nc, es):
        self.nc, self.es, self._s = nc, es, {}

    def s(self, name):
        if name not in self._s:
            self._s[name] = [self.es.enter_context(self.nc.semaphore(name)), 0]
        return self._s[name]

    def inc(self, name, inst, k=1):
        s = self.s(name)
        inst.then_inc(s[0], k)
        s[1] += k
        return s[1]

    def val(self, name):
        return self.s(name)[1]

    def wait(self, eng, name, v=None):
        s = self.s(name)
        eng.wait_ge(s[0], s[1] if v is None else v)

    def sw(self, eng, name, inst):
        n = self.inc(name, inst)
        eng.wait_ge(self.s(name)[0], n)


def build_program():
    nc = bass.Bass("TRN2", target_bir_lowering=False)
    es = contextlib.ExitStack()
    E = es.enter_context
    C = Ctx(nc, es)

    x_in = nc.dram_tensor("x", [B_CORE, S, D], F32, kind="ExternalInput").ap()
    out = nc.dram_tensor("out", [B_CORE, S, D], F32, kind="ExternalOutput").ap()
    W = {k: nc.dram_tensor(k, sh, F32, kind="ExternalInput").ap()
         for k, sh in _W_SHAPES.items()}
    WB = {k: nc.dram_tensor("bf_" + k, _W_SHAPES[k], BF16, kind="Internal").ap()
          for k in _WBF}

    DS = {}
    for name, sh in [
        ("xh", [S, D]), ("qd", [S, D]), ("kd", [S, D]),
        ("od", [D, S]), ("opd", [S, 16 * 512]),
        ("loc", [S, D]), ("aoT", [D, S]), ("otw", [D, 512]),
        ("l1", [S, D]), ("m2T", [D, S]), ("xhl", [S, D]),
        ("opT", [16 * 512, S]),
    ]:
        DS[name] = nc.dram_tensor("ds_" + name, sh, BF16, kind="Internal").ap()
    sums_d = nc.dram_tensor("ds_sums", [16, S], F32, kind="Internal").ap()
    dbg = {}
    for i, (nm, r, c_) in enumerate(_DEBUG_DUMPS):
        dbg[i] = nc.dram_tensor(f"dbg{i}", [r, c_], BF16, kind="ExternalOutput").ap()

    XRES = E(nc.sbuf_tensor("XRES", [128, B_CORE * NT * D], BF16))
    IDENT = E(nc.sbuf_tensor("IDENT", [128, 128], BF16))
    ONES = E(nc.sbuf_tensor("ONESB", [128, 128], BF16))
    COS = E(nc.sbuf_tensor("COS", [128, NT * 16], F32))
    SIN = E(nc.sbuf_tensor("SIN", [128, NT * 16], F32))
    TRI = E(nc.sbuf_tensor("TRI", [128, 4], F32))
    DENR = E(nc.sbuf_tensor("DENR", [128, NT], F32))
    LCOL = E(nc.sbuf_tensor("LCOL", [128, NT], F32))
    OFFS = E(nc.sbuf_tensor("OFFS", [128, 33], F32))
    AOFFS = E(nc.sbuf_tensor("AOFFS", [128, 33], F32))
    GMT = E(nc.sbuf_tensor("GMT", [128, NC * 64], BF16))
    SCR = E(nc.sbuf_tensor("SCR", [128, D], F32))
    SCRB = E(nc.sbuf_tensor("SCRB", [128, 2 * D], BF16))
    STAT = E(nc.sbuf_tensor("STAT", [128, 32], F32))
    XHT = E(nc.sbuf_tensor("XHT", [128, NC * S], BF16))
    IDENTF = E(nc.sbuf_tensor("IDENTF", [128, 128], F32))
    JUNK = E(nc.sbuf_tensor("JUNK", [128, D], BF16))

    def xres(b, t):
        return XRES[:, (b * NT + t) * D:(b * NT + t + 1) * D]

    with nc.Block() as blk:
        def _(g):
            for b in range(B_CORE):
                for t in range(NT):
                    C.inc("g", g.dma_start(xres(b, t),
                                           x_in[b, t * 128:(t + 1) * 128, :]), 16)
            for k in _WBF:
                src, dst = W[k], WB[k]
                n = int(np.prod(_W_SHAPES[k]))
                dims = " ".join(f"a{i}" for i in range(len(_W_SHAPES[k])))
                fs = src.rearrange(f"{dims} -> ({dims})")
                fd = dst.rearrange(f"{dims} -> ({dims})")
                step = 1 << 21
                for off in range(0, n, step):
                    m = min(step, n - off)
                    C.inc("g", g.dma_start(fd[off:off + m], fs[off:off + m]), 16)
        blk.gpsimd(_)

        def _(s):
            C.inc("d", s.dma_start(SCR[:, 0:128], W["c_ident"]), 16)
            C.inc("d", s.dma_start(IDENTF[:], W["c_ident"]), 16)
            C.inc("d", s.dma_start(SCR[:, 128:256], W["c_ones"]), 16)
            for t in range(NT):
                C.inc("d", s.dma_start(COS[:, t * 16:(t + 1) * 16],
                                       W["c_cos"][t * 128:(t + 1) * 128, :]), 16)
                C.inc("d", s.dma_start(SIN[:, t * 16:(t + 1) * 16],
                                       W["c_sin"][t * 128:(t + 1) * 128, :]), 16)
                C.inc("d", s.dma_start(DENR[:, t:t + 1],
                                       W["c_denr"][t * 128:(t + 1) * 128, :]), 16)
                C.inc("d", s.dma_start(LCOL[:, t:t + 1],
                                       W["c_lcol"][t * 128:(t + 1) * 128, :]), 16)
            for q in range(4):
                C.inc("d", s.dma_start(TRI[:, q:q + 1],
                                       W["c_tri"][q * 128:(q + 1) * 128, :]), 16)
            C.inc("d", s.dma_start(OFFS[:], W["c_offs"]), 16)
            C.inc("d", s.dma_start(AOFFS[:], W["c_aoffs"]), 16)
            for cc in range(NC):
                C.inc("d", s.dma_start(SCR[:, 256 + cc * 64:256 + (cc + 1) * 64],
                                       W["aa_gmT"][cc * 128:(cc + 1) * 128, :]), 16)
        blk.sync(_)

        def _(v):
            C.wait(v, "d")
            C.wait(v, "g")
            v.tensor_copy(IDENT[:], SCR[:, 0:128])
            v.tensor_copy(ONES[:], SCR[:, 128:256])
            v.tensor_copy(GMT[:], SCR[:, 256:256 + NC * 64])
        blk.vector(_)

    def emit_ln(b, dram, make_xht=True, tok_buf=None, lo_dram=None):
        nv = 3 if lo_dram is not None else 2
        with nc.Block() as blk:
            a0, v0, d0 = C.val("a"), C.val("v"), C.val("d")

            def _(v):
                C.wait(v, "d", d0)
                for t in range(NT):
                    src = xres(b, t)
                    buf = (tok_buf[:, t * D:(t + 1) * D] if tok_buf is not None
                           else SCRB[:, (t % 2) * D:(t % 2) * D + D])
                    lobuf = SCRB[:, ((t + 1) % 2) * D:((t + 1) % 2) * D + D]
                    C.sw(v, "vS", v.tensor_reduce(STAT[:, 0:1], src, axis=AXX,
                                                  op=OP.add))
                    C.sw(v, "vS", v.tensor_scalar(STAT[:, 5:6], STAT[:, 0:1],
                                                  1.0 / D, None, OP.mult))
                    v.wait_ge(C.s("a")[0], a0 + 2 * t + 1)
                    C.sw(v, "vS", v.tensor_scalar(STAT[:, 3:4], STAT[:, 1:2],
                                                  1.0 / D, None, OP.mult))
                    C.sw(v, "vS", v.tensor_tensor(STAT[:, 2:3], STAT[:, 5:6],
                                                  STAT[:, 5:6], OP.mult))
                    C.sw(v, "vS", v.tensor_tensor(STAT[:, 3:4], STAT[:, 3:4],
                                                  STAT[:, 2:3], OP.subtract))
                    C.inc("v", v.tensor_scalar(STAT[:, 3:4], STAT[:, 3:4],
                                               1e-5, None, OP.add))
                    v.wait_ge(C.s("a")[0], a0 + 2 * t + 2)
                    if t >= 1 and tok_buf is None:
                        v.wait_ge(C.s("d")[0],
                                  d0 + 16 * (2 if lo_dram is not None else 1) * t)
                    v.tensor_scalar(SCR[:], src, STAT[:, 5:6],
                                    STAT[:, 4:5], OP.subtract, OP.mult)
                    C.inc("v", v.tensor_copy(buf, SCR[:]))
                    if lo_dram is not None:
                        C.inc("v", v.tensor_tensor(lobuf, SCR[:], buf,
                                                   OP.subtract))
            blk.vector(_)

            def _(a):
                for t in range(NT):
                    C.inc("a", a.activation(JUNK[:], xres(b, t), AF.Square,
                                            accum_out=STAT[:, 1:2]))
                    a.wait_ge(C.s("v")[0], v0 + nv * t + 1)
                    C.sw(a, "aS", a.activation(STAT[:, 4:5], STAT[:, 3:4],
                                                AF.Ln))
                    C.inc("a", a.activation(STAT[:, 4:5], STAT[:, 4:5], AF.Exp,
                                            scale=-0.5))
            blk.scalar(_)

            def _(s):
                for t in range(NT):
                    buf = (tok_buf[:, t * D:(t + 1) * D] if tok_buf is not None
                           else SCRB[:, (t % 2) * D:(t % 2) * D + D])
                    s.wait_ge(C.s("v")[0], v0 + nv * t + 2)
                    C.inc("d", s.dma_start(dram[t * 128:(t + 1) * 128, :], buf),
                          16)
                    if lo_dram is not None:
                        s.wait_ge(C.s("v")[0], v0 + nv * t + 3)
                        C.inc("d", s.dma_start(
                            lo_dram[t * 128:(t + 1) * 128, :],
                            SCRB[:, ((t + 1) % 2) * D:((t + 1) % 2) * D + D]),
                            16)
                if make_xht:
                    s.wait_ge(C.s("d")[0], C.val("d"))
                    for cc in range(NC):
                        C.inc("d", s.dma_start_transpose(
                            XHT[:, cc * S:(cc + 1) * S],
                            dram[:, cc * 128:(cc + 1) * 128]), 16)
            blk.sync(_)

    def emit_resadd_featT(b, srcT):
        with nc.Block() as blk:
            d0, v0 = C.val("d"), C.val("v")

            def _(s):
                s.wait_ge(C.s("d")[0], d0)
                for t in range(NT):
                    buf = SCRB[:, (t % 2) * D:(t % 2) * D + D]
                    if t >= 2:
                        s.wait_ge(C.s("v")[0], v0 + t - 1)
                    C.inc("d", s.dma_start_transpose(
                        buf, srcT[:, t * 128:(t + 1) * 128]), 16)
            blk.sync(_)

            def _(v):
                for t in range(NT):
                    buf = SCRB[:, (t % 2) * D:(t % 2) * D + D]
                    v.wait_ge(C.s("d")[0], d0 + 16 * (t + 1))
                    C.inc("v", v.tensor_tensor(xres(b, t), xres(b, t), buf,
                                               OP.add))
            blk.vector(_)

    COLS = E(nc.sbuf_tensor("COLS", [128, 24], F32))
    GTT = E(nc.sbuf_tensor("GTT", [128, NC * 128], BF16))
    GTOK = E(nc.sbuf_tensor("GTOK", [128, D], BF16))

    def emit_attn_a(b, XHTOK):
        pes = contextlib.ExitStack()
        PE_ = pes.enter_context
        b1s = contextlib.ExitStack()
        B1_ = b1s.enter_context
        LOCAL = B1_(nc.sbuf_tensor(f"aa{b}_LOCAL", [128, NT * D], BF16))
        EBUF = B1_(nc.sbuf_tensor(f"aa{b}_EBUF", [128, 8 * 512], BF16))
        OTWSB = B1_(nc.sbuf_tensor(f"aa{b}_OTW", [128, 2 * 512], BF16))
        OTOK = B1_(nc.sbuf_tensor(f"aa{b}_OTOK", [128, 4 * D], BF16))
        SUMW = B1_(nc.sbuf_tensor(f"aa{b}_SUMW", [128, 64], F32))
        RS = B1_(nc.sbuf_tensor(f"aa{b}_RS", [128, 64], F32))
        ONF = B1_(nc.sbuf_tensor(f"aa{b}_ONF", [128, D], F32))
        NW, NHP = 7, 8

        with nc.Block() as blk:
            pps = contextlib.ExitStack()
            SA = [pps.enter_context(nc.psum_tensor(f"aa{b}_SA{i}", [128, 512], F32))
                  for i in range(2)]
            SB = [pps.enter_context(nc.psum_tensor(f"aa{b}_SB{i}", [128, 512], F32))
                  for i in range(2)]
            OW = [pps.enter_context(nc.psum_tensor(f"aa{b}_OW{i}", [128, 512], F32))
                  for i in range(2)]
            t0, a0, v0, d0, g0 = (C.val("t"), C.val("a"), C.val("v"),
                                  C.val("d"), C.val("g"))

            def t_smm(n, hp, qt, j):
                return t0 + (n * NHP + hp) * 9 + 2 * qt + j

            def t_av(n, hp):
                return t0 + (n * NHP + hp) * 9 + 9

            def a_exp(n, hp, qt, j):
                return a0 + (n * NHP + hp) * 8 + 2 * qt + j

            def a_hp(n, hp):
                return a0 + (n * NHP + hp + 1) * 8

            def v_evac(n, hp):
                return v0 + n * 13 + hp + 1

            def v_recip(n):
                return v0 + n * 13 + 9

            def v_fold(n, qt):
                return v0 + n * 13 + 9 + qt + 1

            def d_st(n, hp):
                return d0 + n * 192 + 16 * (hp + 1)

            def d_tr(n, qt):
                return d0 + n * 192 + 128 + 16 * (qt + 1)

            def _(g):
                C.inc("g", g.memset(LOCAL[:], 0.0))
            blk.gpsimd(_)

            def _(te):
                C.wait(te, "d", d0)
                C.wait(te, "g", g0 + 1)
                for n in range(NW):
                    for hp in range(NHP):
                        xh_c = XHT[:, hp * S:(hp + 1) * S]
                        for qt in range(4):
                            pa, pb = SA[qt % 2], SB[qt % 2]
                            idx = (n * NHP + hp) * 4 + qt
                            if idx >= 2:
                                pn, rem = divmod(idx - 2, NHP * 4)
                                ph, pq = divmod(rem, 4)
                                te.wait_ge(C.s("a")[0], a_exp(pn, ph, pq, 2))
                            qlo = 256 * n + 128 * qt
                            C.inc("t", te.matmul(
                                pa[:], xh_c[0:64, qlo:qlo + 128],
                                xh_c[0:64, 256 * n:256 * n + 512],
                                start=True, stop=True, tile_position=(0, 0)))
                            C.inc("t", te.matmul(
                                pb[:], xh_c[64:128, qlo:qlo + 128],
                                xh_c[64:128, 256 * n:256 * n + 512],
                                start=True, stop=True, tile_position=(64, 0)))
                        te.wait_ge(C.s("a")[0], a_hp(n, hp))
                        j = n * NHP + hp
                        if j >= 2:
                            pn, ph = divmod(j - 2, NHP)
                            te.wait_ge(C.s("v")[0], v_evac(pn, ph))
                        ow = OW[hp % 2]
                        mm = None
                        for kc in range(4):
                            tt = 2 * n + kc
                            te.matmul(
                                ow[0:64, :],
                                XHTOK[:, tt * D + (2 * hp) * 64:
                                      tt * D + (2 * hp) * 64 + 64],
                                EBUF[:, kc * 512:(kc + 1) * 512],
                                start=(kc == 0), stop=(kc == 3),
                                tile_position=(0, 0))
                            mm = te.matmul(
                                ow[64:128, :],
                                XHTOK[:, tt * D + (2 * hp + 1) * 64:
                                      tt * D + (2 * hp + 1) * 64 + 64],
                                EBUF[:, (4 + kc) * 512:(4 + kc + 1) * 512],
                                start=(kc == 0), stop=(kc == 3),
                                tile_position=(0, 64))
                        C.inc("t", mm)
            blk.tensor(_)

            def _(a):
                for n in range(NW):
                    for hp in range(NHP):
                        for qt in range(4):
                            pa, pb = SA[qt % 2], SB[qt % 2]
                            h0, h1 = 2 * hp, 2 * hp + 1
                            a.wait_ge(C.s("t")[0], t_smm(n, hp, qt, 2))
                            if n > 0 and hp == 0 and qt == 0:
                                a.wait_ge(C.s("v")[0], v_recip(n - 1))
                            C.inc("a", a.activation(
                                EBUF[:, qt * 512:(qt + 1) * 512], pa[:],
                                AF.Exp, scale=0.125,
                                accum_out=SUMW[:, qt * 16 + h0:qt * 16 + h0 + 1]))
                            C.inc("a", a.activation(
                                EBUF[:, (4 + qt) * 512:(4 + qt + 1) * 512],
                                pb[:], AF.Exp, scale=0.125,
                                accum_out=SUMW[:, qt * 16 + h1:qt * 16 + h1 + 1]))
            blk.scalar(_)

            def _(v):
                for n in range(NW):
                    for hp in range(NHP):
                        v.wait_ge(C.s("t")[0], t_av(n, hp))
                        j = n * NHP + hp
                        if j >= 2:
                            pn, ph = divmod(j - 2, NHP)
                            v.wait_ge(C.s("d")[0], d_st(pn, ph))
                        C.inc("v", v.tensor_copy(
                            OTWSB[:, (hp % 2) * 512:(hp % 2) * 512 + 512],
                            OW[hp % 2][:]))
                    v.wait_ge(C.s("a")[0], a0 + (n + 1) * NHP * 8)
                    C.sw(v, "vS", v.reciprocal(RS[:], SUMW[:]))
                    C.inc("v", v.tensor_copy(STAT[:, 8:9], RS[:, 0:1]))
                    for qt in range(4):
                        v.wait_ge(C.s("d")[0], d_tr(n, qt))
                        rsb = RS[:, qt * 16:(qt + 1) * 16].rearrange(
                            "p (h o) -> p h o", o=1).broadcast_to((128, 16, 64))
                        v.tensor_tensor(
                            ONF[:].rearrange("p (h e) -> p h e", h=16),
                            OTOK[:, qt * D:(qt + 1) * D].rearrange(
                                "p (h e) -> p h e", h=16),
                            rsb, OP.mult)
                        tile = 2 * n + qt
                        C.inc("v", v.scalar_tensor_tensor(
                            LOCAL[:, tile * D:(tile + 1) * D], ONF[:],
                            TRI[:, qt:qt + 1],
                            LOCAL[:, tile * D:(tile + 1) * D],
                            OP.mult, OP.add))
            blk.vector(_)

            def _(s):
                for n in range(NW):
                    for hp in range(NHP):
                        s.wait_ge(C.s("v")[0], v_evac(n, hp))
                        C.inc("d", s.dma_start(
                            DS["otw"][hp * 128:(hp + 1) * 128, :],
                            OTWSB[:, (hp % 2) * 512:(hp % 2) * 512 + 512]), 16)
                    for qt in range(4):
                        s.wait_ge(C.s("d")[0], d_st(n, 7))
                        if n > 0:
                            s.wait_ge(C.s("v")[0], v_fold(n - 1, qt))
                        C.inc("d", s.dma_start_transpose(
                            OTOK[:, qt * D:(qt + 1) * D],
                            DS["otw"][:, qt * 128:(qt + 1) * 128]), 16)
            blk.sync(_)
            pps.close()

        with nc.Block() as blk:
            v0b = C.val("v")

            def _(v):
                for t in range(NT):
                    C.inc("v", v.tensor_scalar(
                        LOCAL[:, t * D:(t + 1) * D],
                        LOCAL[:, t * D:(t + 1) * D],
                        DENR[:, t:t + 1], None, OP.mult))
            blk.vector(_)

            def _(s):
                for t in range(NT):
                    s.wait_ge(C.s("v")[0], v0b + t + 1)
                    C.inc("d", s.dma_start(
                        DS["loc"][t * 128:(t + 1) * 128, :],
                        LOCAL[:, t * D:(t + 1) * D]), 16)
            blk.sync(_)

        with nc.Block() as blk:
            pps = contextlib.ExitStack()
            CPP = pps.enter_context(nc.psum_tensor(f"aa{b}_CP", [128, 512], F32))
            CP = [CPP[:, i * 64:(i + 1) * 64] for i in range(8)]
            PT = pps.enter_context(nc.psum_tensor(f"aa{b}_PT", [128, 128], BF16))
            WCB = pps.enter_context(nc.sbuf_tensor(f"aa{b}_WCB", [128, 2 * D], BF16))
            t0, v0, d0 = C.val("t"), C.val("v"), C.val("d")

            def _(s):
                C.wait(s, "g")
                for j in range(8):
                    C.inc("d", s.dma_start(
                        COLS[:, j:j + 1],
                        W["aa_convb"][j * 128:(j + 1) * 128, :]), 16)
                    C.inc("d", s.dma_start(
                        COLS[:, 8 + j:9 + j],
                        W["aa_mixb"][j * 128:(j + 1) * 128, :]), 16)
                    C.inc("d", s.dma_start(
                        COLS[:, 16 + j:17 + j],
                        W["aa_outb"][j * 128:(j + 1) * 128, :]), 16)
                for rc in range(32):
                    if rc >= 2:
                        s.wait_ge(C.s("t")[0], t0 + rc - 1)
                    C.inc("d", s.dma_start(
                        WCB[:, (rc % 2) * D:(rc % 2) * D + D],
                        WB["aa_wcat"][rc * 128:(rc + 1) * 128, :]), 16)
            blk.sync(_)

            def _(te):
                for rc in range(32):
                    te.wait_ge(C.s("d")[0], d0 + 24 * 16 + 16 * (rc + 1))
                    k, cc = divmod(rc, 8)
                    rhs = XHT[:, cc * S + k:cc * S + k + 256].rearrange(
                        "p (t four) -> p t four", four=4)[:, :, 0:1]
                    mm = None
                    for dt in range(8):
                        mm = te.matmul(CP[dt],
                                       WCB[:, (rc % 2) * D + dt * 128:
                                           (rc % 2) * D + (dt + 1) * 128],
                                       rhs, start=(rc == 0), stop=(rc == 31))
                    C.inc("t", mm)
                for cc2 in range(8):
                    te.wait_ge(C.s("v")[0], v0 + 2 * (cc2 + 1))
                    if cc2 >= 1:
                        te.wait_ge(C.s("v")[0], v0 + 16 + cc2)
                    C.inc("t", te.transpose(PT[:],
                                            GTT[:, cc2 * 128:(cc2 + 1) * 128],
                                            IDENT[:]))
            blk.tensor(_)

            def _(v):
                v.wait_ge(C.s("t")[0], t0 + 32)
                C.wait(v, "d", d0 + 24 * 16)
                for dt in range(8):
                    v.tensor_scalar(GTT[:, dt * 128:dt * 128 + 64], CP[dt],
                                    COLS[:, dt:dt + 1], None, OP.add)
                    C.inc("v", v.tensor_copy(
                        GTT[:, dt * 128 + 64:(dt + 1) * 128],
                        GMT[:, dt * 64:(dt + 1) * 64]), 2)
                for cc2 in range(8):
                    v.wait_ge(C.s("t")[0], t0 + 32 + cc2 + 1)
                    C.inc("v", v.tensor_copy(GTOK[:, cc2 * 128:(cc2 + 1) * 128],
                                             PT[:]))
            blk.vector(_)
            pps.close()

        b1s.close()
        LTB = PE_(nc.sbuf_tensor(f"aa{b}_LTB", [128, NC * 512], BF16))
        GTBUF = PE_(nc.sbuf_tensor(f"aa{b}_GTB", [128, NC * 512], BF16))
        GBK = PE_(nc.sbuf_tensor(f"aa{b}_GBK", [128, 2 * 512], BF16))
        MXT = PE_(nc.sbuf_tensor(f"aa{b}_MXT", [128, NC * 512], BF16))
        CIRCB = PE_(nc.sbuf_tensor(f"aa{b}_CIRCB", [128, NT * 512], BF16))
        EG = PE_(nc.sbuf_tensor(f"aa{b}_EG", [128, 2 * 512], BF16))
        RSF = PE_(nc.sbuf_tensor(f"aa{b}_RSF", [1, 2 * 512], F32))
        RSBF = PE_(nc.sbuf_tensor(f"aa{b}_RSBF", [1, 2 * 512], BF16))
        D1F = SCR[:, 0:512]
        D2F = SCR[:, 512:1024]
        RB = SCR[:, 512:1024]
        WMB = PE_(nc.sbuf_tensor(f"aa{b}_WMB", [128, 4 * 1024], BF16))
        AOT = PE_(nc.sbuf_tensor(f"aa{b}_AOT", [128, 512], BF16))

        with nc.Block() as blk:
            pps = contextlib.ExitStack()
            GA = pps.enter_context(nc.psum_tensor(f"aa{b}_GA", [128, 512], F32))
            GB_ = pps.enter_context(nc.psum_tensor(f"aa{b}_GB", [128, 512], F32))
            PS0 = pps.enter_context(nc.psum_tensor(f"aa{b}_PS0", [1, 512], F32))
            PS1 = pps.enter_context(nc.psum_tensor(f"aa{b}_PS1", [1, 512], F32))
            PR = pps.enter_context(nc.psum_tensor(f"aa{b}_PR", [128, 512], F32))
            PO = pps.enter_context(nc.psum_tensor(f"aa{b}_PO", [128, 512], F32))
            PG, PBC, PAO = GA, GB_, PR

            for qb in range(4):
                qsl = slice(qb * 512, (qb + 1) * 512)
                tq, aq, vq, dq = (C.val("t"), C.val("a"), C.val("v"),
                                  C.val("d"))

                def _(s, qb=qb, tq=tq):
                    C.wait(s, "t", tq)
                    C.wait(s, "d")
                    for cc in range(NC):
                        C.inc("d", s.dma_start_transpose(
                            LTB[:, cc * 512:(cc + 1) * 512],
                            DS["loc"][qb * 512:(qb + 1) * 512,
                                      cc * 128:(cc + 1) * 128]), 16)
                    for st in range(NT):
                        C.inc("d", s.dma_start(
                            CIRCB[:, st * 512:(st + 1) * 512],
                            WB["c_circ"][st * 128:(st + 1) * 128,
                                         qb * 512:(qb + 1) * 512]), 16)
                blk.sync(_)
                d_l = C.val("d")

                for hp in range(NHP):
                    h0, h1 = 2 * hp, 2 * hp + 1
                    tb, ab, vb = C.val("t"), C.val("a"), C.val("v")

                    def _(te, hp=hp, tb=tb, ab=ab, vb=vb, qb=qb):
                        xh_c = XHT[:, hp * S:(hp + 1) * S]
                        if hp > 0 or qb > 0:
                            te.wait_ge(C.s("v")[0], vb)
                            te.wait_ge(C.s("a")[0], ab)
                        C.inc("t", te.matmul(
                            GA[:], GTT[0:64, hp * 128:(hp + 1) * 128],
                            xh_c[0:64, qb * 512:(qb + 1) * 512],
                            start=True, stop=True, tile_position=(0, 0)))
                        C.inc("t", te.matmul(
                            GB_[:], GTT[64:128, hp * 128:(hp + 1) * 128],
                            xh_c[64:128, qb * 512:(qb + 1) * 512],
                            start=True, stop=True, tile_position=(64, 0)))
                        te.wait_ge(C.s("a")[0], ab + 2)
                        C.inc("t", te.matmul(PS0[:], ONES[:, 0:1],
                                             EG[:, 0:512],
                                             start=True, stop=True))
                        C.inc("t", te.matmul(PS1[:], ONES[:, 0:1],
                                             EG[:, 512:1024],
                                             start=True, stop=True))
                        te.wait_ge(C.s("v")[0], vb + 1)
                        te.matmul(PR[0:64, :], ONES[0:1, 0:64],
                                  RSBF[0:1, 0:512],
                                  start=True, stop=True, tile_position=(0, 0))
                        C.inc("t", te.matmul(PR[64:128, :], ONES[0:1, 0:64],
                                             RSBF[0:1, 512:1024],
                                             start=True, stop=True,
                                             tile_position=(0, 64)))
                        te.matmul(PO[0:64, :], GTOK[:, h0 * 64:h0 * 64 + 64],
                                  EG[:, 0:512], start=True, stop=True,
                                  tile_position=(0, 0))
                        C.inc("t", te.matmul(PO[64:128, :],
                                             GTOK[:, h1 * 64:h1 * 64 + 64],
                                             EG[:, 512:1024],
                                             start=True, stop=True,
                                             tile_position=(0, 64)))
                    blk.tensor(_)

                    def _(a, tb=tb, vb=vb):
                        a.wait_ge(C.s("t")[0], tb + 2)
                        if True:
                            a.wait_ge(C.s("t")[0], tb)
                        C.inc("a", a.activation(EG[:, 0:512], GA[:], AF.Exp,
                                                scale=0.125))
                        C.inc("a", a.activation(EG[:, 512:1024], GB_[:],
                                                AF.Exp, scale=0.125))
                    blk.scalar(_)

                    def _(v, hp=hp, tb=tb, vb=vb):
                        v.wait_ge(C.s("t")[0], tb + 4)
                        C.sw(v, "vS", v.reciprocal(RSF[0:1, 0:512], PS0[:]))
                        C.sw(v, "vS", v.reciprocal(RSF[0:1, 512:1024], PS1[:]))
                        C.inc("v", v.tensor_copy(RSBF[:], RSF[:]))
                        v.wait_ge(C.s("t")[0], tb + 6)
                        C.inc("v", v.tensor_copy(RB, PR[:]))
                        C.inc("v", v.tensor_tensor(
                            GTBUF[:, hp * 512:(hp + 1) * 512], PO[:], RB,
                            OP.mult))
                    blk.vector(_)

                a_glob_end = C.val("a")
                v_glob_end = C.val("v")
                gate_tend = []
                mix_vend = []
                for dt in range(8):
                    tb, ab, vb, dbq = (C.val("t"), C.val("a"), C.val("v"),
                                       C.val("d"))

                    def _(s, dt=dt):
                        if dt >= 2:
                            s.wait_ge(C.s("t")[0], gate_tend[dt - 2])
                        w0 = (dt % 2) * 1024
                        C.inc("d", s.dma_start(
                            WMB[:, w0:w0 + 1024].rearrange(
                                "p (c n) -> p c n", c=8),
                            WB["aa_m1"][:, dt * 128:(dt + 1) * 128].rearrange(
                                "(c p) n -> p c n", p=128)), 16)
                        C.inc("d", s.dma_start(
                            WMB[:, 2048 + w0:3072 + w0].rearrange(
                                "p (c n) -> p c n", c=8),
                            WB["aa_m2"][:, dt * 128:(dt + 1) * 128].rearrange(
                                "(c p) n -> p c n", p=128)), 16)
                    blk.sync(_)

                    def _(te, dt=dt, tb=tb, ab=ab, vb=vb, dbq=dbq):
                        te.wait_ge(C.s("d")[0], dbq + 32)
                        if dt == 0:
                            te.wait_ge(C.s("a")[0], a_glob_end)
                            te.wait_ge(C.s("v")[0], v_glob_end)
                        else:
                            te.wait_ge(C.s("a")[0], ab)
                        w0 = (dt % 2) * 1024
                        mm = None
                        for cc in range(NC):
                            mm = te.matmul(
                                PG[:], WMB[:, w0 + cc * 128:w0 + (cc + 1) * 128],
                                LTB[:, cc * 512:(cc + 1) * 512],
                                start=(cc == 0), stop=False)
                        for cc in range(NC):
                            mm = te.matmul(
                                PG[:],
                                WMB[:, 2048 + w0 + cc * 128:
                                    2048 + w0 + (cc + 1) * 128],
                                GTBUF[:, cc * 512:(cc + 1) * 512],
                                start=False, stop=(cc == NC - 1))
                        C.inc("t", mm)
                        gate_tend.append(C.val("t"))
                        te.wait_ge(C.s("v")[0], vb)
                        te.wait_ge(C.s("d")[0], d_l)
                        mm = None
                        for st in range(NT):
                            mm = te.matmul(
                                PBC[:], XHTOK[:, st * D + dt * 128:
                                              st * D + (dt + 1) * 128],
                                CIRCB[:, st * 512:(st + 1) * 512],
                                start=(st == 0), stop=(st == NT - 1))
                        C.inc("t", mm)
                    blk.tensor(_)

                    def _(a, dt=dt):
                        a.wait_ge(C.s("t")[0], gate_tend[dt])
                        if dt >= 2:
                            a.wait_ge(C.s("v")[0], mix_vend[dt - 2])
                        C.inc("a", a.activation(
                            GBK[:, (dt % 2) * 512:(dt % 2) * 512 + 512], PG[:],
                            AF.Sigmoid, bias=COLS[:, 8 + dt:9 + dt]))
                    blk.scalar(_)

                    def _(v, dt=dt, tb=tb):
                        C.wait(v, "a")
                        C.wait(v, "t")
                        dsl = slice(dt * 512, (dt + 1) * 512)
                        gsl = slice((dt % 2) * 512, (dt % 2) * 512 + 512)
                        v.tensor_tensor(D1F, LTB[:, dsl], GTBUF[:, dsl],
                                        OP.subtract)
                        v.tensor_tensor(D2F, GBK[:, gsl], D1F, OP.mult)
                        v.tensor_tensor(D1F, D2F, GTBUF[:, dsl], OP.add)
                        C.inc("v", v.tensor_tensor(MXT[:, dsl], D1F,
                                                   PBC[:], OP.add))
                        mix_vend.append(C.val("v"))
                    blk.vector(_)
                v_mx_end = C.val("v")

                outw_tend = []
                aot_dend = []
                for dot in range(8):
                    tb, ab, dbq = C.val("t"), C.val("a"), C.val("d")

                    def _(s, dot=dot):
                        if dot < 2:
                            s.wait_ge(C.s("t")[0], gate_tend[7])
                        else:
                            s.wait_ge(C.s("t")[0], outw_tend[dot - 2])
                        C.inc("d", s.dma_start(
                            WMB[:, (dot % 2) * 1024:
                                (dot % 2) * 1024 + 1024].rearrange(
                                "p (c n) -> p c n", c=8),
                            WB["aa_outw"][:, dot * 128:(dot + 1) * 128].rearrange(
                                "(c p) n -> p c n", p=128)), 16)
                    blk.sync(_)

                    def _(te, dot=dot, tb=tb, ab=ab, dbq=dbq):
                        te.wait_ge(C.s("d")[0], dbq + 16)
                        te.wait_ge(C.s("v")[0], v_mx_end)
                        if dot == 0:
                            pass
                        else:
                            te.wait_ge(C.s("a")[0], ab)
                        w0 = (dot % 2) * 1024
                        mm = None
                        for cc in range(NC):
                            mm = te.matmul(
                                PAO[:], WMB[:, w0 + cc * 128:w0 + (cc + 1) * 128],
                                MXT[:, cc * 512:(cc + 1) * 512],
                                start=(cc == 0), stop=(cc == NC - 1))
                        C.inc("t", mm)
                        outw_tend.append(C.val("t"))
                    blk.tensor(_)

                    def _(a, dot=dot):
                        C.wait(a, "t")
                        if dot >= 1:
                            a.wait_ge(C.s("d")[0], aot_dend[dot - 1])
                        C.inc("a", a.activation(
                            AOT[:], PAO[:], AF.Identity,
                            bias=COLS[:, 16 + dot:17 + dot]))
                    blk.scalar(_)

                    def _(s, dot=dot, qb=qb):
                        C.wait(s, "a")
                        C.inc("d", s.dma_start(
                            DS["aoT"][dot * 128:(dot + 1) * 128,
                                      qb * 512:(qb + 1) * 512],
                            AOT[:]), 16)
                        aot_dend.append(C.val("d"))
                    blk.sync(_)
            pps.close()
        pes.close()
        emit_resadd_featT(b, DS["aoT"])

    def emit_ffn(b, call):
        tag = f"f{b}{call}"
        fes = contextlib.ExitStack()
        FE = fes.enter_context
        GATES = FE(nc.sbuf_tensor(tag + "GATES", [128, NT * 256], BF16))
        RSUMf = FE(nc.sbuf_tensor(tag + "RSUM", [128, 256], F32))

        xls = contextlib.ExitStack()
        XHLT = xls.enter_context(nc.sbuf_tensor(tag + "XHLT", [128, NC * S],
                                                BF16))
        WSHh = xls.enter_context(nc.sbuf_tensor(tag + "WSH", [128, NC * 256],
                                                BF16))
        WSLl = xls.enter_context(nc.sbuf_tensor(tag + "WSL", [128, NC * 256],
                                                BF16))
        with nc.Block() as blk:
            def _(s):
                C.wait(s, "d")
                for cc in range(NC):
                    C.inc("d", s.dma_start_transpose(
                        XHLT[:, cc * S:(cc + 1) * S],
                        DS["xhl"][:, cc * 128:(cc + 1) * 128]), 16)
            blk.sync(_)
            v_ws = []
            for cc in range(NC):
                def _(s, cc=cc):
                    if cc >= 2:
                        s.wait_ge(C.s("v")[0], v_ws[cc - 2])
                    C.inc("d", s.dma_start(
                        SCR[:, (cc % 2) * 256:(cc % 2) * 256 + 256],
                        W["f_wsel"][cc * 128:(cc + 1) * 128, :]), 16)
                blk.sync(_)
                d_ws = C.val("d")

                def _(v, cc=cc, d_ws=d_ws):
                    v.wait_ge(C.s("d")[0], d_ws)
                    csl = slice(cc * 256, (cc + 1) * 256)
                    ssl = slice((cc % 2) * 256, (cc % 2) * 256 + 256)
                    v.tensor_copy(WSHh[:, csl], SCR[:, ssl])
                    G(v)
                    C.inc("v", v.tensor_tensor(WSLl[:, csl], SCR[:, ssl],
                                               WSHh[:, csl], OP.subtract))
                blk.vector(_)
                v_ws.append(C.val("v"))

        with nc.Block() as blk:
            pps = contextlib.ExitStack()
            PGT = [pps.enter_context(nc.psum_tensor(f"{tag}PGT{i}", [128, 256],
                                                    F32)) for i in range(2)]
            SG = xls.enter_context(nc.sbuf_tensor(tag + "SG", [128, 2 * 256],
                                                   F32))
            M12 = xls.enter_context(nc.sbuf_tensor(tag + "M12", [128, 64], F32))
            EQS = xls.enter_context(nc.sbuf_tensor(tag + "EQS", [128, 256],
                                                   F32))
            a_sig = []
            v_gate = []
            for t in range(NT):
                tb, ab, vb = C.val("t"), C.val("a"), C.val("v")

                def _(te, t=t, vb=vb):
                    if t == 0:
                        C.wait(te, "d")
                    if t >= 2:
                        te.wait_ge(C.s("a")[0], a_sig[t - 2])
                    pg = PGT[t % 2]
                    mm = None
                    for cc in range(NC):
                        csl = slice(cc * 256, (cc + 1) * 256)
                        xsl = slice(cc * S + t * 128, cc * S + (t + 1) * 128)
                        mm = te.matmul(pg[:], XHT[:, xsl], WSHh[:, csl],
                                       start=(cc == 0), stop=False)
                        mm = te.matmul(pg[:], XHT[:, xsl], WSLl[:, csl],
                                       start=False, stop=False)
                        mm = te.matmul(pg[:], XHLT[:, xsl], WSHh[:, csl],
                                       start=False, stop=(cc == NC - 1))
                    C.inc("t", mm)
                blk.tensor(_)
                t_mm = C.val("t")

                def _(a, t=t, t_mm=t_mm, vb=vb):
                    a.wait_ge(C.s("t")[0], t_mm)
                    if t >= 2:
                        a.wait_ge(C.s("v")[0], v_gate[t - 2])
                    C.inc("a", a.activation(SG[:, (t % 2) * 256:(t % 2) * 256 + 256],
                                            PGT[t % 2][:], AF.Sigmoid))
                blk.scalar(_)
                a_sig.append(C.val("a"))

                def _(v, t=t):
                    C.wait(v, "a", a_sig[t])
                    sg = SG[:, (t % 2) * 256:(t % 2) * 256 + 256]
                    sg3 = sg.rearrange("p (g e) -> p g e", e=8)
                    C.sw(v, "vS", v.tensor_reduce(M12[:, 0:32], sg3, axis=AXX,
                                                  op=OP.max))
                    m1b = M12[:, 0:32].rearrange("p (g o) -> p g o", o=1
                                                 ).broadcast_to((128, 32, 8))
                    v.tensor_tensor(EQS[:].rearrange("p (g e) -> p g e", e=8),
                                    sg3, m1b, OP.is_ge)
                    G(v)
                    v.scalar_tensor_tensor(SCR[:, 0:256], EQS[:], -1e30, sg,
                                           OP.mult, OP.add)
                    C.sw(v, "vS", v.tensor_reduce(
                        M12[:, 32:64],
                        SCR[:, 0:256].rearrange("p (g e) -> p g e", e=8),
                        axis=AXX, op=OP.max))
                    m2b = M12[:, 32:64].rearrange("p (g o) -> p g o", o=1
                                                  ).broadcast_to((128, 32, 8))
                    v.tensor_tensor(EQS[:].rearrange("p (g e) -> p g e", e=8),
                                    sg3, m2b, OP.is_ge)
                    G(v)
                    C.inc("v", v.tensor_tensor(
                        GATES[:, t * 256:(t + 1) * 256], sg, EQS[:], OP.mult))
                blk.vector(_)
                v_gate.append(C.val("v"))
            pps.close()
        xls.close()

        if _FFN_UPTO < 2:
            fes.close()
            return
        with nc.Block() as blk:
            pps = contextlib.ExitStack()
            QP = [pps.enter_context(nc.psum_tensor(f"{tag}QP{i}", [128, 512],
                                                   F32)) for i in range(4)]
            WQKB = pps.enter_context(nc.sbuf_tensor(tag + "WQKB",
                                                    [128, NC * 2048], BF16))
            QKB = pps.enter_context(nc.sbuf_tensor(tag + "QKB", [128, 2048],
                                                   BF16))
            d0 = C.val("d")

            def _(s):
                for cc in range(NC):
                    C.inc("d", s.dma_start(
                        WQKB[:, cc * 2048:(cc + 1) * 2048],
                        WB["f_wqk"][cc * 128:(cc + 1) * 128, :]), 16)
            blk.sync(_)
            d_w = C.val("d")
            v_rope = []
            for t in range(NT):
                def _(te, t=t):
                    te.wait_ge(C.s("d")[0], d_w)
                    if t >= 1:
                        te.wait_ge(C.s("v")[0], v_rope[t - 1])
                    for nb in range(4):
                        mm = None
                        for cc in range(NC):
                            mm = te.matmul(
                                QP[nb][:],
                                XHT[:, cc * S + t * 128:cc * S + (t + 1) * 128],
                                WQKB[:, cc * 2048 + nb * 512:
                                     cc * 2048 + (nb + 1) * 512],
                                start=(cc == 0), stop=(cc == NC - 1))
                        C.inc("t", mm)
                blk.tensor(_)
                t_qk = C.val("t")

                def _(v, t=t, t_qk=t_qk):
                    last = [None]
                    v.wait_ge(C.s("t")[0], t_qk)
                    if t >= 1:
                        C.wait(v, "d")
                    cosb = COS[:, t * 16:(t + 1) * 16].rearrange(
                        "p (o i) -> p o i", o=1).broadcast_to((128, 8, 16))
                    sinb = SIN[:, t * 16:(t + 1) * 16].rearrange(
                        "p (o i) -> p o i", o=1).broadcast_to((128, 8, 16))
                    for half in range(2):
                        for nb in range(2):
                            pp = QP[half * 2 + nb][:]
                            p3 = pp.rearrange("p (h x) -> p h x", h=8)
                            x1 = p3[:, :, 0:16]
                            x2 = p3[:, :, 16:32]
                            out = QKB[:, half * 1024 + nb * 512:
                                      half * 1024 + (nb + 1) * 512].rearrange(
                                      "p (h x) -> p h x", h=8)
                            t1 = SCR[:, 0:128].rearrange("p (h x) -> p h x", h=8)
                            t2 = SCR[:, 128:256].rearrange("p (h x) -> p h x", h=8)
                            v.tensor_tensor(t1, x1, cosb, OP.mult)
                            v.tensor_tensor(t2, x2, sinb, OP.mult)
                            v.tensor_tensor(out[:, :, 0:16], t1, t2, OP.subtract)
                            v.tensor_tensor(t1, x1, sinb, OP.mult)
                            v.tensor_tensor(t2, x2, cosb, OP.mult)
                            v.tensor_tensor(out[:, :, 16:32], t1, t2, OP.add)
                            last[0] = v.tensor_copy(out[:, :, 32:64],
                                                    p3[:, :, 32:64])
                    C.inc("v", last[0])
                blk.vector(_)
                v_rope.append(C.val("v"))

                def _(s, t=t):
                    s.wait_ge(C.s("v")[0], v_rope[t])
                    C.inc("d", s.dma_start(DS["qd"][t * 128:(t + 1) * 128, :],
                                           QKB[:, 0:1024]), 16)
                    C.inc("d", s.dma_start(DS["kd"][t * 128:(t + 1) * 128, :],
                                           QKB[:, 1024:2048]), 16)
                blk.sync(_)
            pps.close()

        if _FFN_UPTO < 3:
            fes.close()
            return
        with nc.Block() as blk:
            pps = contextlib.ExitStack()
            SP = [pps.enter_context(nc.psum_tensor(f"{tag}SP{i}", [128, 1024],
                                                   F32)) for i in range(2)]
            PAV = [pps.enter_context(nc.psum_tensor(f"{tag}PAV{i}", [65, 512],
                                                    F32)) for i in range(2)]
            ZP = [pps.enter_context(nc.psum_tensor(f"{tag}ZP{i}", [128, 512],
                                                   F32)) for i in range(2)]
            QT = pps.enter_context(nc.sbuf_tensor(tag + "QT", [128, S], BF16))
            KT = pps.enter_context(nc.sbuf_tensor(tag + "KT", [128, S], BF16))
            VAUG = pps.enter_context(nc.sbuf_tensor(tag + "VAUG",
                                                    [128, 2 * NT * 68], BF16))
            EB = pps.enter_context(nc.sbuf_tensor(tag + "EB", [128, 2 * NT * 512],
                                                  BF16))
            WVB = pps.enter_context(nc.sbuf_tensor(tag + "WVB", [128, 4096],
                                                   BF16))
            OTS = pps.enter_context(nc.sbuf_tensor(tag + "OTS", [128, 512], BF16))
            SRW = SCR[:, 0:512]
            ZC = SCR[:, 512:1024]

            def _(g):
                C.inc("g", g.memset(
                    VAUG[:].rearrange("p (k s) -> p k s", s=68)[:, :, 64:65],
                    1.0))
            blk.gpsimd(_)
            g_ones = C.val("g")

            for hp in range(8):
                h0, h1 = 2 * hp, 2 * hp + 1
                d_qk = []

                def _(s, hp=hp):
                    C.wait(s, "d")
                    if hp >= 1:
                        C.wait(s, "t")
                    C.inc("d", s.dma_start_transpose(
                        QT[:], DS["qd"][:, hp * 128:(hp + 1) * 128]), 16)
                    C.inc("d", s.dma_start_transpose(
                        KT[:], DS["kd"][:, hp * 128:(hp + 1) * 128]), 16)
                    d_qk.append(C.val("d"))
                blk.sync(_)

                for j, h in enumerate((h0, h1)):
                    def _(s, j=j, h=h):
                        if not (hp == 0 and j == 0):
                            C.wait(s, "t")
                        C.inc("d", s.dma_start(
                            WVB[:].rearrange("p (c n) -> p c n", c=8),
                            WB["f_wv"][h].rearrange("(c p) n -> p c n", p=128)),
                            16)
                        d_qk.append(C.val("d"))
                    blk.sync(_)
                    d_wv = C.val("d")
                    v_comb = []
                    for t in range(NT):
                        def _(te, j=j, t=t, hp=hp, d_wv=d_wv):
                            te.wait_ge(C.s("d")[0], d_wv)
                            if t >= 2:
                                te.wait_ge(C.s("v")[0], v_comb[t - 2])
                            elif not (hp == 0 and j == 0):
                                C.wait(te, "v")
                            mm = None
                            for cc in range(NC):
                                mm = te.matmul(
                                    ZP[t % 2][:],
                                    XHT[:, cc * S + t * 128:cc * S + (t + 1) * 128],
                                    WVB[:, cc * 512:(cc + 1) * 512],
                                    start=(cc == 0), stop=(cc == NC - 1))
                            C.inc("t", mm)
                        blk.tensor(_)
                        t_z = C.val("t")

                        def _(v, j=j, h=h, t=t, t_z=t_z):
                            v.wait_ge(C.s("t")[0], t_z)
                            gvb = GATES[:, t * 256 + h * 8:t * 256 + (h + 1) * 8
                                        ].rearrange("p (e o) -> p e o", o=1
                                                    ).broadcast_to((128, 8, 64))
                            zin = ZP[t % 2][:].rearrange("p (e x) -> p e x", x=64)
                            zout = ZC.rearrange("p (x e) -> p x e", e=8
                                                ).transpose((0, 2, 1))
                            v.tensor_tensor(zout, zin, gvb, OP.mult)
                            with nc.allow_low_precision(
                                    "fp32-internal reduce, bf16 store"):
                                C.sw(v, "vS", v.tensor_reduce(
                                    VAUG[:, (j * NT + t) * 68:
                                         (j * NT + t) * 68 + 64],
                                    ZC.rearrange("p (x e) -> p x e", e=8),
                                    axis=AXX, op=OP.add))
                            C.inc("v", v.nop())
                        blk.vector(_)
                        v_comb.append(C.val("v"))

                for qb in range(4):
                    ab_exp = C.val("a")
                    t_s_list = []
                    t_av_j = [None, None]

                    def _(te, hp=hp, qb=qb, ab_exp=ab_exp, t_av_j=t_av_j):
                        if qb == 0:
                            C.wait(te, "v")
                            te.wait_ge(C.s("d")[0], d_qk[0])
                            te.wait_ge(C.s("g")[0], g_ones)
                        else:
                            C.wait(te, "v")
                        for kc in range(NT + 1):
                            if kc < NT:
                                if kc >= 2:
                                    te.wait_ge(C.s("a")[0], ab_exp + kc - 1)
                                elif qb > 0 or hp > 0:
                                    te.wait_ge(C.s("a")[0], ab_exp - 1 + kc)
                                te.matmul(
                                    SP[kc % 2][:, 0:512],
                                    KT[0:64, kc * 128:(kc + 1) * 128],
                                    QT[0:64, qb * 512:(qb + 1) * 512],
                                    start=True, stop=True,
                                    tile_position=(0, 0))
                                C.inc("t", te.matmul(
                                    SP[kc % 2][:, 512:1024],
                                    KT[64:128, kc * 128:(kc + 1) * 128],
                                    QT[64:128, qb * 512:(qb + 1) * 512],
                                    start=True, stop=True,
                                    tile_position=(64, 0)))
                                t_s_list.append(C.val("t"))
                            if kc >= 1:
                                kp = kc - 1
                                te.wait_ge(C.s("a")[0], ab_exp + kp + 1)
                                for j in range(2):
                                    mm = te.matmul(
                                        PAV[j][:],
                                        VAUG[:, (j * NT + kp) * 68:
                                             (j * NT + kp) * 68 + 65],
                                        EB[:, kp * 1024 + j * 512:
                                           kp * 1024 + (j + 1) * 512],
                                        start=(kp == 0), stop=(kp == NT - 1))
                                    if kp == NT - 1:
                                        C.inc("t", mm)
                                        t_av_j[j] = C.val("t")
                    blk.tensor(_)

                    def _(a, qb=qb, t_s_list=t_s_list):
                        for kc in range(NT):
                            a.wait_ge(C.s("t")[0], t_s_list[kc])
                            C.inc("a", a.activation(
                                EB[:, kc * 1024:(kc + 1) * 1024],
                                SP[kc % 2][:], AF.Exp, scale=0.125))
                    blk.scalar(_)

                    for j, h in enumerate((h0, h1)):
                        t_av = t_av_j[j]

                        def _(v, j=j, t_av=t_av):
                            v.wait_ge(C.s("t")[0], t_av)
                            C.wait(v, "d")
                            v.tensor_copy(OTS[0:64, :], PAV[j][0:64, :])
                            C.inc("v", v.tensor_copy(SRW[64:65, 0:512],
                                                     PAV[j][64:65, :]))
                        blk.vector(_)
                        v_o = C.val("v")

                        def _(s, j=j, h=h, qb=qb, v_o=v_o):
                            s.wait_ge(C.s("v")[0], v_o)
                            C.inc("d", s.dma_start(
                                DS["od"][h * 64:(h + 1) * 64,
                                         qb * 512:(qb + 1) * 512],
                                OTS[0:64, :]), 16)
                            C.inc("d", s.dma_start(
                                sums_d[h:h + 1, qb * 512:(qb + 1) * 512],
                                SRW[64:65, 0:512]), 16)
                        blk.sync(_)
            pps.close()

        if _FFN_UPTO < 4:
            fes.close()
            return
        with nc.Block() as blk:
            pps = contextlib.ExitStack()
            PTR = pps.enter_context(nc.psum_tensor(tag + "PTR", [128, 16], F32))
            S16 = pps.enter_context(nc.sbuf_tensor(tag + "S16", [16, S], F32))
            OTOK = pps.enter_context(nc.sbuf_tensor(tag + "OTOK", [128, D], BF16))
            GS = pps.enter_context(nc.sbuf_tensor(tag + "GS", [128, 128], F32))
            OPX = pps.enter_context(nc.sbuf_tensor(tag + "OPX", [128, 16 * 512],
                                                   BF16))
            d0 = C.val("d")

            def _(s):
                C.wait(s, "d")
                C.inc("d", s.dma_start(S16[:], sums_d), 16)
            blk.sync(_)
            d_s16 = C.val("d")

            t_tr = []
            v_rs = []
            for t in range(NT):
                def _(te, t=t):
                    te.wait_ge(C.s("d")[0], d_s16)
                    if t >= 1:
                        te.wait_ge(C.s("v")[0], v_rs[t - 1])
                    C.inc("t", te.transpose(PTR[:],
                                            S16[0:16, t * 128:(t + 1) * 128],
                                            IDENTF[0:16, 0:16]))
                blk.tensor(_)
                t_tr.append(C.val("t"))

                def _(v, t=t):
                    v.wait_ge(C.s("t")[0], t_tr[t])
                    C.sw(v, "vS", v.reciprocal(RSUMf[:, t * 16:(t + 1) * 16],
                                               PTR[:]))
                    C.inc("v", v.nop())
                blk.vector(_)
                v_rs.append(C.val("v"))

            v_opx = []
            d_opd = []
            for t in range(NT):
                def _(s, t=t):
                    if t == 0:
                        C.wait(s, "d")
                    if t >= 1:
                        s.wait_ge(C.s("v")[0], v_opx[t - 1])
                    C.inc("d", s.dma_start_transpose(
                        OTOK[:], DS["od"][:, t * 128:(t + 1) * 128]), 16)
                blk.sync(_)
                d_ot = C.val("d")

                def _(v, t=t, d_ot=d_ot):
                    v.wait_ge(C.s("d")[0], d_ot)
                    if t >= 1:
                        C.wait(v, "d")
                    v.wait_ge(C.s("v")[0], v_rs[NT - 1])
                    gob = GATES[:, t * 256 + 128:(t + 1) * 256].rearrange(
                        "p (h e) -> p h e", e=8)
                    rsb = RSUMf[:, t * 16:(t + 1) * 16].rearrange(
                        "p (h o) -> p h o", o=1).broadcast_to((128, 16, 8))
                    v.tensor_tensor(GS[:].rearrange("p (h e) -> p h e", e=8),
                                    gob, rsb, OP.mult)
                    G(v)
                    lastv = None
                    for h in range(16):
                        orep = OTOK[:, h * 64:(h + 1) * 64].rearrange(
                            "p (o x) -> p o x", o=1).broadcast_to((128, 8, 64))
                        gsb = GS[:, h * 8:(h + 1) * 8].rearrange(
                            "p (e o) -> p e o", o=1).broadcast_to((128, 8, 64))
                        lastv = v.tensor_tensor(
                            OPX[:, h * 512:(h + 1) * 512].rearrange(
                                "p (e x) -> p e x", e=8),
                            orep, gsb, OP.mult)
                    C.inc("v", lastv)
                blk.vector(_)
                v_opx.append(C.val("v"))

                def _(s, t=t):
                    s.wait_ge(C.s("v")[0], v_opx[t])
                    C.inc("d", s.dma_start(DS["opd"][t * 128:(t + 1) * 128, :],
                                           OPX[:]), 16)
                blk.sync(_)
                d_opd.append(C.val("d"))
            pps.close()

        if _FFN_UPTO < 5:
            fes.close()
            return
        if _FFN_UPTO < 6:
            fes.close()
            return
        LN1Gr = FE(nc.sbuf_tensor(tag + "LN1G", [128, D], F32))
        LN1Br = FE(nc.sbuf_tensor(tag + "LN1B", [128, D], F32))
        LN2Gr = FE(nc.sbuf_tensor(tag + "LN2G", [128, D], F32))
        LN2Br = FE(nc.sbuf_tensor(tag + "LN2B", [128, D], F32))
        B1C = FE(nc.sbuf_tensor(tag + "B1C", [128, 32], F32))
        B2C = FE(nc.sbuf_tensor(tag + "B2C", [128, 8], F32))
        ylns = contextlib.ExitStack()
        YLN = ylns.enter_context(nc.sbuf_tensor(tag + "YLN", [128, NT * D],
                                                BF16))
        with nc.Block() as blk:
            def _(s):
                C.inc("d", s.dma_start(LN1Gr[:], W["f_ln1g"]), 16)
                C.inc("d", s.dma_start(LN1Br[:], W["f_ln1b"]), 16)
                C.inc("d", s.dma_start(LN2Gr[:], W["f_ln2g"]), 16)
                C.inc("d", s.dma_start(LN2Br[:], W["f_ln2b"]), 16)
                C.inc("d", s.dma_start(
                    B1C[:].rearrange("p (c o) -> p c o", o=1),
                    W["f_b1"].rearrange("(c p) o -> p c o", p=128),
                    allow_slow_non_contiguous=True), 16)
                C.inc("d", s.dma_start(
                    B2C[:].rearrange("p (c o) -> p c o", o=1),
                    W["f_b2"].rearrange("(c p) o -> p c o", p=128),
                    allow_slow_non_contiguous=True), 16)
            blk.sync(_)
        with nc.Block() as blk:
            pps = contextlib.ExitStack()
            PW = [pps.enter_context(nc.psum_tensor(f"{tag}PW{i}", [128, 1024],
                                                   F32)) for i in range(4)]
            LHW = [pps.enter_context(nc.sbuf_tensor(f"{tag}LHW{i}", [128, 512],
                                                    BF16)) for i in range(4)]
            WOB = [pps.enter_context(nc.sbuf_tensor(f"{tag}WOB{i}", [128, 1024],
                                                    BF16)) for i in range(4)]
            XRT = pps.enter_context(nc.sbuf_tensor(tag + "XRT", [128, 4 * D],
                                                   BF16))
            for grp in range(4):
                def _(s, grp=grp):
                    if grp > 0:
                        C.wait(s, "v")
                    for ti in range(4):
                        t = grp * 4 + ti
                        C.inc("d", s.dma_start(
                            XRT[:, ti * D:(ti + 1) * D],
                            DS["xh"][t * 128:(t + 1) * 128, :]), 16)
                blk.sync(_)
                d_xr = C.val("d")
                t_wo = []
                d_lh = []
                g_wb = []
                for ce in range(64):
                    h, ep = divmod(ce, 4)

                    def _(s, grp=grp, ce=ce):
                        if ce == 0:
                            s.wait_ge(C.s("d")[0], d_opd[grp * 4 + 3])
                        if ce >= 4:
                            s.wait_ge(C.s("t")[0], t_wo[ce - 4])
                        elif grp > 0:
                            s.wait_ge(C.s("t")[0], t_wo_prev[60 + ce])
                        C.inc("d", s.dma_start_transpose(
                            LHW[ce % 4][:],
                            DS["opd"][grp * 512:(grp + 1) * 512,
                                      ce * 128:(ce + 1) * 128]), 16)
                    blk.sync(_)
                    d_lh.append(C.val("d"))

                    def _(g, grp=grp, ce=ce, h=h, ep=ep):
                        if ce >= 4:
                            g.wait_ge(C.s("t")[0], t_wo[ce - 4])
                        elif grp > 0:
                            g.wait_ge(C.s("t")[0], t_wo_prev[60 + ce])
                        C.inc("g", g.dma_start(
                            WOB[ce % 4][:].rearrange("p (a n) -> p a n", a=1),
                            WB["f_wo"][h, ep].rearrange("p (a n) -> p a n", a=1)),
                            16)
                    blk.gpsimd(_)
                    g_wb.append(C.val("g"))

                    def _(te, grp=grp, ce=ce):
                        te.wait_ge(C.s("d")[0], d_lh[ce])
                        te.wait_ge(C.s("g")[0], g_wb[ce])
                        if ce == 0:
                            C.wait(te, "v")
                        mm = None
                        for ti in range(4):
                            tsl = slice(ti * 128, (ti + 1) * 128)
                            for nh in range(2):
                                mm = te.matmul(
                                    PW[ti][:, nh * 512:(nh + 1) * 512],
                                    LHW[ce % 4][:, tsl],
                                    WOB[ce % 4][:, nh * 512:(nh + 1) * 512],
                                    start=(ce == 0), stop=(ce == 63))
                        C.inc("t", mm)
                    blk.tensor(_)
                    t_wo.append(C.val("t"))
                t_wo_prev = t_wo

                def _(v, grp=grp, d_xr=d_xr):
                    v.wait_ge(C.s("t")[0], t_wo[63])
                    v.wait_ge(C.s("d")[0], d_xr)
                    for ti in range(4):
                        t = grp * 4 + ti
                        C.inc("v", v.tensor_tensor(
                            YLN[:, t * D:(t + 1) * D], PW[ti][:],
                            XRT[:, ti * D:(ti + 1) * D], OP.add))
                blk.vector(_)
            pps.close()

        if _FFN_UPTO < 7:
            ylns.close()
            fes.close()
            return
        with nc.Block() as blk:
            a0, v0 = C.val("a"), C.val("v")

            def _(v):
                C.wait(v, "d")
                for t in range(NT):
                    src = YLN[:, t * D:(t + 1) * D]
                    C.sw(v, "vS", v.tensor_reduce(STAT[:, 0:1], src, axis=AXX,
                                                  op=OP.add))
                    C.sw(v, "vS", v.tensor_scalar(STAT[:, 5:6], STAT[:, 0:1],
                                                  1.0 / D, None, OP.mult))
                    v.wait_ge(C.s("a")[0], a0 + 2 * t + 1)
                    C.sw(v, "vS", v.tensor_scalar(STAT[:, 3:4], STAT[:, 1:2],
                                                  1.0 / D, None, OP.mult))
                    C.sw(v, "vS", v.tensor_tensor(STAT[:, 2:3], STAT[:, 5:6],
                                                  STAT[:, 5:6], OP.mult))
                    C.sw(v, "vS", v.tensor_tensor(STAT[:, 3:4], STAT[:, 3:4],
                                                  STAT[:, 2:3], OP.subtract))
                    C.inc("v", v.tensor_scalar(STAT[:, 3:4], STAT[:, 3:4],
                                               1e-5, None, OP.add))
                    v.wait_ge(C.s("a")[0], a0 + 2 * t + 2)
                    v.tensor_scalar(SCR[:], src, STAT[:, 5:6], STAT[:, 4:5],
                                    OP.subtract, OP.mult)
                    v.tensor_tensor(SCR[:], SCR[:], LN1Gr[:], OP.mult)
                    C.inc("v", v.tensor_tensor(src, SCR[:], LN1Br[:], OP.add))
            blk.vector(_)

            def _(a):
                for t in range(NT):
                    C.inc("a", a.activation(JUNK[:], YLN[:, t * D:(t + 1) * D],
                                            AF.Square, accum_out=STAT[:, 1:2]))
                    a.wait_ge(C.s("v")[0], v0 + 2 * t + 1)
                    C.sw(a, "aS", a.activation(STAT[:, 4:5], STAT[:, 3:4], AF.Ln))
                    C.inc("a", a.activation(STAT[:, 4:5], STAT[:, 4:5], AF.Exp,
                                            scale=-0.5))
            blk.scalar(_)

            def _(s):
                for t in range(NT):
                    s.wait_ge(C.s("v")[0], v0 + 2 * (t + 1))
                    C.inc("d", s.dma_start(DS["l1"][t * 128:(t + 1) * 128, :],
                                           YLN[:, t * D:(t + 1) * D]), 16)
            blk.sync(_)

        if _FFN_UPTO < 8:
            ylns.close()
            fes.close()
            return
        ylns.close()

        with nc.Block() as blk:
            pps = contextlib.ExitStack()
            HP = [pps.enter_context(nc.psum_tensor(f"{tag}HP{i}", [128, 512],
                                                   F32)) for i in range(2)]
            OP2 = [pps.enter_context(nc.psum_tensor(f"{tag}OP2{i}", [128, 512],
                                                    F32)) for i in range(2)]
            L1T = pps.enter_context(nc.sbuf_tensor(tag + "L1T", [128, NC * 512],
                                                   BF16))
            HT = pps.enter_context(nc.sbuf_tensor(tag + "HT", [128, 32 * 512],
                                                  BF16))
            W1C = [pps.enter_context(nc.sbuf_tensor(f"{tag}W1C{i}", [128, 1024],
                                                    BF16)) for i in range(2)]
            W2C = [pps.enter_context(nc.sbuf_tensor(f"{tag}W2C0", [128, 4096],
                                                    BF16))] * 2
            OT2 = pps.enter_context(nc.sbuf_tensor(tag + "OT2", [128, 2 * 512],
                                                   BF16))

            for tb in range(4):
                d0 = C.val("d")

                def _(s, tb=tb, d0=d0):
                    C.wait(s, "d", d0)
                    if tb >= 1:
                        C.wait(s, "t")
                    for cc in range(NC):
                        C.inc("d", s.dma_start_transpose(
                            L1T[:, cc * 512:(cc + 1) * 512],
                            DS["l1"][tb * 512:(tb + 1) * 512,
                                     cc * 128:(cc + 1) * 128]), 16)
                blk.sync(_)
                d_l1 = C.val("d")

                ab_relu = C.val("a")
                t_w1 = []
                for ft in range(32):
                    def _(s, ft=ft):
                        if ft >= 2:
                            s.wait_ge(C.s("t")[0], t_w1[ft - 2])
                        C.inc("d", s.dma_start(
                            W1C[ft % 2][:].rearrange("p (c n) -> p c n", c=8),
                            WB["f_w1"][:, ft * 128:(ft + 1) * 128].rearrange(
                                "(c p) n -> p c n", p=128)), 16)
                    blk.sync(_)
                    d_w1 = C.val("d")

                    def _(te, ft=ft, d_w1=d_w1, d_l1=d_l1, ab_relu=ab_relu):
                        te.wait_ge(C.s("d")[0], max(d_w1, d_l1))
                        if ft >= 2:
                            te.wait_ge(C.s("a")[0], ab_relu + ft - 1)
                        mm = None
                        for cc in range(NC):
                            mm = te.matmul(
                                HP[ft % 2][:],
                                W1C[ft % 2][:, cc * 128:(cc + 1) * 128],
                                L1T[:, cc * 512:(cc + 1) * 512],
                                start=(cc == 0), stop=(cc == NC - 1))
                        C.inc("t", mm)
                    blk.tensor(_)
                    t_w1.append(C.val("t"))

                def _(a):
                    for ft in range(32):
                        a.wait_ge(C.s("t")[0], t_w1[ft])
                        C.inc("a", a.activation(HT[:, ft * 512:(ft + 1) * 512],
                                                HP[ft % 2][:], AF.Relu,
                                                bias=B1C[:, ft:ft + 1]))
                blk.scalar(_)
                a_relu = [ab_relu + ft + 1 for ft in range(32)]

                t_w2 = []
                a_o2 = []
                for dt in range(8):
                    def _(s, dt=dt, tb=tb):
                        if dt >= 1:
                            s.wait_ge(C.s("t")[0], t_w2[dt - 1])
                        C.inc("d", s.dma_start(
                            W2C[dt % 2][:].rearrange("p (c n) -> p c n", c=32),
                            WB["f_w2"][:, dt * 128:(dt + 1) * 128].rearrange(
                                "(c p) n -> p c n", p=128)), 16)
                    blk.sync(_)
                    d_w2 = C.val("d")

                    def _(te, dt=dt, d_w2=d_w2):
                        te.wait_ge(C.s("d")[0], d_w2)
                        te.wait_ge(C.s("a")[0], a_relu[31])
                        if dt >= 2:
                            te.wait_ge(C.s("a")[0], a_o2[dt - 2])
                        mm = None
                        for fc in range(32):
                            mm = te.matmul(
                                OP2[dt % 2][:],
                                W2C[dt % 2][:, fc * 128:(fc + 1) * 128],
                                HT[:, fc * 512:(fc + 1) * 512],
                                start=(fc == 0), stop=(fc == 31))
                        C.inc("t", mm)
                    blk.tensor(_)
                    t_w2.append(C.val("t"))

                    def _(a, dt=dt):
                        a.wait_ge(C.s("t")[0], t_w2[dt])
                        if dt >= 2:
                            C.wait(a, "d")
                        C.inc("a", a.activation(
                            OT2[:, (dt % 2) * 512:(dt % 2) * 512 + 512],
                            OP2[dt % 2][:], AF.Identity,
                            bias=B2C[:, dt:dt + 1]))
                    blk.scalar(_)
                    a_o2.append(C.val("a"))

                    def _(s, dt=dt, tb=tb):
                        C.wait(s, "a", a_o2[dt])
                        C.inc("d", s.dma_start(
                            DS["m2T"][dt * 128:(dt + 1) * 128,
                                      tb * 512:(tb + 1) * 512],
                            OT2[:, (dt % 2) * 512:(dt % 2) * 512 + 512]), 16)
                    blk.sync(_)
            pps.close()

        if _FFN_UPTO < 9:
            fes.close()
            return
        with nc.Block() as blk:
            d0 = C.val("d")
            v_fin = []
            LB = FE(nc.sbuf_tensor(tag + "LB", [128, 2 * D], BF16))
            for t in range(NT):
                def _(s, t=t):
                    if t == 0:
                        C.wait(s, "d", d0)
                    if t >= 2:
                        s.wait_ge(C.s("v")[0], v_fin[t - 2])
                    C.inc("d", s.dma_start_transpose(
                        SCRB[:, (t % 2) * D:(t % 2) * D + D],
                        DS["m2T"][:, t * 128:(t + 1) * 128]), 16)
                    C.inc("d", s.dma_start(
                        LB[:, (t % 2) * D:(t % 2) * D + D],
                        DS["l1"][t * 128:(t + 1) * 128, :]), 16)
                blk.sync(_)
                d_m2 = C.val("d")
                a0, v0 = C.val("a"), C.val("v")

                def _(v, t=t, d_m2=d_m2, a0=a0, v0=v0):
                    v.wait_ge(C.s("d")[0], d_m2)
                    v.tensor_tensor(SCR[:], LB[:, (t % 2) * D:(t % 2) * D + D],
                                    SCRB[:, (t % 2) * D:(t % 2) * D + D], OP.add)
                    C.sw(v, "vS", v.tensor_reduce(STAT[:, 0:1], SCR[:],
                                                  axis=AXX, op=OP.add))
                    C.sw(v, "vS", v.tensor_scalar(STAT[:, 5:6], STAT[:, 0:1],
                                                  1.0 / D, None, OP.mult))
                    C.inc("v", v.nop())
                    v.wait_ge(C.s("a")[0], a0 + 1)
                    C.sw(v, "vS", v.tensor_scalar(STAT[:, 3:4], STAT[:, 1:2],
                                                  1.0 / D, None, OP.mult))
                    C.sw(v, "vS", v.tensor_tensor(STAT[:, 2:3], STAT[:, 5:6],
                                                  STAT[:, 5:6], OP.mult))
                    C.sw(v, "vS", v.tensor_tensor(STAT[:, 3:4], STAT[:, 3:4],
                                                  STAT[:, 2:3], OP.subtract))
                    C.inc("v", v.tensor_scalar(STAT[:, 3:4], STAT[:, 3:4],
                                               1e-5, None, OP.add))
                    v.wait_ge(C.s("a")[0], a0 + 2)
                    v.tensor_scalar(SCR[:], SCR[:], STAT[:, 5:6], STAT[:, 4:5],
                                    OP.subtract, OP.mult)
                    v.tensor_tensor(SCR[:], SCR[:], LN2Gr[:], OP.mult)
                    v.tensor_tensor(SCR[:], SCR[:], LN2Br[:], OP.add)
                    C.inc("v", v.tensor_tensor(xres(b, t), xres(b, t), SCR[:],
                                               OP.add))
                blk.vector(_)
                v_fin.append(C.val("v"))

                def _(a, t=t, a0=a0, v0=v0):
                    a.wait_ge(C.s("v")[0], v0 + 1)
                    C.inc("a", a.activation(JUNK[:], SCR[:], AF.Square,
                                            accum_out=STAT[:, 1:2]))
                    a.wait_ge(C.s("v")[0], v0 + 2)
                    C.sw(a, "aS", a.activation(STAT[:, 4:5], STAT[:, 3:4], AF.Ln))
                    C.inc("a", a.activation(STAT[:, 4:5], STAT[:, 4:5], AF.Exp,
                                            scale=-0.5))
                blk.scalar(_)
        fes.close()

    def emit_attn_b(b):
        tag = f"ab{b}"
        bes = contextlib.ExitStack()
        BE = bes.enter_context
        WALL = BE(nc.sbuf_tensor(tag + "WALL", [128, NT * 264], F32))
        IDXA = BE(nc.sbuf_tensor(tag + "IDXA", [128, NT * 33], I32))

        d1s = contextlib.ExitStack()
        WWH = d1s.enter_context(nc.sbuf_tensor(tag + "WWH", [128, NC * 152], BF16))
        WWL = d1s.enter_context(nc.sbuf_tensor(tag + "WWL", [128, NC * 152], BF16))
        WQB = d1s.enter_context(nc.sbuf_tensor(tag + "WQB", [128, 152], F32))
        KWR = d1s.enter_context(nc.sbuf_tensor(tag + "KWR", [128, 16], F32))
        SWV = d1s.enter_context(nc.sbuf_tensor(tag + "SWV", [128, 152], F32))
        FPD = d1s.enter_context(nc.sbuf_tensor(tag + "FPD", [128, 32], F32))
        REL = d1s.enter_context(nc.sbuf_tensor(tag + "REL", [128, 264], F32))
        KEY = d1s.enter_context(nc.sbuf_tensor(tag + "KEY", [128, 4224], F32))
        LOGB = d1s.enter_context(nc.sbuf_tensor(tag + "LOG", [128, 264], F32))
        ENV = d1s.enter_context(nc.sbuf_tensor(tag + "ENV", [128, 264], F32))
        TMW = d1s.enter_context(nc.sbuf_tensor(tag + "TMW", [128, 264], F32))
        VMB = d1s.enter_context(nc.sbuf_tensor(tag + "VMB", [128, 33], F32))
        IWK = d1s.enter_context(nc.sbuf_tensor(tag + "IWK", [128, 4 * 33], F32))
        XHLT = d1s.enter_context(nc.sbuf_tensor(tag + "XHLT", [128, NC * S],
                                                BF16))

        with nc.Block() as blk:
            pps = contextlib.ExitStack()
            PWQ = [pps.enter_context(nc.psum_tensor(f"{tag}PWQ{i}", [128, 152],
                                                    F32)) for i in range(2)]
            d0 = C.val("d")

            def _(s):
                C.wait(s, "d")
                for cc in range(NC):
                    C.inc("d", s.dma_start_transpose(
                        XHLT[:, cc * S:(cc + 1) * S],
                        DS["xhl"][:, cc * 128:(cc + 1) * 128]), 16)
                C.inc("d", s.dma_start(WQB[:], W["ab_wqb"]), 16)
                C.inc("d", s.dma_start(KWR[:], W["ab_kw"]), 16)
            blk.sync(_)
            v_wl = []
            for cc in range(NC):
                def _(s, cc=cc):
                    if cc >= 2:
                        s.wait_ge(C.s("v")[0], v_wl[cc - 2])
                    C.inc("d", s.dma_start(
                        SCR[:, (cc % 2) * 256:(cc % 2) * 256 + 152],
                        W["ab_wwq"][cc * 128:(cc + 1) * 128, :]), 16)
                blk.sync(_)
                d_w = C.val("d")

                def _(v, cc=cc, d_w=d_w):
                    v.wait_ge(C.s("d")[0], d_w)
                    csl = slice(cc * 152, (cc + 1) * 152)
                    ssl = slice((cc % 2) * 256, (cc % 2) * 256 + 152)
                    v.tensor_copy(WWH[:, csl], SCR[:, ssl])
                    G(v)
                    C.inc("v", v.tensor_tensor(WWL[:, csl], SCR[:, ssl],
                                               WWH[:, csl], OP.subtract))
                blk.vector(_)
                v_wl.append(C.val("v"))

            for t in range(NT):
                tb, ab, vb = C.val("t"), C.val("a"), C.val("v")

                def _(te, t=t, vb=vb):
                    if t == 0:
                        C.wait(te, "d")
                        C.wait(te, "v")
                    if t >= 2:
                        te.wait_ge(C.s("v")[0], v_d1[t - 2])
                    pg = PWQ[t % 2]
                    mm = None
                    for cc in range(NC):
                        csl = slice(cc * 152, (cc + 1) * 152)
                        xsl = slice(cc * S + t * 128, cc * S + (t + 1) * 128)
                        mm = te.matmul(pg[:], XHT[:, xsl], WWH[:, csl],
                                       start=(cc == 0), stop=False)
                        mm = te.matmul(pg[:], XHT[:, xsl], WWL[:, csl],
                                       start=False, stop=False)
                        mm = te.matmul(pg[:], XHLT[:, xsl], WWH[:, csl],
                                       start=False, stop=(cc == NC - 1))
                    C.inc("t", mm)
                blk.tensor(_)
                t_p = C.val("t")

                def _(v, t=t, t_p=t_p, ab=ab):
                    v.wait_ge(C.s("t")[0], t_p)
                    C.inc("v", v.tensor_tensor(SWV[:], PWQ[t % 2][:], WQB[:],
                                               OP.add))
                blk.vector(_)
                v_bias = C.val("v")

                def _(a, t=t, v_bias=v_bias):
                    a.wait_ge(C.s("v")[0], v_bias)
                    C.sw(a, "aS", a.activation(SWV[:], SWV[:], AF.Silu))
                    C.inc("a", a.activation(FPD[:, 0:8], SWV[:, 0:8],
                                            AF.Sigmoid))
                    C.inc("a", a.activation(FPD[:, 8:16], SWV[:, 8:16],
                                            AF.Tanh))
                    C.inc("a", a.activation(FPD[:, 16:24], SWV[:, 16:24],
                                            AF.Sigmoid))
                blk.scalar(_)
                a_fpd = C.val("a")

                def _(v, t=t, a_fpd=a_fpd):
                    v.wait_ge(C.s("a")[0], a_fpd)
                    C.sw(v, "vS", v.tensor_scalar(FPD[:, 0:8], FPD[:, 0:8],
                                                  15.0, 1.0, OP.mult, OP.add))
                    C.sw(v, "vS", v.tensor_scalar(FPD[:, 8:16], FPD[:, 8:16],
                                                  16.0, None, OP.mult))
                    C.sw(v, "vS", v.tensor_scalar(FPD[:, 16:24], FPD[:, 16:24],
                                                  9.5, 0.5, OP.mult, OP.add))
                    C.sw(v, "vS", v.tensor_reduce(FPD[:, 24:25], FPD[:, 0:8],
                                                  axis=AXX, op=OP.add))
                    C.sw(v, "vS", v.tensor_scalar(FPD[:, 24:25], FPD[:, 24:25],
                                                  0.125, None, OP.mult))
                    C.sw(v, "vS", v.tensor_reduce(FPD[:, 25:26], FPD[:, 8:16],
                                                  axis=AXX, op=OP.add))
                    C.sw(v, "vS", v.tensor_scalar(FPD[:, 25:26], FPD[:, 25:26],
                                                  0.125, None, OP.mult))
                    C.sw(v, "vS", v.tensor_scalar(IWK[:, 0:33], OFFS[:],
                                                  FPD[:, 24:25], FPD[:, 25:26],
                                                  OP.mult, OP.add))
                    C.sw(v, "vS", v.tensor_scalar(IWK[:, 0:33], IWK[:, 0:33],
                                                  LCOL[:, t:t + 1], None,
                                                  OP.add))
                    v.tensor_single_scalar(VMB[:], IWK[:, 0:33], 0.0, OP.is_ge)
                    G(v)
                    v.tensor_single_scalar(IWK[:, 33:66], IWK[:, 0:33], 2048.0,
                                           OP.is_lt)
                    G(v)
                    C.sw(v, "vS", v.tensor_tensor(VMB[:], VMB[:], IWK[:, 33:66],
                                                  OP.mult))
                    C.sw(v, "vS", v.tensor_scalar(IWK[:, 0:33], IWK[:, 0:33],
                                                  0.0, 2047.0, OP.max, OP.min))
                    C.sw(v, "vS", v.tensor_copy(
                        IDXA[:, t * 33:(t + 1) * 33], IWK[:, 0:33]))
                    C.sw(v, "vS", v.tensor_copy(IWK[:, 33:66],
                                                IDXA[:, t * 33:(t + 1) * 33]))
                    v.tensor_tensor(IWK[:, 66:99], IWK[:, 33:66], IWK[:, 0:33],
                                    OP.is_gt)
                    G(v)
                    v.tensor_tensor(IWK[:, 33:66], IWK[:, 33:66], IWK[:, 66:99],
                                    OP.subtract)
                    G(v)
                    C.sw(v, "vS", v.tensor_copy(IDXA[:, t * 33:(t + 1) * 33],
                                                IWK[:, 33:66]))
                    frq = FPD[:, 0:8].rearrange("p (h o) -> p h o", o=1
                                                ).broadcast_to((128, 8, 33))
                    ab_ = AOFFS[:].rearrange("p (o s) -> p o s", o=1
                                             ).broadcast_to((128, 8, 33))
                    C.inc("v", v.tensor_tensor(
                        REL[:].rearrange("p (h s) -> p h s", h=8), ab_, frq,
                        OP.mult))
                    relb = REL[:].rearrange("p (x o) -> p x o", o=1
                                            ).broadcast_to((128, 264, 16))
                    kwb = KWR[:].rearrange("p (o i) -> p o i", o=1
                                           ).broadcast_to((128, 264, 16))
                    C.inc("v", v.tensor_tensor(
                        KEY[:].rearrange("p (x i) -> p x i", i=16), relb, kwb,
                        OP.mult))
                blk.vector(_)
                v_idx = C.val("v")

                def _(a, t=t, v_idx=v_idx):
                    a.wait_ge(C.s("v")[0], v_idx)
                    C.inc("a", a.activation(KEY[:], KEY[:], AF.Silu))
                blk.scalar(_)
                a_key = C.val("a")

                def _(v, t=t, a_key=a_key):
                    v.wait_ge(C.s("a")[0], a_key)
                    qb_ = SWV[:, 24:152].rearrange("p (h i) -> p h i", i=16
                                                   ).rearrange(
                        "p h i -> p h () i" if False else "p h i -> p h i")
                    qb3 = SWV[:, 24:152].rearrange(
                        "p (h i) -> p h i", i=16).unsqueeze(2).broadcast_to(
                        (128, 8, 33, 16))
                    C.inc("v", v.tensor_tensor(
                        KEY[:].rearrange("p (h s i) -> p h s i", h=8, s=33),
                        KEY[:].rearrange("p (h s i) -> p h s i", h=8, s=33),
                        qb3, OP.mult))
                    C.sw(v, "vS", v.tensor_reduce(
                        LOGB[:], KEY[:].rearrange("p (x i) -> p x i", i=16),
                        axis=AXX, op=OP.add))
                    C.sw(v, "vS", v.tensor_scalar(LOGB[:], LOGB[:], 0.25,
                                                  None, OP.mult))
                    C.sw(v, "vS", v.tensor_reduce(
                        FPD[:, 26:34] if False else IWK[:, 0:8],
                        LOGB[:].rearrange("p (h s) -> p h s", s=33),
                        axis=AXX, op=OP.max))
                    mb = IWK[:, 0:8].rearrange("p (h o) -> p h o", o=1
                                               ).broadcast_to((128, 8, 33))
                    C.inc("v", v.tensor_tensor(
                        LOGB[:].rearrange("p (h s) -> p h s", s=33),
                        LOGB[:].rearrange("p (h s) -> p h s", s=33), mb,
                        OP.subtract))
                    C.sw(v, "vS", v.tensor_scalar(IWK[:, 8:16], FPD[:, 16:24],
                                                  0.1, None, OP.max))
                    C.sw(v, "vS", v.reciprocal(IWK[:, 16:24], IWK[:, 8:16]))
                    rdb = IWK[:, 16:24].rearrange("p (h o) -> p h o", o=1
                                                  ).broadcast_to((128, 8, 33))
                    C.inc("v", v.tensor_tensor(
                        ENV[:].rearrange("p (h s) -> p h s", s=33),
                        REL[:].rearrange("p (h s) -> p h s", s=33), rdb,
                        OP.mult))
                blk.vector(_)
                v_log = C.val("v")

                def _(a, t=t, v_log=v_log):
                    a.wait_ge(C.s("v")[0], v_log)
                    C.sw(a, "aS", a.activation(LOGB[:], LOGB[:], AF.Exp))
                    C.inc("a", a.activation(ENV[:], ENV[:], AF.Exp, scale=-1.0))
                blk.scalar(_)
                a_exp = C.val("a")

                def _(v, t=t, a_exp=a_exp):
                    v.wait_ge(C.s("a")[0], a_exp)
                    vmb3 = VMB[:].rearrange("p (o s) -> p o s", o=1
                                            ).broadcast_to((128, 8, 33))
                    C.inc("v", v.tensor_tensor(
                        TMW[:].rearrange("p (h s) -> p h s", s=33),
                        LOGB[:].rearrange("p (h s) -> p h s", s=33), vmb3,
                        OP.mult))
                    C.sw(v, "vS", v.tensor_reduce(
                        IWK[:, 0:8], TMW[:].rearrange("p (h s) -> p h s", s=33),
                        axis=AXX, op=OP.add))
                    G(v)
                    v.tensor_tensor(TMW[:], TMW[:], ENV[:], OP.mult)
                    G(v)
                    C.sw(v, "vS", v.tensor_reduce(
                        IWK[:, 8:16], TMW[:].rearrange("p (h s) -> p h s", s=33),
                        axis=AXX, op=OP.add))
                    C.sw(v, "vS", v.scalar_tensor_tensor(
                        IWK[:, 16:24], IWK[:, 0:8], 1e-8, IWK[:, 8:16],
                        OP.mult, OP.add))
                    C.sw(v, "vS", v.reciprocal(IWK[:, 24:32], IWK[:, 16:24]))
                    rwb = IWK[:, 24:32].rearrange("p (h o) -> p h o", o=1
                                                  ).broadcast_to((128, 8, 33))
                    C.inc("v", v.tensor_tensor(
                        WALL[:, t * 264:(t + 1) * 264].rearrange(
                            "p (h s) -> p h s", s=33),
                        TMW[:].rearrange("p (h s) -> p h s", s=33), rwb,
                        OP.mult))
                blk.vector(_)
                if t == 0:
                    v_d1 = []
                v_d1.append(C.val("v"))
            pps.close()
        d1s.close()

        d2s = contextlib.ExitStack()
        D2 = d2s.enter_context
        VALS = D2(nc.sbuf_tensor(tag + "VALS", [128, 17 * D], BF16))
        ACC = D2(nc.sbuf_tensor(tag + "ACC", [128, D], BF16))
        OUTB = D2(nc.sbuf_tensor(tag + "OUTB", [128, D], BF16))
        OTT = D2(nc.sbuf_tensor(tag + "OTT", [128, D], BF16))
        SE1W = D2(nc.sbuf_tensor(tag + "SE1W", [128, NC * 256], BF16))
        SE2W = D2(nc.sbuf_tensor(tag + "SE2W", [128, 2 * 1024], BF16))
        ABOW = D2(nc.sbuf_tensor(tag + "ABOW", [128, NC * 1024], BF16))
        SE1B = D2(nc.sbuf_tensor(tag + "SE1B", [128, 256], F32))
        SE2B = D2(nc.sbuf_tensor(tag + "SE2B", [128, 1024], F32))
        SEH = D2(nc.sbuf_tensor(tag + "SEH", [128, 256], BF16))
        SEHT = D2(nc.sbuf_tensor(tag + "SEHT", [128, 256], BF16))
        OT2T = D2(nc.sbuf_tensor(tag + "OT2T", [128, D], BF16))
        SEG = SCR

        with nc.Block() as blk:
            pps = contextlib.ExitStack()
            PTP = pps.enter_context(nc.psum_tensor(tag + "PTP", [128, 128],
                                                   BF16))
            PSE = pps.enter_context(nc.psum_tensor(tag + "PSE", [128, 256], F32))
            PS2 = pps.enter_context(nc.psum_tensor(tag + "PS2", [128, 1024],
                                                   F32))
            PAB = pps.enter_context(nc.psum_tensor(tag + "PAB", [128, 1024],
                                                   F32))

            def _(s):
                C.inc("d", s.dma_start(
                    SE1W[:].rearrange("p (c n) -> p c n", c=NC),
                    WB["ab_se1"].rearrange("(c p) n -> p c n", p=128)), 16)
                C.inc("d", s.dma_start(
                    SE2W[:].rearrange("p (c n) -> p c n", c=2),
                    WB["ab_se2"].rearrange("(c p) n -> p c n", p=128)), 16)
                C.inc("d", s.dma_start(
                    ABOW[:].rearrange("p (c n) -> p c n", c=NC),
                    WB["ab_outw"].rearrange("(c p) n -> p c n", p=128)), 16)
                C.inc("d", s.dma_start(SE1B[:], W["ab_se1b"]), 16)
                C.inc("d", s.dma_start(SE2B[:], W["ab_se2b"]), 16)
            blk.sync(_)
            d_w2 = C.val("d")

            red_v = []
            for t in range(NT):
                for half in range(2):
                    s_lo = 0 if half == 0 else 17
                    s_hi = 17 if half == 0 else 33

                    def _(g, t=t, s_lo=s_lo, s_hi=s_hi, half=half):
                        if t == 0 and half == 0:
                            C.wait(g, "d")
                        if red_v:
                            g.wait_ge(C.s("v")[0], red_v[-1])
                        for s_ in range(s_lo, s_hi):
                            C.inc("ga", g.indirect_dma_start(
                                VALS[:, (s_ - s_lo) * D:(s_ - s_lo + 1) * D],
                                None, DS["xh"],
                                IndirectOffsetOnAxis(
                                    IDXA[:, t * 33 + s_:t * 33 + s_ + 1], 0)),
                                16)
                    blk.gpsimd(_)
                    g_t = C.val("ga")

                    def _(v, t=t, g_t=g_t, s_lo=s_lo, s_hi=s_hi):
                        v.wait_ge(C.s("ga")[0], g_t)
                        lastv = None
                        for s_ in range(s_lo, s_hi):
                            for h in range(8):
                                wsc = WALL[:, t * 264 + h * 33 + s_:
                                           t * 264 + h * 33 + s_ + 1]
                                vsl = VALS[:, (s_ - s_lo) * D + h * 128:
                                           (s_ - s_lo) * D + (h + 1) * 128]
                                asl = ACC[:, h * 128:(h + 1) * 128]
                                if s_ == 0:
                                    lastv = v.tensor_scalar(asl, vsl, wsc, None,
                                                            OP.mult)
                                else:
                                    lastv = v.scalar_tensor_tensor(
                                        asl, vsl, wsc, asl, OP.mult, OP.add)
                        C.inc("v", lastv)
                    blk.vector(_)
                    red_v.append(C.val("v"))

                def _(v, t=t):
                    G(v)
                    C.inc("v", v.tensor_copy(OUTB[:], ACC[:]))
                blk.vector(_)
                v_red = C.val("v")

                def tr_block(srcbuf, dstbuf, nchunks, t):
                    for cc in range(nchunks):
                        tb2 = C.val("t")

                        def _(te, cc=cc, tb2=tb2):
                            C.wait(te, "v")
                            if t == 0 and cc == 0:
                                te.wait_ge(C.s("d")[0], d_w2)
                            C.inc("t", te.transpose(
                                PTP[:], srcbuf[:, cc * 128:(cc + 1) * 128],
                                IDENT[:]))
                        blk.tensor(_)
                        t_tr = C.val("t")

                        def _(v, cc=cc, t_tr=t_tr):
                            v.wait_ge(C.s("t")[0], t_tr)
                            C.inc("v", v.tensor_copy(
                                dstbuf[:, cc * 128:(cc + 1) * 128], PTP[:]))
                        blk.vector(_)

                tr_block(OUTB, OTT, NC, t)

                def _(te, t=t):
                    C.wait(te, "v")
                    mm = None
                    for cc in range(NC):
                        mm = te.matmul(PSE[:], OTT[:, cc * 128:(cc + 1) * 128],
                                       SE1W[:, cc * 256:(cc + 1) * 256],
                                       start=(cc == 0), stop=(cc == NC - 1))
                    C.inc("t", mm)
                blk.tensor(_)

                def _(v):
                    C.wait(v, "t")
                    C.inc("v", v.tensor_tensor(SCR[:, 0:256], PSE[:], SE1B[:],
                                               OP.add))
                blk.vector(_)

                def _(a):
                    C.wait(a, "v")
                    C.inc("a", a.activation(SEH[:], SCR[:, 0:256], AF.Silu))
                blk.scalar(_)
                a_seh = C.val("a")

                def _(v, a_seh=a_seh):
                    v.wait_ge(C.s("a")[0], a_seh)
                    C.inc("v", v.nop())
                blk.vector(_)
                tr_block(SEH, SEHT, 2, 1)

                def _(te):
                    C.wait(te, "v")
                    mm = None
                    for cc in range(2):
                        for nh in range(2):
                            mm = te.matmul(
                                PS2[:, nh * 512:(nh + 1) * 512],
                                SEHT[:, cc * 128:(cc + 1) * 128],
                                SE2W[:, cc * 1024 + nh * 512:
                                     cc * 1024 + (nh + 1) * 512],
                                start=(cc == 0), stop=(cc == 1))
                    C.inc("t", mm)
                blk.tensor(_)

                def _(v):
                    C.wait(v, "t")
                    C.inc("v", v.tensor_tensor(SEG[:, 0:D], PS2[:], SE2B[:], OP.add))
                blk.vector(_)

                def _(a):
                    C.wait(a, "v")
                    C.inc("a", a.activation(SEG[:, 0:D], SEG[:, 0:D], AF.Sigmoid))
                blk.scalar(_)
                a_seg = C.val("a")

                def _(v, a_seg=a_seg):
                    v.wait_ge(C.s("a")[0], a_seg)
                    C.inc("v", v.tensor_tensor(OUTB[:], OUTB[:], SEG[:, 0:D],
                                               OP.mult))
                blk.vector(_)
                tr_block(OUTB, OT2T, NC, 1)

                def _(te):
                    C.wait(te, "v")
                    mm = None
                    for nh in range(2):
                        for cc in range(NC):
                            mm = te.matmul(
                                PAB[:, nh * 512:(nh + 1) * 512],
                                OT2T[:, cc * 128:(cc + 1) * 128],
                                ABOW[:, cc * 1024 + nh * 512:
                                     cc * 1024 + (nh + 1) * 512],
                                start=(cc == 0), stop=(cc == NC - 1))
                    C.inc("t", mm)
                blk.tensor(_)

                def _(a):
                    C.wait(a, "t")
                    C.inc("a", a.activation(OTT[:], PAB[:], AF.Silu))
                blk.scalar(_)
                a_res = C.val("a")

                def _(v, t=t, a_res=a_res):
                    v.wait_ge(C.s("a")[0], a_res)
                    C.inc("v", v.tensor_tensor(xres(b, t), xres(b, t), OTT[:],
                                               OP.add))
                blk.vector(_)
            pps.close()
        d2s.close()
        bes.close()

    for b in _B_LIST:
        with nc.sbuf_tensor(f"XT{b}", [128, NT * D], BF16) as XT_A:
            emit_ln(b, DS["xh"], tok_buf=XT_A)
            if _BUILD_UPTO >= 1:
                emit_attn_a(b, XT_A)
        if _BUILD_UPTO >= 2:
            emit_ln(b, DS["xh"], lo_dram=DS["xhl"])
            emit_ffn(b, 0)
        if _BUILD_UPTO >= 3:
            emit_ln(b, DS["xh"], lo_dram=DS["xhl"])
            emit_attn_b(b)
        if _BUILD_UPTO >= 4:
            emit_ln(b, DS["xh"], lo_dram=DS["xhl"])
            emit_ffn(b, 1)

    with nc.Block() as blk:
        v0 = C.val("v")
        fin0 = C.val("fin")

        def _(v):
            for i, (b, t) in enumerate((b, t) for b in range(B_CORE)
                                       for t in range(NT)):
                if i >= 1:
                    v.wait_ge(C.s("fin")[0], fin0 + 16 * i)
                C.inc("v", v.tensor_copy(SCR[:], xres(b, t)))
        blk.vector(_)

        def _(s):
            for i, (b, t) in enumerate((b, t) for b in range(B_CORE)
                                       for t in range(NT)):
                s.wait_ge(C.s("v")[0], v0 + i + 1)
                C.inc("fin", s.dma_start(out[b, t * 128:(t + 1) * 128, :],
                                         SCR[:]), 16)
            C.wait(s, "d")
            for i, (nm, r, c_) in enumerate(_DEBUG_DUMPS):
                if nm == "XHT":
                    C.inc("fin", s.dma_start(dbg[i], XHT[0:r, 0:c_]), 16)
                    C.wait(s, "fin")
                    continue
                srcap = DS[nm] if nm in DS else None
                for rr in range(0, r, 128):
                    for c0 in range(0, c_, 2048):
                        cw = min(2048, c_ - c0)
                        C.inc("fin", s.dma_start(
                            SCRB[:, 0:cw], srcap[rr:rr + 128, c0:c0 + cw]), 16)
                        C.wait(s, "fin")
                        C.inc("fin", s.dma_start(
                            dbg[i][rr:rr + 128, c0:c0 + cw], SCRB[:, 0:cw]), 16)
                        C.wait(s, "fin")
            C.wait(s, "fin")
        blk.sync(_)

    es.close()
    return nc


def _make_attn_a(nc, C, E, env):
    def emit(b):
        pass
    return emit



def kernel(**inputs):
    x = np.ascontiguousarray(inputs["L0_attn_a_hidden_states"], dtype=np.float32)
    wts = _host_weights({k: np.asarray(v) for k, v in inputs.items()})
    consts = _host_constants()

    nc = build_program()
    in_maps = []
    for core in range(8):
        m = {"x": x[core * 2:(core + 1) * 2]}
        m.update(wts)
        m.update(consts)
        in_maps.append(m)
    res = run_bass_kernel_spmd(nc, in_maps, core_ids=list(range(8)))
    full = np.concatenate([r["out"] for r in res.results], axis=0)
    full = np.ascontiguousarray(full, dtype=np.float32)
    if _DEBUG_DUMPS:
        kernel.debug = [
            {i: r[f"dbg{i}"] for i in range(len(_DEBUG_DUMPS))}
            for r in res.results]
    return full
